# version 10
# baseline (speedup 1.0000x reference)
import numpy as np
import ml_dtypes
from contextlib import ExitStack
import concourse.bass as bass
import concourse.mybir as mybir
from concourse.bass_utils import run_bass_kernel_spmd

F32 = mybir.dt.float32
BF16 = mybir.dt.bfloat16
I32 = mybir.dt.int32
U32 = mybir.dt.uint32
AF = mybir.ActivationFunctionType
ALU = mybir.AluOpType
AX = mybir.AxisListType
bfdt = ml_dtypes.bfloat16
ds = bass.ds

ENGINES = ("sync", "tensor", "vector", "scalar", "gpsimd")
DMA_QUEUES = ("sync", "gpsimd", "scalar")

N_TOK = 4096
D = 1024
N_CORES = 8


class B:

    def __init__(self, nc):
        self.nc = nc
        self.es = ExitStack()
        self.ops = {e: [] for e in ENGINES}
        self.pc = {e: 0 for e in ENGINES}
        self.psem = {e: self.es.enter_context(nc.semaphore(f"p_{e}")) for e in ENGINES}
        self.dsem = {}
        self.dc = {}

    def sb(self, name, shape, dtype):
        st = getattr(self, "_scope_stack", None) or self.es
        return st.enter_context(self.nc.sbuf_tensor(name, shape, dtype))

    def ps(self, name, shape, dtype=F32):
        st = getattr(self, "_scope_stack", None) or self.es
        return st.enter_context(self.nc.psum_tensor(name, shape, dtype))

    def scope(self):
        b = self

        class _S:
            def __enter__(self):
                self.prev = getattr(b, "_scope_stack", None)
                self.stack = ExitStack()
                b._scope_stack = self.stack
                return self

            def __exit__(self, *a):
                self.stack.close()
                b._scope_stack = self.prev
                return False
        return _S()

    def op(self, e, fn, inc=False):
        if inc:
            self.pc[e] += 1
            sem = self.psem[e]

            def cl(eng, fn=fn, sem=sem):
                fn(eng).then_inc(sem, 1)
            cl._is_op = True
            self.ops[e].append(cl)
            return (e, "p", self.pc[e])

        def cl0(eng, fn=fn):
            return fn(eng)
        cl0._is_op = True
        self.ops[e].append(cl0)
        return None

    def mark(self, e):
        self.pc[e] += 1
        sem = self.psem[e]
        ops = self.ops[e]
        if ops and getattr(ops[-1], "_is_op", False):
            last = ops.pop()

            def cl(eng, last=last, sem=sem):
                inst = last(eng)
                inst.then_inc(sem, 1)
            cl._is_op = False
            ops.append(cl)
        else:
            def cl(eng, sem=sem):
                eng.drain().then_inc(sem, 1)
            ops.append(cl)
        return (e, "p", self.pc[e])

    def tok(self, e):
        return (e, "p", self.pc[e])

    def _dsem(self, name):
        if name not in self.dsem:
            self.dsem[name] = self.es.enter_context(self.nc.semaphore(f"d_{name}"))
            self.dc[name] = 0
        return self.dsem[name]

    def dma(self, q, out, in_, transpose=False, sem=None):
        name = sem or q
        s = self._dsem(name)
        self.dc[name] += 16
        if transpose:
            def cl(eng, out=out, in_=in_, s=s):
                eng.dma_start_transpose(out, in_).then_inc(s, 16)
        else:
            def cl(eng, out=out, in_=in_, s=s):
                eng.dma_start(out=out, in_=in_).then_inc(s, 16)
        self.ops[q].append(cl)
        return (name, "d", self.dc[name])

    def dtok(self, name):
        self._dsem(name)
        return (name, "d", self.dc[name])

    def wait(self, e, tok):
        if tok is None:
            return
        src, kind, val = tok
        if kind == "p" and src == e:
            self.drain(e)
            return
        key = (e, kind, src)
        if not hasattr(self, "_wmax"):
            self._wmax = {}
        if self._wmax.get(key, -1) >= val:
            return
        self._wmax[key] = val
        sem = self.psem[src] if kind == "p" else self.dsem[src]

        def cl(eng, sem=sem, val=val):
            eng.wait_ge(sem, val)
        self.ops[e].append(cl)

    def drain(self, e):
        self.ops[e].append(lambda eng: eng.drain())

    def barrier_tokens(self):
        toks = [(e, "p", self.pc[e]) for e in ENGINES if self.pc[e] > 0]
        toks += [(n, "d", c) for n, c in self.dc.items() if c > 0]
        return toks

    def barrier(self):
        toks = self.barrier_tokens()
        for e in ENGINES:
            for t in toks:
                if not (t[1] == "p" and t[0] == e):
                    self.wait(e, t)

    def build(self):
        with self.es:
            block = self.es.enter_context(self.nc.Block())

            def run(eng, oplist):
                for f in oplist:
                    f(eng)

            @block.sync
            def _(sync):
                run(sync, self.ops["sync"])

            @block.tensor
            def _(tensor):
                run(tensor, self.ops["tensor"])

            @block.vector
            def _(vector):
                run(vector, self.ops["vector"])

            @block.scalar
            def _(scalar):
                run(scalar, self.ops["scalar"])

            @block.gpsimd
            def _(gpsimd):
                run(gpsimd, self.ops["gpsimd"])
        return self.nc



def emit_ln(b, xt, xn_out, st, eps=1e-5, ncols=1024):
    v, s = "vector", "scalar"
    h = ncols // 2
    b.op(v, lambda e: e.bn_stats(st["stats"][:, 0:6], xt[:, 0:h]))
    b.op(v, lambda e: e.bn_stats(st["stats"][:, 6:12], xt[:, h:ncols]))
    b.drain(v)
    b.op(v, lambda e: e.bn_aggr(st["mv"][:], st["stats"][:]))
    b.drain(v)
    b.op(v, lambda e: e.tensor_scalar(st["r1"][:], st["mv"][:, 1:2], eps, None, ALU.add))
    b.drain(v)
    tv = b.op(v, lambda e: e.reciprocal(st["r1"][:], st["r1"][:]), inc=True)
    b.wait(s, tv)
    ts = b.op(s, lambda e: e.activation(st["r1"][:], st["r1"][:], AF.Sqrt), inc=True)
    b.wait(v, ts)
    tok = b.op(v, lambda e: e.tensor_scalar(xn_out[:], xt[:], st["mv"][:, 0:1], st["r1"][:],
                                            ALU.subtract, op1=ALU.mult), inc=True)
    return tok


class TransposeHelper:

    def __init__(self, b, name, idt):
        self.b = b
        self.idt = idt
        self.ps = [b.ps(name + "_t0", [128, 1024], BF16), b.ps(name + "_t1", [128, 1024], BF16)]
        self.vt = [None, None]
        self.i = 0

    def emit(self, src_tile, dst_tile, pre_tensor_waits=(), src_p=128, src_f=128):
        b = self.b
        slot = self.i % 2
        self.i += 1
        for t in pre_tensor_waits:
            b.wait("tensor", t)
        if self.vt[slot] is not None:
            b.wait("tensor", self.vt[slot])
        tt = b.op("tensor", lambda e, slot=slot, src=src_tile, p=src_p, f=src_f: e.transpose(
            self.ps[slot][0:f, 0:p], src, self.idt[0:p, 0:p]), inc=True)
        b.wait("vector", tt)
        vt = b.op("vector", lambda e, slot=slot, dst=dst_tile, p=src_p, f=src_f: e.tensor_copy(
            dst, self.ps[slot][0:f, 0:p]), inc=True)
        self.vt[slot] = vt
        return tt, vt



def emit_l1_ffn_prep(b, wg, wu, wd, wg_bf, wu_bf, wd_bf, wgT, wuT, wdT):
    i32 = b.sb("pp_i32", [128, 3072], I32)
    bft = b.sb("pp_bf", [128, 3072], BF16)
    SC = 1.0 / 127.0
    for (w, w_bf, nrow, ncol) in ((wg, wg_bf, 3072, 1024), (wu, wu_bf, 3072, 1024), (wd, wd_bf, 1024, 3072)):
        for r in range(nrow // 128):
            t_in = b.dma("sync", i32[:, 0:ncol], w[r * 128:(r + 1) * 128, :], sem="ppld")
            b.wait("vector", t_in)
            tv = b.op("vector", lambda e, ncol=ncol: e.tensor_scalar(
                bft[:, 0:ncol], i32[:, 0:ncol], SC, None, ALU.mult), inc=True)
            b.wait("gpsimd", tv)
            t_out = b.dma("gpsimd", w_bf[r * 128:(r + 1) * 128, :], bft[:, 0:ncol], sem="ppst")
            b.wait("sync", tv)
            b.wait("vector", t_out)
    tb = b.sb("pp_tb", [128, 3072], BF16)
    b.wait("sync", b.dtok("ppst"))
    for (w_bf, wT, nrow, ncol) in ((wg_bf, wgT, 3072, 1024), (wu_bf, wuT, 3072, 1024), (wd_bf, wdT, 1024, 3072)):
        for kc in range(ncol // 128):
            t_in = b.dma("sync", tb[:, 0:nrow], w_bf[0:nrow, kc * 128:(kc + 1) * 128], transpose=True, sem="pptr")
            b.wait("gpsimd", t_in)
            t_out = b.dma("gpsimd", wT[kc * 128:(kc + 1) * 128, 0:nrow], tb[:, 0:nrow], sem="ppst2")
            b.wait("sync", t_out)



def emit_glu_ffn(b, x_in, x_out, wgT, wuT, wdT, HID, idt_ap, scale=0.5, name="l1f"):
    NOC = HID // 128
    wg_sb = b.sb(name + "_wg", [128, 8 * HID], BF16)
    wu_sb = b.sb(name + "_wu", [128, 8 * HID], BF16)
    wd_sb = b.sb(name + "_wd", [128, 2 * 1024], BF16)
    xt = [b.sb(name + f"_x{i}", [128, 1024], F32) for i in range(4)]
    xn_tm = b.sb(name + "_xntm", [128, 1024], BF16)
    xnfm = b.sb(name + "_xnfm", [128, 8 * 512], BF16)
    h_sb = b.sb(name + "_h", [128, NOC * 512], BF16)
    sil = b.sb(name + "_sil", [128, 512], BF16)
    idt = b.sb(name + "_idt", [128, 128], BF16)
    st = {"stats": b.sb(name + "_st", [128, 12], F32),
          "mv": b.sb(name + "_mv", [128, 2], F32),
          "r1": b.sb(name + "_r1", [128, 1], F32)}
    ps_g = b.ps(name + "_psg", [128, 512], F32)
    ps_u = b.ps(name + "_psu", [128, 512], F32)
    ps_y = [b.ps(name + f"_psy{i}", [128, 512], F32) for i in range(4)]

    t_id = b.dma("sync", idt[:], idt_ap[:, :], sem="wres")
    tr = TransposeHelper(b, name, idt)

    for kc in range(8):
        b.dma("sync", wg_sb[:, kc * HID:(kc + 1) * HID], wgT[kc * 128:(kc + 1) * 128, :], sem="wres")
        b.dma("sync", wu_sb[:, kc * HID:(kc + 1) * HID], wuT[kc * 128:(kc + 1) * 128, :], sem="wres")
    t_id = b.dtok("wres")
    w_toks = [t_id]

    prev_store = [None] * 4
    wd_slot_tok = [None, None]
    prev_sil_read = None

    for tc in range(8):
        ln_evac_toks = []
        x_toks = []
        for i in range(4):
            if prev_store[i] is not None:
                b.wait("sync", prev_store[3])
            x_toks.append(b.dma("sync", xt[i][:], x_in[(tc * 4 + i) * 128:(tc * 4 + i + 1) * 128, :], sem="xld"))
        for i in range(4):
            b.wait("vector", x_toks[3])
            tln = emit_ln(b, xt[i], xn_tm, st)
            last_vt = None
            for kc in range(8):
                pre = [tln, t_id] if kc == 0 else []
                tt, vt = tr.emit(xn_tm[:, kc * 128:(kc + 1) * 128],
                                 xnfm[:, kc * 512 + i * 128: kc * 512 + (i + 1) * 128],
                                 pre_tensor_waits=pre)
                last_vt = vt
            b.wait("vector", tt)
            ln_evac_toks.append(last_vt)
        xnfm_ready = ln_evac_toks[-1]

        b.wait("tensor", xnfm_ready)
        for t in w_toks:
            b.wait("tensor", t)
        w_toks = []
        ev_toks = []
        for oc in range(NOC):
            if oc >= 1:
                b.wait("tensor", ev_toks[oc - 1][1])
            for kc in range(8):
                b.op("tensor", lambda e, oc=oc, kc=kc: e.matmul(
                    ps_g[:], wg_sb[:, kc * HID + oc * 128: kc * HID + (oc + 1) * 128],
                    xnfm[:, kc * 512:(kc + 1) * 512], start=(kc == 0), stop=(kc == 7)))
            tmm_g = b.mark("tensor")
            for kc in range(8):
                b.op("tensor", lambda e, oc=oc, kc=kc: e.matmul(
                    ps_u[:], wu_sb[:, kc * HID + oc * 128: kc * HID + (oc + 1) * 128],
                    xnfm[:, kc * 512:(kc + 1) * 512], start=(kc == 0), stop=(kc == 7)))
            tmm_u = b.mark("tensor")
            b.wait("scalar", tmm_g)
            if prev_sil_read is not None:
                b.wait("scalar", prev_sil_read)
            tsil = b.op("scalar", lambda e: e.activation(sil[:], ps_g[:], AF.Silu), inc=True)
            b.wait("vector", tsil)
            b.wait("vector", tmm_u)
            thv = b.op("vector", lambda e, oc=oc: e.tensor_tensor(
                h_sb[:, oc * 512:(oc + 1) * 512], sil[:], ps_u[:], ALU.mult), inc=True)
            prev_sil_read = thv
            ev_toks.append((tsil, thv))

        st_toks = []
        for half in range(2):
            for dc in range(NOC):
                slot = dc % 2
                if wd_slot_tok[slot] is not None:
                    b.wait("sync", wd_slot_tok[slot])
                t_w = b.dma("sync", wd_sb[:, slot * 1024:(slot + 1) * 1024],
                            wdT[dc * 128:(dc + 1) * 128, :], sem=f"wd{slot}")
                b.wait("tensor", t_w)
                b.wait("tensor", ev_toks[dc][1])
                last = None
                for t in range(2):
                    t128 = half * 2 + t
                    for n2 in range(2):
                        last = b.op("tensor", lambda e, dc=dc, t128=t128, t=t, n2=n2, slot=slot: e.matmul(
                            ps_y[t * 2 + n2][:],
                            h_sb[:, dc * 512 + t128 * 128: dc * 512 + (t128 + 1) * 128],
                            wd_sb[:, slot * 1024 + n2 * 512: slot * 1024 + (n2 + 1) * 512],
                            start=(dc == 0), stop=(dc == NOC - 1)),
                            inc=(True if (t == 1 and n2 == 1) else False))
                wd_slot_tok[slot] = last
            dn_done = b.tok("tensor")
            for t in range(2):
                t128 = half * 2 + t
                b.wait("vector", dn_done)
                for n2 in range(2):
                    tvy = b.op("vector", lambda e, t128=t128, t=t, n2=n2: e.scalar_tensor_tensor(
                        xt[t128][:, n2 * 512:(n2 + 1) * 512], ps_y[t * 2 + n2][:], scale,
                        xt[t128][:, n2 * 512:(n2 + 1) * 512], ALU.mult, ALU.add), inc=True)
                    b.wait("gpsimd", tvy)
                st_toks.append(b.dma("gpsimd", x_out[(tc * 4 + t128) * 128:(tc * 4 + t128 + 1) * 128, :],
                                     xt[t128][:], sem="xst"))
            if half == 0:
                b.wait("tensor", b.tok("vector"))
        prev_store = st_toks



def build_l1_ffn_launch():
    nc = bass.Bass("TRN2", target_bir_lowering=False)
    x_in = nc.dram_tensor("x", [N_TOK, D], F32, kind="ExternalInput").ap()
    wg = nc.dram_tensor("w_gate", [3072, 1024], I32, kind="ExternalInput").ap()
    wu = nc.dram_tensor("w_up", [3072, 1024], I32, kind="ExternalInput").ap()
    wd = nc.dram_tensor("w_down", [1024, 3072], I32, kind="ExternalInput").ap()
    ident = nc.dram_tensor("ident", [128, 128], BF16, kind="ExternalInput").ap()
    x_out = nc.dram_tensor("x_out", [N_TOK, D], F32, kind="ExternalOutput").ap()
    wg_bf = nc.dram_tensor("wg_bf", [3072, 1024], BF16).ap()
    wu_bf = nc.dram_tensor("wu_bf", [3072, 1024], BF16).ap()
    wd_bf = nc.dram_tensor("wd_bf", [1024, 3072], BF16).ap()
    wgT = nc.dram_tensor("wgT", [1024, 3072], BF16).ap()
    wuT = nc.dram_tensor("wuT", [1024, 3072], BF16).ap()
    wdT = nc.dram_tensor("wdT", [3072, 1024], BF16).ap()

    b = B(nc)
    with b.scope():
        emit_l1_ffn_prep(b, wg, wu, wd, wg_bf, wu_bf, wd_bf, wgT, wuT, wdT)
    b.barrier()
    emit_glu_ffn(b, x_in, x_out, wgT, wuT, wdT, HID=3072, idt_ap=ident, scale=0.5, name="l1f")
    b.wait("gpsimd", b.dtok("xst"))
    return b.build()


_IDENT = np.eye(128, dtype=bfdt)


def run_l1_ffn(x_full, w_gate, w_up, w_down):
    nc = build_l1_ffn_launch()
    in_maps = []
    for c in range(N_CORES):
        in_maps.append({
            "x": np.ascontiguousarray(x_full[c]),
            "w_gate": np.asarray(w_gate), "w_up": np.asarray(w_up), "w_down": np.asarray(w_down),
            "ident": _IDENT,
        })
    res = run_bass_kernel_spmd(nc, in_maps, core_ids=list(range(N_CORES)))
    return np.stack([res.results[c]["x_out"] for c in range(N_CORES)])



class Fp4Quant:
    def __init__(self, b, name, F):
        self.b = b
        self.F = F
        self.A = b.sb(name + "_qA", [128, F], F32)
        self.Q = b.sb(name + "_qQ", [128, F], BF16)
        self.V = b.sb(name + "_qV", [128, F], BF16)
        self.V2 = b.sb(name + "_qV2", [128, F], BF16)
        self.G = b.sb(name + "_qG", [128, F], BF16)
        self.S = b.sb(name + "_qS", [128, F // 16], F32)
        self.S2 = b.sb(name + "_qS2", [128, F // 16], F32)
        self.prev = None
        self.prev_g = None

    def emit(self, src_ap, dst_ap, F=None, wait_toks=()):
        b = self.b
        F = F or self.F
        A, Q, V, G, S, S2 = self.A, self.Q, self.V, self.G, self.S, self.S2
        v, s = "vector", "scalar"
        for t in wait_toks:
            b.wait(v, t)
            b.wait(s, t)
        if self.prev is not None:
            b.wait(s, self.prev)
        ts_abs = b.op(s, lambda e: e.activation(A[:, 0:F], src_ap, AF.Abs), inc=True)
        ts_sgn = b.op(s, lambda e: e.activation(G[:, 0:F], src_ap, AF.Sign), inc=True)
        b.wait(v, ts_abs)
        b.op(v, lambda e: e.tensor_reduce(S[:, 0:F // 16],
             A[:, 0:F].rearrange("p (c s) -> p c s", s=16), AX.X, ALU.max))
        b.drain(v)
        b.op(v, lambda e: e.tensor_scalar(S[:, 0:F // 16], S[:, 0:F // 16],
             1.0 / 6.0, 1e-12, ALU.mult, op1=ALU.max))
        b.drain(v)
        b.op(v, lambda e: e.tensor_scalar(S2[:, 0:F // 16], S[:, 0:F // 16],
             0.5, None, ALU.mult))
        b.drain(v)
        b.op(v, lambda e: e.reciprocal(S[:, 0:F // 16], S[:, 0:F // 16]))
        b.drain(v)
        tA = b.op(v, lambda e: e.tensor_tensor(
            A[:, 0:F].rearrange("p (c s) -> p c s", s=16),
            A[:, 0:F].rearrange("p (c s) -> p c s", s=16),
            S[:, 0:F // 16].unsqueeze(2).broadcast_to([128, F // 16, 16]), ALU.mult),
            inc=True)
        b.drain(v)
        g = "gpsimd"
        V2 = self.V2
        if self.prev_g is not None:
            b.wait(g, self.prev_g)
        b.wait(g, tA)
        b.op(g, lambda e: e.tensor_scalar(V[:, 0:F], A[:, 0:F], 2.5, None, ALU.is_gt))
        b.op(g, lambda e: e.tensor_scalar(V2[:, 0:F], A[:, 0:F], 3.5, None, ALU.is_gt))
        b.drain(g)
        b.op(g, lambda e: e.tensor_tensor(V[:, 0:F], V[:, 0:F], V2[:, 0:F], ALU.add))
        b.drain(g)
        tg2 = b.op(g, lambda e: e.tensor_scalar(
            V2[:, 0:F], A[:, 0:F], 5.0, 4.0, ALU.is_gt, op1=ALU.mult), inc=True)
        b.drain(g)
        b.op(v, lambda e: e.tensor_scalar(Q[:, 0:F], A[:, 0:F], 0.25, None, ALU.is_gt))
        b.drain(v)
        for t in (0.75, 1.25, 1.75):
            b.op(v, lambda e, t=t: e.scalar_tensor_tensor(
                Q[:, 0:F], A[:, 0:F], t, Q[:, 0:F], ALU.is_gt, ALU.add))
            b.drain(v)
        b.wait(v, tg2)
        b.op(v, lambda e: e.scalar_tensor_tensor(
            Q[:, 0:F], V[:, 0:F], 2.0, Q[:, 0:F], ALU.mult, ALU.add))
        b.drain(v)
        tqv = b.op(v, lambda e: e.tensor_tensor(Q[:, 0:F], Q[:, 0:F], V2[:, 0:F], ALU.add),
                   inc=True)
        self.prev_g = tqv
        b.drain(v)
        b.op(v, lambda e: e.tensor_tensor(
            Q[:, 0:F].rearrange("p (c s) -> p c s", s=16),
            Q[:, 0:F].rearrange("p (c s) -> p c s", s=16),
            S2[:, 0:F // 16].unsqueeze(2).broadcast_to([128, F // 16, 16]), ALU.mult))
        b.drain(v)
        b.wait(v, ts_sgn)
        tok = b.op(v, lambda e: e.tensor_tensor(dst_ap, Q[:, 0:F], G[:, 0:F], ALU.mult), inc=True)
        b.drain(v)
        self.prev = tok
        return tok



def emit_fp4_weight_quant(b, w, wq_bf, nrow, ncol, qz, name):
    wt = b.sb(name + "_wt", [128, ncol], F32)
    wq = b.sb(name + "_wq", [128, ncol], BF16)
    prev_q = None
    for r in range(nrow // 128):
        if prev_q is not None:
            b.wait("sync", prev_q)
        t_in = b.dma("sync", wt[:, 0:ncol], w[r * 128:(r + 1) * 128, :], sem=name + "ld")
        tq = qz.emit(wt[:, 0:ncol], wq[:, 0:ncol], F=ncol, wait_toks=[t_in])
        prev_q = tq
        b.wait("gpsimd", tq)
        t_out = b.dma("gpsimd", wq_bf[r * 128:(r + 1) * 128, :], wq[:, 0:ncol], sem=name + "st")
        b.wait("vector", t_out)


def emit_transpose_pass(b, w_bf, wT, nrow, ncol, tbuf):
    for kc in range(ncol // 128):
        t_in = b.dma("sync", tbuf[:, 0:nrow], w_bf[0:nrow, kc * 128:(kc + 1) * 128], transpose=True, sem="pptr")
        b.wait("gpsimd", t_in)
        t_out = b.dma("gpsimd", wT[kc * 128:(kc + 1) * 128, 0:nrow], tbuf[:, 0:nrow], sem="ppst")
        b.wait("sync", t_out)


def emit_l2_ffn(b, x_in, x_out, wgqT, wuqT, wdqT, idt_ap, name="l2f"):
    HID = 2816
    NB = HID // 128
    OCS = [(i * 512, 512) for i in range(5)] + [(2560, 256)]
    wgq_sb = b.sb(name + "_wg", [128, 8 * HID], BF16)
    wuq_sb = b.sb(name + "_wu", [128, 8 * HID], BF16)
    wdq_sb = b.sb(name + "_wd", [128, NB * 1024], BF16)
    xt = b.sb(name + "_x", [128, 1024], F32)
    xn = b.sb(name + "_xn", [128, 1024], F32)
    xq = b.sb(name + "_xq", [128, 1024], BF16)
    xqT = b.sb(name + "_xqT", [128, 8 * 128], BF16)
    h_tm = b.sb(name + "_h", [128, HID], F32)
    hq = b.sb(name + "_hq", [128, HID], BF16)
    hqT = b.sb(name + "_hqT", [128, NB * 128], BF16)
    sil = b.sb(name + "_sil", [128, 512], F32)
    idt = b.sb(name + "_idt", [128, 128], BF16)
    st = {"stats": b.sb(name + "_st", [128, 12], F32),
          "mv": b.sb(name + "_mv", [128, 2], F32),
          "r1": b.sb(name + "_r1", [128, 1], F32)}
    qz = Fp4Quant(b, name, HID)
    ps_g2 = [b.ps(name + "_psg", [128, 512], F32), b.ps(name + "_psg1", [128, 512], F32)]
    ps_u2 = [b.ps(name + "_psu", [128, 512], F32), b.ps(name + "_psu1", [128, 512], F32)]
    ps_y = [b.ps(name + "_psy0", [128, 512], F32), b.ps(name + "_psy1", [128, 512], F32)]
    tr = TransposeHelper(b, name, idt)

    t_id = b.dma("sync", idt[:], idt_ap[:, :], sem="wres")
    for kc in range(8):
        b.dma("sync", wgq_sb[:, kc * HID:(kc + 1) * HID], wgqT[kc * 128:(kc + 1) * 128, :], sem="wres")
        b.dma("sync", wuq_sb[:, kc * HID:(kc + 1) * HID], wuqT[kc * 128:(kc + 1) * 128, :], sem="wres")
    for dc in range(NB):
        b.dma("sync", wdq_sb[:, dc * 1024:(dc + 1) * 1024], wdqT[dc * 128:(dc + 1) * 128, :], sem="wres")
    t_id = b.dtok("wres")
    w_toks = [t_id]

    prev_store = None
    prev_gu = None
    prev_dn = None
    first = True
    for i in range(32):
        if prev_store is not None:
            b.wait("sync", prev_store)
        t_x = b.dma("sync", xt[:], x_in[i * 128:(i + 1) * 128, :], sem="xld")
        b.wait("vector", t_x)
        tln = emit_ln(b, xt, xn, st)
        tq = qz.emit(xn[:], xq[:], F=1024, wait_toks=[tln])
        for kc in range(8):
            pre = [tq, t_id] if kc == 0 else []
            if kc == 0 and prev_gu is not None:
                pre.append(prev_gu)
            tt, vt = tr.emit(xq[:, kc * 128:(kc + 1) * 128], xqT[:, kc * 128:(kc + 1) * 128],
                             pre_tensor_waits=pre)
        xqT_ready = vt
        b.wait("vector", tt)

        b.wait("tensor", xqT_ready)
        if first:
            for t in w_toks:
                b.wait("tensor", t)
            w_toks = []
        ev = []
        for ocix, (off, n) in enumerate(OCS):
            ps_g = ps_g2[ocix % 2]
            ps_u = ps_u2[ocix % 2]
            if ocix >= 2:
                b.wait("tensor", ev[ocix - 2][1])
            for kc in range(8):
                b.op("tensor", lambda e, off=off, n=n, kc=kc, ps_g=ps_g: e.matmul(
                    ps_g[:, 0:n], xqT[:, kc * 128:(kc + 1) * 128],
                    wgq_sb[:, kc * HID + off: kc * HID + off + n], start=(kc == 0), stop=(kc == 7)))
            tg = b.mark("tensor")
            for kc in range(8):
                b.op("tensor", lambda e, off=off, n=n, kc=kc, ps_u=ps_u: e.matmul(
                    ps_u[:, 0:n], xqT[:, kc * 128:(kc + 1) * 128],
                    wuq_sb[:, kc * HID + off: kc * HID + off + n], start=(kc == 0), stop=(kc == 7)))
            tu = b.mark("tensor")
            b.wait("scalar", tg)
            if ev:
                b.wait("scalar", ev[-1][1])
            tsil = b.op("scalar", lambda e, n=n, ps_g=ps_g: e.activation(sil[:, 0:n], ps_g[:, 0:n], AF.Silu), inc=True)
            b.wait("vector", tsil)
            b.wait("vector", tu)
            thv = b.op("vector", lambda e, off=off, n=n, ps_u=ps_u: e.tensor_tensor(
                h_tm[:, off:off + n], sil[:, 0:n], ps_u[:, 0:n], ALU.mult), inc=True)
            ev.append((tsil, thv))
        gu_done = b.tok("tensor")
        prev_gu = gu_done

        tqh = qz.emit(h_tm[:], hq[:], F=HID, wait_toks=[ev[-1][1]])
        for dc in range(NB):
            pre = [tqh] if dc == 0 else []
            if dc == 0 and prev_dn is not None:
                pre.append(prev_dn)
            tt2, vt2 = tr.emit(hq[:, dc * 128:(dc + 1) * 128], hqT[:, dc * 128:(dc + 1) * 128],
                               pre_tensor_waits=pre)
        hqT_ready = vt2
        b.wait("vector", tt2)

        b.wait("tensor", hqT_ready)
        last = None
        for n2 in range(2):
            for dc in range(NB):
                last = b.op("tensor", lambda e, n2=n2, dc=dc: e.matmul(
                    ps_y[n2][:], hqT[:, dc * 128:(dc + 1) * 128],
                    wdq_sb[:, dc * 1024 + n2 * 512: dc * 1024 + (n2 + 1) * 512],
                    start=(dc == 0), stop=(dc == NB - 1)),
                    inc=(n2 == 1 and dc == NB - 1))
        prev_dn = last
        b.wait("vector", last)
        for n2 in range(2):
            tvy = b.op("vector", lambda e, n2=n2: e.scalar_tensor_tensor(
                xt[:, n2 * 512:(n2 + 1) * 512], ps_y[n2][:], 0.5,
                xt[:, n2 * 512:(n2 + 1) * 512], ALU.mult, ALU.add), inc=True)
            b.wait("gpsimd", tvy)
        prev_store = b.dma("gpsimd", x_out[i * 128:(i + 1) * 128, :], xt[:], sem="xst")
        first = False


def build_l2_ffn_launch():
    nc = bass.Bass("TRN2", target_bir_lowering=False)
    x_in = nc.dram_tensor("x", [N_TOK, D], F32, kind="ExternalInput").ap()
    wg = nc.dram_tensor("w_gate", [2816, 1024], F32, kind="ExternalInput").ap()
    wu = nc.dram_tensor("w_up", [2816, 1024], F32, kind="ExternalInput").ap()
    wd = nc.dram_tensor("w_down", [1024, 2816], F32, kind="ExternalInput").ap()
    ident = nc.dram_tensor("ident", [128, 128], BF16, kind="ExternalInput").ap()
    x_out = nc.dram_tensor("x_out", [N_TOK, D], F32, kind="ExternalOutput").ap()
    wgq_bf = nc.dram_tensor("wgq_bf", [2816, 1024], BF16).ap()
    wuq_bf = nc.dram_tensor("wuq_bf", [2816, 1024], BF16).ap()
    wdq_bf = nc.dram_tensor("wdq_bf", [1024, 2816], BF16).ap()
    wgqT = nc.dram_tensor("wgqT", [1024, 2816], BF16).ap()
    wuqT = nc.dram_tensor("wuqT", [1024, 2816], BF16).ap()
    wdqT = nc.dram_tensor("wdqT", [2816, 1024], BF16).ap()

    b = B(nc)
    with b.scope():
        qz = Fp4Quant(b, "pp", 2816)
        emit_fp4_weight_quant(b, wg, wgq_bf, 2816, 1024, qz, "ppg")
        emit_fp4_weight_quant(b, wu, wuq_bf, 2816, 1024, qz, "ppu")
        emit_fp4_weight_quant(b, wd, wdq_bf, 1024, 2816, qz, "ppd")
        for nm in ("ppg", "ppu", "ppd"):
            b.wait("sync", b.dtok(nm + "st"))
        tb = b.sb("pp_tb2", [128, 2816], BF16)
        emit_transpose_pass(b, wgq_bf, wgqT, 2816, 1024, tb)
        emit_transpose_pass(b, wuq_bf, wuqT, 2816, 1024, tb)
        emit_transpose_pass(b, wdq_bf, wdqT, 1024, 2816, tb)
    b.barrier()
    emit_l2_ffn(b, x_in, x_out, wgqT, wuqT, wdqT, ident, name="l2f")
    b.wait("gpsimd", b.dtok("xst"))
    return b.build()


def run_l2_ffn(x_full, w_gate, w_up, w_down):
    nc = build_l2_ffn_launch()
    in_maps = []
    for c in range(N_CORES):
        in_maps.append({
            "x": np.ascontiguousarray(x_full[c]),
            "w_gate": np.asarray(w_gate), "w_up": np.asarray(w_up), "w_down": np.asarray(w_down),
            "ident": _IDENT,
        })
    res = run_bass_kernel_spmd(nc, in_maps, core_ids=list(range(N_CORES)))
    return np.stack([res.results[c]["x_out"] for c in range(N_CORES)])



def emit_cast_pass(b, w, w_bf, nrow, sem="ppcast"):
    step = 2048
    for r0 in range(0, nrow, step):
        r1 = min(nrow, r0 + step)
        t = b.dma("gpsimd", w_bf[r0:r1, :], w[r0:r1, :], sem=sem)
    return t



def emit_l3_ffn(b, x_in, x_out, w1T, w2T, rwT, b1_ap, b2_ap, rb_ap, idt_ap, ones_ap, name="l3f"):
    w1_sb = b.sb(name + "_w1", [128, 8 * 4096], BF16)
    w2_sb = b.sb(name + "_w2", [128, 32 * 1024], BF16)
    rw_sb = b.sb(name + "_rw", [128, 8 * 16], BF16)
    b1_sb = b.sb(name + "_b1", [1, 4096], BF16)
    b2_sb = b.sb(name + "_b2", [1, 1024], BF16)
    rb_sb = b.sb(name + "_rb", [1, 16], BF16)
    ones_sb = b.sb(name + "_ones", [1, 128], BF16)
    idt = b.sb(name + "_idt", [128, 128], BF16)
    xt = b.sb(name + "_x", [128, 1024], F32)
    xn = b.sb(name + "_xn", [128, 1024], BF16)
    xnT = b.sb(name + "_xnT", [128, 8 * 128], BF16)
    h_tm = b.sb(name + "_h", [128, 4096], BF16)
    hT = b.sb(name + "_hT", [128, 32 * 128], BF16)
    lg = b.sb(name + "_lg", [128, 16], F32)
    topv = b.sb(name + "_topv", [128, 8], F32)
    mask = b.sb(name + "_mask", [128, 16], F32)
    st = {"stats": b.sb(name + "_st", [128, 12], F32),
          "mv": b.sb(name + "_mv", [128, 2], F32),
          "r1": b.sb(name + "_r1", [128, 1], F32)}
    ps_r = b.ps(name + "_psr", [128, 512], F32)
    ps_h = [b.ps(name + "_psh0", [128, 512], F32), b.ps(name + "_psh1", [128, 512], F32)]
    ps_y = [b.ps(name + "_psy0", [128, 512], F32), b.ps(name + "_psy1", [128, 512], F32)]
    tr = TransposeHelper(b, name, idt)

    b.dma("sync", idt[:], idt_ap[:, :], sem="wres")
    b.dma("sync", ones_sb[:], ones_ap[:, :], sem="wres")
    for kc in range(8):
        b.dma("sync", w1_sb[:, kc * 4096:(kc + 1) * 4096], w1T[kc * 128:(kc + 1) * 128, :], sem="wres")
        b.dma("sync", rw_sb[:, kc * 16:(kc + 1) * 16], rwT[kc * 128:(kc + 1) * 128, :], sem="wres")
    for dc in range(32):
        b.dma("sync", w2_sb[:, dc * 1024:(dc + 1) * 1024], w2T[dc * 128:(dc + 1) * 128, :], sem="wres")
    w_tok = b.dtok("wres")
    b.dma("gpsimd", b1_sb[:], b1_ap.unsqueeze(0), sem="wres2")
    b.dma("gpsimd", b2_sb[:], b2_ap.unsqueeze(0), sem="wres2")
    b.dma("gpsimd", rb_sb[:], rb_ap.unsqueeze(0), sem="wres2")
    w_tok2 = b.dtok("wres2")

    prev_store = None
    prev_hmm = None
    prev_dn = None
    first = True
    for i in range(32):
        if prev_store is not None:
            b.wait("sync", prev_store)
        t_x = b.dma("sync", xt[:], x_in[i * 128:(i + 1) * 128, :], sem="xld")
        b.wait("vector", t_x)
        tln = emit_ln(b, xt, xn, st)
        for kc in range(8):
            pre = [tln] + ([w_tok] if first else [])
            if kc == 0 and prev_hmm is not None:
                pre.append(prev_hmm)
            tt, vt = tr.emit(xn[:, kc * 128:(kc + 1) * 128], xnT[:, kc * 128:(kc + 1) * 128],
                             pre_tensor_waits=pre)
        xnT_ready = vt
        b.wait("vector", tt)

        b.wait("tensor", xnT_ready)
        if first:
            b.wait("tensor", w_tok2)
        for kc in range(8):
            b.op("tensor", lambda e, kc=kc: e.matmul(
                ps_r[:, 0:16], xnT[:, kc * 128:(kc + 1) * 128],
                rw_sb[:, kc * 16:(kc + 1) * 16], start=(kc == 0), stop=False))
        trt = b.op("tensor", lambda e: e.matmul(
            ps_r[:, 0:16], ones_sb[0:1, :], rb_sb[0:1, :], start=False, stop=True), inc=True)
        b.wait("vector", trt)
        b.op("vector", lambda e: e.tensor_copy(lg[:], ps_r[:, 0:16]))
        b.drain("vector")
        b.op("vector", lambda e: e.max(topv[:], lg[:]))
        b.drain("vector")
        tmask = b.op("vector", lambda e: e.tensor_scalar(
            mask[:], lg[:], topv[:, 3:4], None, ALU.is_ge), inc=True)

        hev = []
        for oc in range(8):
            ph = ps_h[oc % 2]
            if oc >= 2:
                b.wait("tensor", hev[oc - 2])
            for kc in range(8):
                b.op("tensor", lambda e, oc=oc, kc=kc, ph=ph: e.matmul(
                    ph[:], xnT[:, kc * 128:(kc + 1) * 128],
                    w1_sb[:, kc * 4096 + oc * 512: kc * 4096 + (oc + 1) * 512],
                    start=(kc == 0), stop=False))
            th = b.op("tensor", lambda e, oc=oc, ph=ph: e.matmul(
                ph[:], ones_sb[0:1, :], b1_sb[0:1, oc * 512:(oc + 1) * 512],
                start=False, stop=True), inc=True)
            b.wait("vector", th)
            if oc == 0:
                b.wait("vector", tmask)
            for j in range(2):
                blk = oc * 2 + j
                tev = b.op("vector", lambda e, oc=oc, j=j, blk=blk, ph=ph: e.tensor_scalar(
                    h_tm[:, oc * 512 + j * 256: oc * 512 + (j + 1) * 256],
                    ph[:, j * 256:(j + 1) * 256], 0.0, mask[:, blk:blk + 1],
                    ALU.max, op1=ALU.mult), inc=True)
            hev.append(tev)
        hmm_done = b.tok("tensor")
        prev_hmm = hmm_done

        for dc in range(32):
            pre = [hev[dc // 4]] if dc % 4 == 0 else []
            if dc == 0 and prev_dn is not None:
                pre.append(prev_dn)
            tt2, vt2 = tr.emit(h_tm[:, dc * 128:(dc + 1) * 128], hT[:, dc * 128:(dc + 1) * 128],
                               pre_tensor_waits=pre)
        hT_ready = vt2
        b.wait("vector", tt2)

        b.wait("tensor", hT_ready)
        last = None
        for n2 in range(2):
            for dc in range(32):
                b.op("tensor", lambda e, n2=n2, dc=dc: e.matmul(
                    ps_y[n2][:], hT[:, dc * 128:(dc + 1) * 128],
                    w2_sb[:, dc * 1024 + n2 * 512: dc * 1024 + (n2 + 1) * 512],
                    start=(dc == 0), stop=False))
            last = b.op("tensor", lambda e, n2=n2: e.matmul(
                ps_y[n2][:], ones_sb[0:1, :], b2_sb[0:1, n2 * 512:(n2 + 1) * 512],
                start=False, stop=True), inc=True)
        prev_dn = last
        b.wait("vector", last)
        for n2 in range(2):
            tvy = b.op("vector", lambda e, n2=n2: e.scalar_tensor_tensor(
                xt[:, n2 * 512:(n2 + 1) * 512], ps_y[n2][:], 0.5,
                xt[:, n2 * 512:(n2 + 1) * 512], ALU.mult, ALU.add), inc=True)
            b.wait("gpsimd", tvy)
        prev_store = b.dma("gpsimd", x_out[i * 128:(i + 1) * 128, :], xt[:], sem="xst")
        first = False


def build_l3_ffn_launch():
    nc = bass.Bass("TRN2", target_bir_lowering=False)
    x_in = nc.dram_tensor("x", [N_TOK, D], F32, kind="ExternalInput").ap()
    rw = nc.dram_tensor("router_w", [16, 1024], F32, kind="ExternalInput").ap()
    rb = nc.dram_tensor("router_b", [16], F32, kind="ExternalInput").ap()
    w1 = nc.dram_tensor("w1", [4096, 1024], F32, kind="ExternalInput").ap()
    b1 = nc.dram_tensor("b1", [4096], F32, kind="ExternalInput").ap()
    w2 = nc.dram_tensor("w2", [1024, 4096], F32, kind="ExternalInput").ap()
    b2 = nc.dram_tensor("b2", [1024], F32, kind="ExternalInput").ap()
    ident = nc.dram_tensor("ident", [128, 128], BF16, kind="ExternalInput").ap()
    ones = nc.dram_tensor("ones", [1, 128], BF16, kind="ExternalInput").ap()
    x_out = nc.dram_tensor("x_out", [N_TOK, D], F32, kind="ExternalOutput").ap()
    w1_bf = nc.dram_tensor("w1_bf", [4096, 1024], BF16).ap()
    w2_bf = nc.dram_tensor("w2_bf", [1024, 4096], BF16).ap()
    rw_bf = nc.dram_tensor("rw_bf", [16, 1024], BF16).ap()
    w1T = nc.dram_tensor("w1T", [1024, 4096], BF16).ap()
    w2T = nc.dram_tensor("w2T", [4096, 1024], BF16).ap()
    rwT = nc.dram_tensor("rwT", [1024, 16], BF16).ap()

    b = B(nc)
    with b.scope():
        emit_cast_pass(b, w1, w1_bf, 4096)
        emit_cast_pass(b, w2, w2_bf, 1024)
        emit_cast_pass(b, rw, rw_bf, 16)
        b.wait("sync", b.dtok("ppcast"))
        tb = b.sb("pp_tb3", [128, 4096], BF16)
        emit_transpose_pass(b, w1_bf, w1T, 4096, 1024, tb)
        emit_transpose_pass(b, w2_bf, w2T, 1024, 4096, tb)
        emit_transpose_pass(b, rw_bf, rwT, 16, 1024, tb)
    b.barrier()
    emit_l3_ffn(b, x_in, x_out, w1T, w2T, rwT, b1, b2, rb, ident, ones, name="l3f")
    b.wait("gpsimd", b.dtok("xst"))
    return b.build()


_ONES = np.ones((1, 128), dtype=bfdt)


def run_l3_ffn(x_full, rw, rb, w1, b1, w2, b2):
    nc = build_l3_ffn_launch()
    in_maps = []
    for c in range(N_CORES):
        in_maps.append({
            "x": np.ascontiguousarray(x_full[c]),
            "router_w": np.asarray(rw), "router_b": np.asarray(rb),
            "w1": np.asarray(w1), "b1": np.asarray(b1),
            "w2": np.asarray(w2), "b2": np.asarray(b2),
            "ident": _IDENT, "ones": _ONES,
        })
    res = run_bass_kernel_spmd(nc, in_maps, core_ids=list(range(N_CORES)))
    return np.stack([res.results[c]["x_out"] for c in range(N_CORES)])



def emit_l0_ffn(b, x_in, x_out, w1, w2, b1_ap, b2_ap, gw_ap, idt_ap, ones_ap, name="l0f"):
    E, DFF = 8, 4096
    xr = b.sb(name + "_xr", [128, 16 * 1024], F32)
    xnfm = b.sb(name + "_xnfm", [128, 8 * 2048], BF16)
    coef = b.sb(name + "_coef", [128, 16 * 8], F32)
    xn_tm = b.sb(name + "_xn", [128, 1024], BF16)
    gw_sb = b.sb(name + "_gw", [128, 8 * 8], BF16)
    w1s = [b.sb(name + f"_w1s{i}", [128, 8 * 1024], BF16) for i in range(2)]
    w2s = [b.sb(name + f"_w2s{i}", [128, 8 * 1024], BF16) for i in range(2)]
    h_fm = b.sb(name + "_h", [128, 8 * 512], BF16)
    b1c = b.sb(name + "_b1c", [32, 128], BF16)
    b1f = b.sb(name + "_b1f", [128, 32], F32)
    b2r = b.sb(name + "_b2r", [1, 1024], BF16)
    ones_sb = b.sb(name + "_ones", [1, 128], BF16)
    idt = b.sb(name + "_idt", [128, 128], BF16)
    lg = b.sb(name + "_lg", [128, 8], F32)
    mm_ = b.sb(name + "_mm", [128, 1], F32)
    nm_ = b.sb(name + "_nm", [128, 1], F32)
    ex_ = b.sb(name + "_ex", [128, 8], F32)
    ss_ = b.sb(name + "_ss", [128, 1], F32)
    msk = b.sb(name + "_msk", [128, 8], F32)
    st = {"stats": b.sb(name + "_st", [128, 12], F32),
          "mv": b.sb(name + "_mv", [128, 2], F32),
          "r1": b.sb(name + "_r1", [128, 1], F32)}
    ps_h = [b.ps(name + "_psh0", [128, 512], F32), b.ps(name + "_psh1", [128, 512], F32)]
    ps_y2 = [b.ps(name + "_psy", [128, 1024], F32), b.ps(name + "_psyb", [128, 1024], F32)]
    ps_m = ps_h[0]
    tr = TransposeHelper(b, name, idt)
    gwbf = b.sb(name + "_gwbf", [8, 1024], BF16)
    b.dma("sync", idt[:], idt_ap[:, :], sem="wres")
    b.dma("sync", ones_sb[:], ones_ap[:, :], sem="wres")
    b.dma("gpsimd", gwbf[:, :], gw_ap[:, :], sem="gwld")
    b.wait("tensor", b.dtok("wres"))
    w_tok0 = b.dtok("gwld")
    for kc in range(8):
        tt0, vt0 = tr.emit(gwbf[0:8, kc * 128:(kc + 1) * 128], gw_sb[:, kc * 8:(kc + 1) * 8],
                           pre_tensor_waits=[w_tok0] if kc == 0 else (), src_p=8)
    w_tok = vt0

    st_toks = [None] * 16
    slot_tok = [None, None]
    prev_apply = None
    for half in range(2):
        base = half * 2048
        x_toks = []
        for i in range(16):
            if st_toks[i] is not None:
                b.wait("sync", st_toks[15])
            x_toks.append(b.dma("sync", xr[:, i * 1024:(i + 1) * 1024],
                                x_in[base + i * 128: base + (i + 1) * 128, :], sem="xld"))
        gate_toks = []
        for i in range(16):
            b.wait("vector", x_toks[15])
            tln = emit_ln(b, xr[:, i * 1024:(i + 1) * 1024], xn_tm, st)
            for kc in range(8):
                pre = [tln] + ([w_tok] if (half == 0 and i == 0) else [])
                if kc == 0 and prev_apply is not None and i == 0:
                    pre.append(prev_apply)
                tt, vt = tr.emit(xn_tm[:, kc * 128:(kc + 1) * 128],
                                 xnfm[:, kc * 2048 + i * 128: kc * 2048 + (i + 1) * 128],
                                 pre_tensor_waits=pre)
            b.wait("vector", tt)
            xnT_done = vt
            b.wait("tensor", xnT_done)
            tg = None
            for kc in range(8):
                tg = b.op("tensor", lambda e, kc=kc, i=i: e.matmul(
                    ps_m[:, 0:8], xnfm[:, kc * 2048 + i * 128: kc * 2048 + (i + 1) * 128],
                    gw_sb[:, kc * 8:(kc + 1) * 8], start=(kc == 0), stop=(kc == 7)),
                    inc=(kc == 7))
            b.wait("vector", tg)
            b.op("vector", lambda e: e.tensor_copy(lg[:], ps_m[:, 0:8]))
            b.drain("vector")
            b.op("vector", lambda e: e.tensor_reduce(mm_[:], lg[:], AX.X, ALU.max))
            b.drain("vector")
            tnm = b.op("vector", lambda e: e.tensor_scalar(nm_[:], mm_[:], -1.0, None, ALU.mult), inc=True)
            b.wait("scalar", tnm)
            tex = b.op("scalar", lambda e: e.activation(ex_[:], lg[:], AF.Exp, bias=nm_[:],
                                                        accum_out=ss_[:]), inc=True)
            b.wait("vector", tex)
            b.op("vector", lambda e: e.reciprocal(ss_[:], ss_[:]))
            b.drain("vector")
            b.op("vector", lambda e: e.tensor_scalar(msk[:], lg[:], mm_[:, 0:1], None, ALU.is_equal))
            b.drain("vector")
            tcoef = b.op("vector", lambda e, i=i: e.tensor_scalar(
                coef[:, i * 8:(i + 1) * 8], msk[:], ss_[:, 0:1], 0.5, ALU.mult, op1=ALU.mult), inc=True)
            gate_toks.append(tcoef)
        phaseA_v = gate_toks[-1]

        ap_toks = [phaseA_v]
        last_hev = None
        prev_ttb = None
        tap_par = [None, None]
        for e in range(E):
            if e > 0:
                b.wait("gpsimd", prev_ttb)
            tb1 = b.dma("gpsimd", b1c[:, :], b1_ap[e].rearrange("(c p) -> c p", p=128), sem="b1ld")
            pre = [tb1]
            if last_hev is not None:
                b.wait("vector", last_hev)
            prev_ttb, tb1f = tr.emit(b1c[:, :], b1f[:], pre_tensor_waits=pre, src_p=32)
            if e > 0:
                b.wait("gpsimd", ap_toks[-1])
            tb2 = b.dma("gpsimd", b2r[:], b2_ap[e].unsqueeze(0), sem="b2ld")
            for dc in range(4):
                k = e * 4 + dc
                slot = k % 2
                if slot_tok[slot] is not None:
                    b.wait("gpsimd", slot_tok[slot])
                for kc in range(8):
                    b.dma("gpsimd", w1s[slot][:, kc * 1024:(kc + 1) * 1024],
                          w1[e, kc * 128:(kc + 1) * 128, dc * 1024:(dc + 1) * 1024],
                          sem=f"w1s{slot}")
                for kc in range(8):
                    b.dma("gpsimd", w2s[slot][:, kc * 1024:(kc + 1) * 1024],
                          w2[e, dc * 1024 + kc * 128: dc * 1024 + (kc + 1) * 128, :],
                          sem=f"w2s{slot}")
                tw1 = b.dtok(f"w1s{slot}")
                tw2 = b.dtok(f"w2s{slot}")
                for tc in range(4):
                    hev = []
                    for hc in range(8):
                        ph = ps_h[hc % 2]
                        if hc >= 2:
                            b.wait("tensor", hev[hc - 2])
                        if hc == 0 and tc == 0:
                            b.wait("tensor", tw1)
                            b.wait("tensor", tw2)
                            if dc == 0:
                                b.wait("tensor", tb1f)
                        for kc in range(8):
                            b.op("tensor", lambda e_, hc=hc, kc=kc, tc=tc, slot=slot, ph=ph: e_.matmul(
                                ph[:], w1s[slot][:, kc * 1024 + hc * 128: kc * 1024 + (hc + 1) * 128],
                                xnfm[:, kc * 2048 + tc * 512: kc * 2048 + (tc + 1) * 512],
                                start=(kc == 0), stop=(kc == 7)))
                        tmm = b.mark("tensor")
                        b.wait("scalar", tmm)
                        thc = b.op("scalar", lambda e_, hc=hc, dc=dc, ph=ph: e_.activation(
                            h_fm[:, hc * 512:(hc + 1) * 512], ph[:], AF.Relu,
                            bias=b1f[:, dc * 8 + hc: dc * 8 + hc + 1]), inc=True)
                        hev.append(thc)
                    last_hev = hev[-1]
                    for t128 in range(4):
                        ti = tc * 4 + t128
                        psy = ps_y2[t128 % 2]
                        b.wait("tensor", hev[7])
                        if dc == 0 and tc == 0 and t128 == 0:
                            b.wait("tensor", tb2)
                        if tap_par[t128 % 2] is not None:
                            b.wait("tensor", tap_par[t128 % 2])
                        for hc in range(8):
                            for n2 in range(2):
                                stop = (dc != 0) and (hc == 7)
                                b.op("tensor", lambda e_, hc=hc, n2=n2, t128=t128, slot=slot, stop=stop, psy=psy: e_.matmul(
                                    psy[:, n2 * 512:(n2 + 1) * 512],
                                    h_fm[:, hc * 512 + t128 * 128: hc * 512 + (t128 + 1) * 128],
                                    w2s[slot][:, hc * 1024 + n2 * 512: hc * 1024 + (n2 + 1) * 512],
                                    start=(hc == 0), stop=stop))
                        if dc == 0:
                            for n2 in range(2):
                                b.op("tensor", lambda e_, n2=n2, psy=psy: e_.matmul(
                                    psy[:, n2 * 512:(n2 + 1) * 512], ones_sb[0:1, :],
                                    b2r[0:1, n2 * 512:(n2 + 1) * 512], start=False, stop=True))
                        last_mm = b.mark("tensor")
                        b.wait("vector", last_mm)
                        for n2 in range(2):
                            tap = b.op("vector", lambda e_, ti=ti, n2=n2, e=e, psy=psy: e_.scalar_tensor_tensor(
                                xr[:, ti * 1024 + n2 * 512: ti * 1024 + (n2 + 1) * 512],
                                psy[:, n2 * 512:(n2 + 1) * 512], coef[:, ti * 8 + e: ti * 8 + e + 1],
                                xr[:, ti * 1024 + n2 * 512: ti * 1024 + (n2 + 1) * 512],
                                ALU.mult, ALU.add), inc=True)
                        ap_toks.append(tap)
                        tap_par[t128 % 2] = tap
                slot_tok[slot] = b.tok("tensor")
        prev_apply = ap_toks[-1]
        b.wait("gpsimd", ap_toks[-1])
        for i in range(16):
            st_toks[i] = b.dma("gpsimd", x_out[base + i * 128: base + (i + 1) * 128, :],
                               xr[:, i * 1024:(i + 1) * 1024], sem="xst")


def build_l0_ffn_launch():
    nc = bass.Bass("TRN2", target_bir_lowering=False)
    x_in = nc.dram_tensor("x", [N_TOK, D], F32, kind="ExternalInput").ap()
    gw = nc.dram_tensor("gate_w", [8, 1024], F32, kind="ExternalInput").ap()
    w1 = nc.dram_tensor("w1", [8, 1024, 4096], F32, kind="ExternalInput").ap()
    b1 = nc.dram_tensor("b1", [8, 4096], F32, kind="ExternalInput").ap()
    w2 = nc.dram_tensor("w2", [8, 4096, 1024], F32, kind="ExternalInput").ap()
    b2 = nc.dram_tensor("b2", [8, 1024], F32, kind="ExternalInput").ap()
    ident = nc.dram_tensor("ident", [128, 128], BF16, kind="ExternalInput").ap()
    ones = nc.dram_tensor("ones", [1, 128], BF16, kind="ExternalInput").ap()
    x_out = nc.dram_tensor("x_out", [N_TOK, D], F32, kind="ExternalOutput").ap()

    b = B(nc)
    emit_l0_ffn(b, x_in, x_out, w1, w2, b1, b2, gw, ident, ones, name="l0f")
    b.wait("gpsimd", b.dtok("xst"))
    return b.build()


def run_l0_ffn(x_full, gw, w1, b1, w2, b2):
    nc = build_l0_ffn_launch()
    in_maps = []
    for c in range(N_CORES):
        in_maps.append({
            "x": np.ascontiguousarray(x_full[c]),
            "gate_w": np.asarray(gw), "w1": np.asarray(w1), "b1": np.asarray(b1),
            "w2": np.asarray(w2), "b2": np.asarray(b2),
            "ident": _IDENT, "ones": _ONES,
        })
    res = run_bass_kernel_spmd(nc, in_maps, core_ids=list(range(N_CORES)))
    return np.stack([res.results[c]["x_out"] for c in range(N_CORES)])



def emit_mha(b, x_in, x_out, w_q, w_k, w_v, idt_ap, ones_ap, onesc_ap, cmask_ap,
             causal, name, ntok=N_TOK, debug_stop=None):
    NT = ntok // 128
    NQC = ntok // 512
    fuse_dn = True
    VW = 1040
    HW = VW // 16
    k_fm = b.sb(name + "_kfm", [128, 8 * ntok], BF16)
    v_sb = b.sb(name + "_v", [128, NT * VW], BF16)
    q_fm = b.sb(name + "_qfm", [128, 8 * 512], BF16)
    w_sb = b.sb(name + "_w", [128, 16 * 1024], BF16)
    xt = [b.sb(name + f"_x{i}", [128, 1024], F32) for i in range(4)]
    xn = b.sb(name + "_xn", [128, 2 * 1024], BF16)
    xnT = b.sb(name + "_xnT", [128, 8 * 128], BF16)
    ktv = b.sb(name + "_ktv", [128, 1024], BF16)
    q2 = b.sb(name + "_q2", [128, 1024], BF16)
    qn2 = b.sb(name + "_qn2", [128, 16], BF16)
    qn2f = b.sb(name + "_qn2f", [128, 16], F32)
    kn2 = b.sb(name + "_kn2", [128, 16], BF16)
    qnF = b.sb(name + "_qnF", [16, 512], BF16)
    knF = b.sb(name + "_knF", [16, 128], BF16)
    kntmp = b.sb(name + "_kntmp", [16, 1], BF16)
    kmax = b.sb(name + "_kmax", [16, 1], BF16)
    qmx = b.sb(name + "_qmx", [16, 1], BF16)
    crow = b.sb(name + "_crow", [1, 16], BF16)
    negc = b.sb(name + "_negc", [128, 16], F32)
    pt_t = b.sb(name + "_pt", [128, 3 * 1024], BF16)
    dcol = b.sb(name + "_dcol", [128, 4], BF16)
    dcolf = b.sb(name + "_dcolf", [128, 4], F32)
    o_fm = ktv[0:64, 0:512]
    drow = ktv[0:1, 512:1024]
    o_tm = b.sb(name + "_otm", [128, 65], F32)
    idt = b.sb(name + "_idt", [128, 128], BF16)
    ones_sb = b.sb(name + "_ones", [1, 128], BF16)
    onesc = b.sb(name + "_onesc", [128, 1], BF16)
    cmask = b.sb(name + "_cmask", [128, 4 * 512], BF16)
    st = {"stats": b.sb(name + "_st", [128, 12], F32),
          "mv": b.sb(name + "_mv", [128, 2], F32),
          "r1": b.sb(name + "_r1", [128, 1], F32)}
    ps_p = [b.ps(name + "_psp0", [128, 512], F32), b.ps(name + "_psp1", [128, 512], F32)]
    ps_s = b.ps(name + "_pss", [128, 1024], F32)
    ps_o = b.ps(name + "_pso", [128, 512], F32)
    ps_d = b.ps(name + "_psd", [128, 512], F32)
    tr = TransposeHelper(b, name, idt)

    b.dma("sync", idt[:], idt_ap[:, :], sem="wres")
    b.dma("sync", ones_sb[:], ones_ap[:, :], sem="wres")
    b.dma("sync", onesc[:], onesc_ap[:, :], sem="wres")
    if causal:
        b.dma("sync", cmask[:], cmask_ap[:, :], sem="wres")
    for kc in range(8):
        b.dma("sync", w_sb[:, kc * 1024:(kc + 1) * 1024], w_k[kc * 128:(kc + 1) * 128, :], sem="wres")
    for kc in range(8):
        b.dma("gpsimd", w_sb[:, 8192 + kc * 1024: 8192 + (kc + 1) * 1024],
              w_v[kc * 128:(kc + 1) * 128, :], sem="wvres")
    w_tok = b.dtok("wres")
    wv_tok = b.dtok("wvres")
    if fuse_dn:
        vones = v_sb[:, :].rearrange("p (c d) -> p c d", d=HW)[:, :, 64:65]
        b.op("vector", lambda e: e.memset(vones, 1.0))
        b.drain("vector")

    prev_evacs = []
    prev_tkn = None
    tknm = None
    ln_toks = [None] * (NT + 8)
    x_toks = [None] * (NT + 8)
    xn_guard = [None, None]

    def p1_load(i):
        if i < NT:
            if i >= 4:
                b.wait("sync", ln_toks[i - 4])
            x_toks[i] = b.dma("sync", xt[i % 4][:], x_in[i * 128:(i + 1) * 128, :], sem="xld")
    for i in range(3):
        p1_load(i)
    for i in range(NT):
        p1_load(i + 3)
        b.wait("vector", x_toks[i])
        if xn_guard[i % 2] is not None:
            b.wait("vector", xn_guard[i % 2])
        xns = xn[:, (i % 2) * 1024:(i % 2 + 1) * 1024]
        tln = emit_ln(b, xt[i % 4], xns, st)
        ln_toks[i] = tln
        for kc in range(8):
            pre = [tln] + ([w_tok, wv_tok] if (i == 0 and kc == 0) else [])
            tt, vt = tr.emit(xns[:, kc * 128:(kc + 1) * 128], xnT[:, kc * 128:(kc + 1) * 128],
                             pre_tensor_waits=pre)
        xn_guard[i % 2] = tt
        b.wait("tensor", vt)
        tev = tsq = None
        k_evac = []
        for half in range(2):
            pp = ps_p[half]
            for t_ in prev_evacs:
                b.wait("tensor", t_)
            for kc in range(8):
                b.op("tensor", lambda e, kc=kc, half=half, pp=pp: e.matmul(
                    pp[:], xnT[:, kc * 128:(kc + 1) * 128],
                    w_sb[:, kc * 1024 + half * 512: kc * 1024 + (half + 1) * 512],
                    start=(kc == 0), stop=(kc == 7)))
            tpk = b.mark("tensor")
            b.wait("vector", tpk)
            tev = b.op("vector", lambda e, half=half, pp=pp: e.tensor_copy(
                ktv[:, half * 512:(half + 1) * 512], pp[:]), inc=True)
            b.wait("scalar", tpk)
            b.wait("scalar", tev)
            if prev_tkn is not None:
                b.wait("scalar", prev_tkn)
            tsq = b.op("scalar", lambda e, half=half, pp=pp: e.activation(
                q2[:, half * 512:(half + 1) * 512], pp[:], AF.Square), inc=True)
            k_evac.append((tev, tsq))
        prev_evacs = [k_evac[0][0], k_evac[0][1], k_evac[1][0], k_evac[1][1]]
        for half in range(2):
            pp = ps_p[half]
            for t_ in prev_evacs:
                b.wait("tensor", t_)
            prev_evacs = []
            for kc in range(8):
                b.op("tensor", lambda e, kc=kc, half=half, pp=pp: e.matmul(
                    pp[:], xnT[:, kc * 128:(kc + 1) * 128],
                    w_sb[:, 8192 + kc * 1024 + half * 512: 8192 + kc * 1024 + (half + 1) * 512],
                    start=(kc == 0), stop=(kc == 7)))
            tpv = b.mark("tensor")
            b.wait("vector", tpv)
            if fuse_dn:
                dst3 = v_sb[:, i * VW + half * 520: i * VW + half * 520 + 520].rearrange(
                    "p (h d) -> p h d", d=65)[:, :, 0:64]
                tvv = b.op("vector", lambda e, dst3=dst3, pp=pp: e.tensor_copy(
                    dst3, pp[:].rearrange("p (h d) -> p h d", d=64)), inc=True)
            else:
                tvv = b.op("vector", lambda e, half=half, i=i, pp=pp: e.tensor_copy(
                    v_sb[:, i * 1024 + half * 512: i * 1024 + (half + 1) * 512], pp[:]), inc=True)
            prev_evacs.append(tvv)
        b.wait("vector", tsq)
        b.drain("vector")
        b.op("vector", lambda e: e.tensor_reduce(
            qn2f[:], q2[:, :].rearrange("p (h d) -> p h d", d=64), AX.X, ALU.add))
        b.drain("vector")
        tkn = b.op("vector", lambda e: e.tensor_copy(kn2[:], qn2f[:]), inc=True)
        prev_tkn = tkn
        for kc in range(8):
            tt, vt = tr.emit(ktv[:, kc * 128:(kc + 1) * 128],
                             k_fm[:, kc * ntok + i * 128: kc * ntok + (i + 1) * 128],
                             pre_tensor_waits=[k_evac[1][0]] if kc == 0 else ())
        tt_k, vt_k = tr.emit(kn2[:, 0:16], knF[:, 0:128],
                             pre_tensor_waits=[tkn], src_p=128, src_f=16)
        b.wait("vector", tt_k)
        b.wait("vector", vt_k)
        b.drain("vector")
        b.op("vector", lambda e: e.tensor_reduce(kntmp[0:16, 0:1], knF[0:16, 0:128], AX.X, ALU.max))
        b.drain("vector")
        if i == 0:
            tknm = b.op("vector", lambda e: e.tensor_copy(kmax[0:16, 0:1], kntmp[0:16, 0:1]), inc=True)
        else:
            tknm = b.op("vector", lambda e: e.tensor_tensor(
                kmax[0:16, 0:1], kmax[0:16, 0:1], kntmp[0:16, 0:1], ALU.max), inc=True)
        b.drain("vector")
    if debug_stop in ("p1a", "p1a1"):
        return

    b.wait("sync", b.tok("tensor"))
    for kc in range(8):
        b.dma("sync", w_sb[:, kc * 1024:(kc + 1) * 1024], w_q[kc * 128:(kc + 1) * 128, :], sem="wres")
    wq_tok = b.dtok("wres")
    prev_tln = ln_toks[NT - 1]

    if debug_stop in ("phase1", "p1a1x"):
        return
    prev_store = [None] * 4
    prev_qmm = None
    prev_tad = None
    prev_tqn = None
    for qc in range(NQC):
        x_toks = []
        for t in range(4):
            if prev_store[t] is not None:
                b.wait("sync", prev_store[3])
            else:
                b.wait("sync", ln_toks[NT - 4 + t])
            x_toks.append(b.dma("sync", xt[t][:],
                                x_in[(qc * 4 + t) * 128:(qc * 4 + t + 1) * 128, :], sem="xld"))
        vtqn = None
        for t in range(4):
            b.wait("vector", x_toks[3])
            tln = emit_ln(b, xt[t], xn[:, 0:1024], st)
            for kc in range(8):
                pre = [tln]
                if kc == 0 and prev_qmm is not None and t == 0:
                    pre.append(prev_qmm)
                if qc == 0 and t == 0 and kc == 0:
                    pre.append(wq_tok)
                tt, vt = tr.emit(xn[:, kc * 128:(kc + 1) * 128], xnT[:, kc * 128:(kc + 1) * 128],
                                 pre_tensor_waits=pre)
            b.wait("vector", tt)
            b.wait("tensor", vt)
            tqe = tq2 = None
            for half in range(2):
                for t_ in prev_evacs:
                    b.wait("tensor", t_)
                prev_evacs = []
                pp = ps_p[half]
                for kc in range(8):
                    b.op("tensor", lambda e, kc=kc, half=half, pp=pp: e.matmul(
                        pp[:], xnT[:, kc * 128:(kc + 1) * 128],
                        w_sb[:, kc * 1024 + half * 512: kc * 1024 + (half + 1) * 512],
                        start=(kc == 0), stop=(kc == 7)))
                tpq = b.mark("tensor")
                b.wait("vector", tpq)
                tqe = b.op("vector", lambda e, half=half, pp=pp: e.tensor_scalar(
                    ktv[:, half * 512:(half + 1) * 512], pp[:], 0.125, None, ALU.mult), inc=True)
                b.wait("scalar", tpq)
                b.wait("scalar", tqe)
                if prev_tqn is not None:
                    b.wait("scalar", prev_tqn)
                tq2 = b.op("scalar", lambda e, half=half, pp=pp: e.activation(
                    q2[:, half * 512:(half + 1) * 512], pp[:], AF.Square, scale=0.125), inc=True)
                prev_evacs = [tqe, tq2]
            b.wait("vector", tq2)
            b.drain("vector")
            b.op("vector", lambda e: e.tensor_reduce(
                qn2f[:], q2[:, :].rearrange("p (h d) -> p h d", d=64), AX.X, ALU.add))
            b.drain("vector")
            tqn = b.op("vector", lambda e: e.tensor_copy(qn2[:], qn2f[:]), inc=True)
            prev_tqn = tqn
            for kc in range(8):
                tt, vt = tr.emit(ktv[:, kc * 128:(kc + 1) * 128],
                                 q_fm[:, kc * 512 + t * 128: kc * 512 + (t + 1) * 128],
                                 pre_tensor_waits=[tqe] if kc == 0 else ())
            ttqn, vtqn = tr.emit(qn2[:, 0:16], qnF[:, t * 128:(t + 1) * 128],
                                 pre_tensor_waits=[tqn], src_p=128, src_f=16)
            b.wait("vector", ttqn)
        b.wait("vector", vtqn)
        if qc == 0:
            b.wait("vector", tknm)
        b.drain("vector")
        b.op("vector", lambda e: e.tensor_reduce(qmx[0:16, 0:1], qnF[0:16, 0:512], AX.X, ALU.max))
        b.drain("vector")
        tcb = b.op("vector", lambda e: e.tensor_tensor(qmx[0:16, 0:1], qmx[0:16, 0:1],
                                                       kmax[0:16, 0:1], ALU.mult), inc=True)
        b.wait("scalar", tcb)
        tcs = b.op("scalar", lambda e: e.activation(qmx[0:16, 0:1], qmx[0:16, 0:1], AF.Sqrt), inc=True)
        ttc, vtc = tr.emit(qmx[0:16, 0:1], crow[0:1, 0:16], pre_tensor_waits=[tcs],
                           src_p=16, src_f=1)
        b.drain("vector")
        tneg = b.op("vector", lambda e: e.tensor_scalar(crow[0:1, 0:16], crow[0:1, 0:16],
                                                        -1.05, None, ALU.mult), inc=True)
        b.wait("tensor", tneg)
        if prev_tad is not None:
            b.wait("tensor", prev_tad)
        tbc = b.op("tensor", lambda e: e.matmul(ps_d[:, 0:16], ones_sb[0:1, :], crow[0:1, 0:16],
                                                start=True, stop=True), inc=True)
        b.wait("vector", tbc)
        tnegc = b.op("vector", lambda e: e.tensor_copy(negc[:], ps_d[:, 0:16]), inc=True)
        b.wait("tensor", tnegc)
        b.wait("scalar", tnegc)

        if debug_stop == "bound":
            continue
        nheads = 1 if debug_stop == "head1" else 16

        def make_epilogue(h, tdn):
            def epi():
                nonlocal prev_tad
                b.wait("vector", tdn)
                if fuse_dn:
                    tor = b.op("vector", lambda e: e.tensor_copy(
                        ktv[0:65, 0:512], ps_o[0:65, :]), inc=True)
                    tad = None
                    for t in range(4):
                        tto, vto = tr.emit(ktv[0:65, t * 128:(t + 1) * 128], o_tm[:, 0:65],
                                           pre_tensor_waits=[tor] if t == 0 else (),
                                           src_p=65, src_f=128)
                        b.wait("vector", vto)
                        b.drain("vector")
                        trc = b.op("vector", lambda e, t=t: e.reciprocal(
                            dcolf[:, t:t + 1], o_tm[:, 64:65]), inc=True)
                        b.drain("vector")
                        tad = b.op("vector", lambda e, t=t, h=h: e.scalar_tensor_tensor(
                            xt[t][:, h * 64:(h + 1) * 64], o_tm[:, 0:64], dcolf[:, t:t + 1],
                            xt[t][:, h * 64:(h + 1) * 64], ALU.mult, ALU.add), inc=True)
                        b.drain("vector")
                    prev_tad = tad
                    return tad, tor
                tdr = b.op("vector", lambda e: e.tensor_copy(drow[:], ps_d[0:1, 0:512]), inc=True)
                tor = b.op("vector", lambda e: e.tensor_copy(o_fm[:], ps_o[0:64, :]), inc=True)
                vtd = None
                for t in range(4):
                    ttd, vtd = tr.emit(drow[0:1, t * 128:(t + 1) * 128], dcol[:, t:t + 1],
                                       pre_tensor_waits=[tdr] if t == 0 else (), src_p=1, src_f=128)
                b.wait("vector", vtd)
                b.drain("vector")
                trc = b.op("vector", lambda e: e.reciprocal(dcolf[:], dcol[:]), inc=True)
                b.drain("vector")
                tad = None
                for t in range(4):
                    tto, vto = tr.emit(o_fm[0:64, t * 128:(t + 1) * 128], o_tm[:, 0:64],
                                       pre_tensor_waits=[tor] if t == 0 else (), src_p=64, src_f=128)
                    b.wait("vector", vto)
                    tad = b.op("vector", lambda e, t=t, h=h: e.scalar_tensor_tensor(
                        xt[t][:, h * 64:(h + 1) * 64], o_tm[:, 0:64], dcolf[:, t:t + 1],
                        xt[t][:, h * 64:(h + 1) * 64], ALU.mult, ALU.add), inc=True)
                    b.drain("vector")
                prev_tad = tad
                return tdr, tor
            return epi

        pending_epi = None
        epi_guard = None
        for h in range(nheads):
            hc, par = h // 2, h % 2
            koff = par * 64
            ndiag = min((qc + 1) * 4, NT) if causal else NT
            nrounds = NT // 2
            texp_r = [[None, None] for _ in range(nrounds)]
            cons = [[None, None] for _ in range(3)]
            for r in range(nrounds + 2):
                if r < nrounds:
                    slot = r % 3
                    for j in range(2):
                        kt = r * 2 + j
                        if r > 0 and texp_r[r - 1][j] is not None:
                            b.wait("tensor", texp_r[r - 1][j])
                        b.op("tensor", lambda e, hc=hc, koff=koff, kt=kt, j=j: e.matmul(
                            ps_s[:, j * 512:(j + 1) * 512],
                            k_fm[koff:koff + 64, hc * ntok + kt * 128: hc * ntok + (kt + 1) * 128],
                            q_fm[koff:koff + 64, hc * 512:(hc + 1) * 512],
                            start=True, stop=True))
                        tsm = b.mark("tensor")
                        b.wait("scalar", tsm)
                        if cons[slot][j] is not None:
                            b.wait("scalar", cons[slot][j])
                        texp_r[r][j] = b.op("scalar", lambda e, slot=slot, j=j, h=h: e.activation(
                            pt_t[:, slot * 1024 + j * 512: slot * 1024 + j * 512 + 512],
                            ps_s[:, j * 512:(j + 1) * 512], AF.Exp, bias=negc[:, h:h + 1]),
                            inc=True)
                if r == 2 and pending_epi is not None:
                    epi_guard = pending_epi()
                    pending_epi = None
                if r >= 2:
                    rc = r - 2
                    slot_c = rc % 3
                    for j in range(2):
                        kt = rc * 2 + j
                        po = slot_c * 1024 + j * 512
                        b.wait("tensor", texp_r[rc][j])
                        diag0 = qc * 4 if causal else NT
                        if kt == 0 and epi_guard is not None:
                            b.wait("tensor", epi_guard[1])
                        if kt < diag0:
                            tav = b.op("tensor", lambda e, kt=kt, h=h, po=po: e.matmul(
                                ps_o[0:65, :], v_sb[:, kt * VW + h * 65: kt * VW + h * 65 + 65],
                                pt_t[:, po:po + 512],
                                start=(kt == 0),
                                stop=(not causal and kt == NT - 1)), inc=True)
                            cons[slot_c][j] = tav
                            continue
                        tdn_mm = b.op("tensor", lambda e, kt=kt, po=po: e.matmul(
                            ps_o[64:65, :], onesc[:, 0:1], pt_t[:, po:po + 512],
                            start=(kt == 0), stop=(kt == NT - 1)), inc=True)
                        last_cons = tdn_mm
                        dg = kt - qc * 4
                        if dg < 4:
                            b.wait("vector", tdn_mm)
                            tpt = b.op("vector", lambda e, dg=dg, po=po: e.tensor_tensor(
                                pt_t[:, po:po + 512], pt_t[:, po:po + 512],
                                cmask[:, dg * 512:(dg + 1) * 512], ALU.mult), inc=True)
                            b.wait("tensor", tpt)
                            tav = b.op("tensor", lambda e, kt=kt, h=h, po=po, ndiag=ndiag: e.matmul(
                                ps_o[0:64, :], v_sb[:, kt * VW + h * 65: kt * VW + h * 65 + 64],
                                pt_t[:, po:po + 512],
                                start=(kt == 0), stop=(kt == ndiag - 1)), inc=True)
                            last_cons = tav
                        cons[slot_c][j] = last_cons
            tdn = b.tok("tensor")
            pending_epi = make_epilogue(h, tdn)
        epi_guard = pending_epi()
        pending_epi = None
        b.wait("gpsimd", prev_tad)
        prev_store = []
        for t in range(4):
            prev_store.append(b.dma("gpsimd", x_out[(qc * 4 + t) * 128:(qc * 4 + t + 1) * 128, :],
                                    xt[t][:], sem="xst"))
        prev_qmm = b.tok("tensor")


def build_mha_launch(causal, ntok=N_TOK, debug_stop=None):
    nc = bass.Bass("TRN2", target_bir_lowering=False)
    x_in = nc.dram_tensor("x", [ntok, D], F32, kind="ExternalInput").ap()
    wq = nc.dram_tensor("wq", [1024, 1024], F32, kind="ExternalInput").ap()
    wk = nc.dram_tensor("wk", [1024, 1024], F32, kind="ExternalInput").ap()
    wv = nc.dram_tensor("wv", [1024, 1024], F32, kind="ExternalInput").ap()
    ident = nc.dram_tensor("ident", [128, 128], BF16, kind="ExternalInput").ap()
    ones = nc.dram_tensor("ones", [1, 128], BF16, kind="ExternalInput").ap()
    onesc = nc.dram_tensor("onesc", [128, 1], BF16, kind="ExternalInput").ap()
    cmask = nc.dram_tensor("cmask", [128, 4 * 512], BF16, kind="ExternalInput").ap()
    x_out = nc.dram_tensor("x_out", [ntok, D], F32, kind="ExternalOutput").ap()
    wq_bf = nc.dram_tensor("wq_bf", [1024, 1024], BF16).ap()
    wk_bf = nc.dram_tensor("wk_bf", [1024, 1024], BF16).ap()
    wv_bf = nc.dram_tensor("wv_bf", [1024, 1024], BF16).ap()

    b = B(nc)
    emit_cast_pass(b, wq, wq_bf, 1024)
    emit_cast_pass(b, wk, wk_bf, 1024)
    emit_cast_pass(b, wv, wv_bf, 1024)
    b.wait("sync", b.dtok("ppcast"))
    b.wait("gpsimd", b.dtok("ppcast"))
    if debug_stop != "cast":
        emit_mha(b, x_in, x_out, wq_bf, wk_bf, wv_bf, ident, ones, onesc, cmask,
                 causal=causal, name="mha", ntok=ntok, debug_stop=debug_stop)
    if b.dc.get("xst"):
        b.wait("gpsimd", b.dtok("xst"))
    return b.build()


_ONESC = np.ones((128, 1), dtype=bfdt)


def _causal_mask_const():
    m = np.zeros((128, 4 * 512), dtype=bfdt)
    for dg in range(4):
        for p in range(128):
            qq = np.arange(512)
            m[p, dg * 512:(dg + 1) * 512] = (dg * 128 + p <= qq).astype(bfdt)
    return m


_CMASK = _causal_mask_const()


def run_mha(x_full, wq, wk, wv, causal, ntok=N_TOK, debug_stop=None):
    nc = build_mha_launch(causal, ntok, debug_stop)
    in_maps = []
    for c in range(N_CORES):
        in_maps.append({
            "x": np.ascontiguousarray(x_full[c]),
            "wq": np.asarray(wq), "wk": np.asarray(wk), "wv": np.asarray(wv),
            "ident": _IDENT, "ones": _ONES, "onesc": _ONESC, "cmask": _CMASK,
        })
    res = run_bass_kernel_spmd(nc, in_maps, core_ids=list(range(N_CORES)))
    return np.stack([res.results[c]["x_out"] for c in range(N_CORES)])


def build_l3_attn_launch(ntok=N_TOK):
    nc = bass.Bass("TRN2", target_bir_lowering=False)
    x_in = nc.dram_tensor("x", [ntok, D], F32, kind="ExternalInput").ap()
    wqkv = nc.dram_tensor("w_qkv", [3072, 1024], F32, kind="ExternalInput").ap()
    ident = nc.dram_tensor("ident", [128, 128], BF16, kind="ExternalInput").ap()
    ones = nc.dram_tensor("ones", [1, 128], BF16, kind="ExternalInput").ap()
    onesc = nc.dram_tensor("onesc", [128, 1], BF16, kind="ExternalInput").ap()
    cmask = nc.dram_tensor("cmask", [128, 4 * 512], BF16, kind="ExternalInput").ap()
    x_out = nc.dram_tensor("x_out", [ntok, D], F32, kind="ExternalOutput").ap()
    qkv_bf = nc.dram_tensor("qkv_bf", [3072, 1024], BF16).ap()
    qkvT = nc.dram_tensor("qkvT", [1024, 3072], BF16).ap()

    b = B(nc)
    with b.scope():
        emit_cast_pass(b, wqkv, qkv_bf, 3072)
        b.wait("sync", b.dtok("ppcast"))
        tb = b.sb("pp_tb4", [128, 3072], BF16)
        emit_transpose_pass(b, qkv_bf, qkvT, 3072, 1024, tb)
    b.barrier()
    emit_mha(b, x_in, x_out, qkvT[:, 0:1024], qkvT[:, 1024:2048], qkvT[:, 2048:3072],
             ident, ones, onesc, cmask, causal=False, name="mha", ntok=ntok)
    b.wait("gpsimd", b.dtok("xst"))
    return b.build()


def run_l3_attn(x_full, w_qkv, ntok=N_TOK):
    nc = build_l3_attn_launch(ntok)
    in_maps = []
    for c in range(N_CORES):
        in_maps.append({
            "x": np.ascontiguousarray(x_full[c]), "w_qkv": np.asarray(w_qkv),
            "ident": _IDENT, "ones": _ONES, "onesc": _ONESC, "cmask": _CMASK,
        })
    res = run_bass_kernel_spmd(nc, in_maps, core_ids=list(range(N_CORES)))
    return np.stack([res.results[c]["x_out"] for c in range(N_CORES)])



GSZ = 256


def _l0_bucket():
    import math
    pos = np.arange(GSZ)
    n = -(pos[None, :] - pos[:, None])
    nb = 16
    ret = (n < 0).astype(np.int64) * nb
    n = np.abs(n)
    max_exact = nb // 2
    is_small = n < max_exact
    val_large = max_exact + (np.log(np.maximum(n, 1).astype(np.float64) / max_exact)
                             / math.log(128 / max_exact) * (nb - max_exact)).astype(np.int64)
    val_large = np.minimum(val_large, nb - 1)
    return ret + np.where(is_small, n, val_large)


def _l0_relmask_const():
    bucket = _l0_bucket()
    m = np.zeros((128, 2 * 32 * 256), dtype=bfdt)
    for jc in range(2):
        for k in range(32):
            blk = (bucket[:, jc * 128:(jc + 1) * 128].T == k).astype(bfdt)
            m[:, (jc * 32 + k) * 256:(jc * 32 + k + 1) * 256] = blk
    return m


_RELMASK = _l0_relmask_const()


def emit_l0_attn(b, x_in, x_out, norm_g, norm_b, w_h, b_h, w_qk, b_qk, qk_gamma, qk_beta,
                 rel_emb, w_o, b_o, hv_scr, vs_scr, idt_ap, ones_ap, onesc_ap, relmask_ap,
                 name="l0a"):
    NT = 32
    wh_sb = b.sb(name + "_wh", [128, 8 * 4096], BF16)
    wqk_sb = b.sb(name + "_wqk", [128, 8 * 128], BF16)
    wo_sb = b.sb(name + "_wo", [128, 16 * 1024], BF16)
    whf = b.sb(name + "_whf", [128, 2048], F32)
    qk_all = b.sb(name + "_qka", [128, 32 * 128], BF16)
    qk_fma = b.sb(name + "_qkf", [128, 32 * 128], BF16)
    xt = b.sb(name + "_x", [128, 1024], F32)
    z1 = b.sb(name + "_z1", [128, 1024], F32)
    zn = b.sb(name + "_zn", [128, 1024], BF16)
    znT = b.sb(name + "_znT", [128, 8 * 128], BF16)
    hv_t = b.sb(name + "_hv", [128, 4096], BF16)
    gcol = b.sb(name + "_gcol", [128, 8], F32)
    bcol = b.sb(name + "_bcol", [128, 8], BF16)
    g8 = b.sb(name + "_g8", [8, 128], BF16)
    bias_h = b.sb(name + "_bh", [1, 4096], BF16)
    bias_qk = b.sb(name + "_bqk", [1, 128], BF16)
    bhrow = b.sb(name + "_bhrow", [1, 4096], BF16)
    bqkrow = b.sb(name + "_bqkrow", [1, 128], BF16)
    bo_row = b.sb(name + "_borow", [1, 1024], BF16)
    gam = b.sb(name + "_gam", [128, 4], F32)
    bet = b.sb(name + "_bet", [128, 4], F32)
    betb = b.sb(name + "_betb", [128, 4], BF16)
    gb4 = b.sb(name + "_gb4", [4, 128], BF16)
    betr = b.sb(name + "_betr", [1, 128], BF16)
    linkv = b.sb(name + "_lkv", [128, 2048], BF16)
    linkv2 = b.sb(name + "_lkv2", [128, 2048], BF16)
    vsrow = b.sb(name + "_vs", [1, 2048], BF16)
    vs16 = b.sb(name + "_vs16", [128, 16], BF16)
    vsf = b.sb(name + "_vsf", [16, 128], BF16)
    rowt = b.sb(name + "_rowt", [1, 2048], BF16)
    relr = b.sb(name + "_relr", [1, 32], BF16)
    relbc = b.sb(name + "_relbc", [128, 32], F32)
    btT = b.sb(name + "_btT", [128, 2 * 256], BF16)
    qq = b.sb(name + "_qq", [128, 256], BF16)
    kk = b.sb(name + "_kk", [128, 256], BF16)
    attnT = b.sb(name + "_attnT", [128, 2 * 256], BF16)
    vg_t = [b.sb(name + f"_vg{i}", [128, 2048], BF16) for i in range(2)]
    gt_t = [b.sb(name + f"_gt{i}", [128, 2048], BF16) for i in range(2)]
    ho = b.sb(name + "_ho", [128, 2048], BF16)
    hoT = b.sb(name + "_hoT", [128, 16 * 128], BF16)
    rm = b.sb(name + "_rm", [128, 256], BF16)
    acc = b.sb(name + "_acc", [128, 256], F32)
    at1 = acc
    idt = b.sb(name + "_idt", [128, 128], BF16)
    ones_sb = b.sb(name + "_ones", [1, 128], BF16)
    onesc = b.sb(name + "_onesc", [128, 1], BF16)
    st = {"stats": b.sb(name + "_st", [128, 12], F32),
          "mv": b.sb(name + "_mv", [128, 2], F32),
          "r1": b.sb(name + "_r1", [128, 1], F32)}
    tr = TransposeHelper(b, name, idt)
    scopeA = b.scope()
    scopeA.__enter__()
    ps_a = b.ps(name + "_psa", [128, 512], F32)
    ps_kv = b.ps(name + "_pskv", [128, 2048], F32)
    ps_vs = b.ps(name + "_psvs", [128, 512], F32)

    b.dma("sync", idt[:], idt_ap[:, :], sem="wres")
    b.dma("sync", ones_sb[:], ones_ap[:, :], sem="wres")
    b.dma("sync", onesc[:], onesc_ap[:, :], sem="wres")
    w_tok = b.dtok("wres")
    b.dma("gpsimd", g8[:, :], norm_g.rearrange("(c p) -> c p", p=128), sem="cst")
    b.dma("gpsimd", relr[:, :], rel_emb.flatten().unsqueeze(0), sem="cst")
    b.dma("gpsimd", gb4[:, :], qk_gamma[:, :], sem="cst")
    c_tok = b.dtok("cst")

    ttg, vtg = tr.emit(g8[0:8, :], gcol[:, 0:8], pre_tensor_waits=[w_tok, c_tok], src_p=8)
    b.wait("gpsimd", vtg)
    b.dma("gpsimd", g8[:, :], norm_b.rearrange("(c p) -> c p", p=128), sem="cst")
    c2 = b.dtok("cst")
    ttb_, vtb_ = tr.emit(g8[0:8, :], bcol[:, 0:8], pre_tensor_waits=[c2], src_p=8)
    ttg2, vtg2 = tr.emit(gb4[0:4, :], gam[:, 0:4], pre_tensor_waits=[c_tok], src_p=4)
    b.wait("gpsimd", vtg2)
    b.dma("gpsimd", gb4[:, :], qk_beta[:, :], sem="cst")
    c3 = b.dtok("cst")
    ttb2, vtb2 = tr.emit(gb4[0:4, :], bet[:, 0:4], pre_tensor_waits=[c3], src_p=4)
    b.wait("vector", vtb2)
    b.drain("vector")
    for buf in (gam, bet):
        for (c, s) in ((0, 1.0 / 16), (2, 1.0 / 16), (3, 1.0 / 4096)):
            b.op("vector", lambda e, buf=buf, c=c, s=s: e.tensor_scalar(
                buf[:, c:c + 1], buf[:, c:c + 1], s, None, ALU.mult))
            b.drain("vector")
    b.op("vector", lambda e: e.tensor_copy(betb[:, :], bet[:, :]))
    b.drain("vector")
    tcst = b.mark("vector")
    tbr, vbr = tr.emit(betb[:, 3:4], betr[0:1, :], pre_tensor_waits=[tcst], src_p=128, src_f=1)

    prev_cast = None
    for kc in range(8):
        for hh in range(2):
            if prev_cast is not None:
                b.wait("sync", prev_cast)
            t_w = b.dma("sync", whf[:, :], w_h[kc * 128:(kc + 1) * 128, hh * 2048:(hh + 1) * 2048],
                        sem="whld")
            b.wait("vector", t_w)
            prev_cast = b.op("vector", lambda e, kc=kc, hh=hh: e.tensor_copy(
                wh_sb[:, kc * 4096 + hh * 2048: kc * 4096 + (hh + 1) * 2048], whf[:, :]), inc=True)
            b.drain("vector")
    b.wait("sync", prev_cast)
    t_w = b.dma("sync", whf[:, 0:1024].rearrange("p (c q) -> p c q", q=128),
                w_qk.rearrange("(c p) q -> p c q", p=128), sem="whld")
    b.wait("vector", t_w)
    tqkw = b.op("vector", lambda e: e.tensor_copy(wqk_sb[:, :], whf[:, 0:1024]), inc=True)
    b.drain("vector")
    for kc in range(16):
        b.dma("gpsimd", wo_sb[:, kc * 1024:(kc + 1) * 1024], w_o[kc * 128:(kc + 1) * 128, :], sem="wold")
    b.dma("gpsimd", bhrow[:, :], b_h.unsqueeze(0), sem="wold")
    b.dma("gpsimd", bqkrow[:, :], b_qk.unsqueeze(0), sem="wold")
    b.dma("gpsimd", bo_row[:, :], b_o.unsqueeze(0), sem="wold")
    two = b.dtok("wold")

    b.wait("tensor", tqkw)
    b.wait("tensor", vbr)
    b.wait("tensor", two)
    for oc in range(8):
        for kc in range(8):
            b.op("tensor", lambda e, oc=oc, kc=kc: e.matmul(
                ps_a[0:1, 0:512], bcol[:, kc:kc + 1],
                wh_sb[:, kc * 4096 + oc * 512: kc * 4096 + (oc + 1) * 512],
                start=(kc == 0), stop=(kc == 7)))
        tb_ = b.mark("tensor")
        b.wait("vector", tb_)
        tvb = b.op("vector", lambda e, oc=oc: e.tensor_copy(
            bias_h[0:1, oc * 512:(oc + 1) * 512], ps_a[0:1, 0:512]), inc=True)
        b.wait("tensor", tvb)
    b.drain("vector")
    tbh = b.op("vector", lambda e: e.tensor_tensor(bias_h[0:1, :], bias_h[0:1, :],
                                                   bhrow[0:1, :], ALU.add), inc=True)
    for kc in range(8):
        b.op("tensor", lambda e, kc=kc: e.matmul(
            ps_a[0:1, 0:128], bcol[:, kc:kc + 1], wqk_sb[:, kc * 128:(kc + 1) * 128],
            start=(kc == 0), stop=(kc == 7)))
    tb_ = b.mark("tensor")
    b.wait("vector", tb_)
    b.op("vector", lambda e: e.tensor_copy(bias_qk[0:1, :], ps_a[0:1, 0:128]))
    b.drain("vector")
    tbqk = b.op("vector", lambda e: e.tensor_tensor(bias_qk[0:1, :], bias_qk[0:1, :],
                                                    bqkrow[0:1, :], ALU.add), inc=True)
    b.wait("tensor", tbqk)
    for kc in range(8):
        b.op("vector", lambda e, kc=kc: e.tensor_scalar(
            wh_sb[:, kc * 4096:(kc + 1) * 4096], wh_sb[:, kc * 4096:(kc + 1) * 4096],
            gcol[:, kc:kc + 1], None, ALU.mult))
        b.drain("vector")
        b.op("vector", lambda e, kc=kc: e.tensor_scalar(
            wqk_sb[:, kc * 128:(kc + 1) * 128], wqk_sb[:, kc * 128:(kc + 1) * 128],
            gcol[:, kc:kc + 1], None, ALU.mult))
        b.drain("vector")
    tsc = b.mark("vector")
    b.wait("tensor", tsc)

    trel = b.op("tensor", lambda e: e.matmul(ps_a[:, 0:32], ones_sb[0:1, :], relr[0:1, :],
                                             start=True, stop=True), inc=True)
    b.wait("vector", trel)
    b.op("vector", lambda e: e.tensor_scalar(relbc[:], ps_a[:, 0:32], float(np.sqrt(128.0)),
                                             None, ALU.mult))
    b.drain("vector")
    prev_rm = None
    for jc in range(2):
        for k in range(32):
            if prev_rm is not None:
                b.wait("sync", prev_rm)
            t_m = b.dma("sync", rm[:, :], relmask_ap[:, (jc * 32 + k) * 256:(jc * 32 + k + 1) * 256],
                        sem="rmld")
            b.wait("vector", t_m)
            if k == 0:
                prev_rm = b.op("vector", lambda e: e.tensor_scalar(
                    acc[:], rm[:], relbc[:, 0:1], None, ALU.mult), inc=True)
            else:
                prev_rm = b.op("vector", lambda e, k=k: e.scalar_tensor_tensor(
                    acc[:], rm[:], relbc[:, k:k + 1], acc[:], ALU.mult, ALU.add), inc=True)
            b.drain("vector")
        b.op("vector", lambda e, jc=jc: e.tensor_copy(btT[:, jc * 256:(jc + 1) * 256], acc[:]))
        b.drain("vector")
    tbt = b.mark("vector")
    b.wait("tensor", tbt)

    prev_tln = None
    prev_pmm = None
    prev_ev = None
    first_lin = True
    qk_last = None
    for i in range(NT):
        if prev_tln is not None:
            b.wait("sync", prev_tln)
        t_x = b.dma("sync", xt[:], x_in[i * 128:(i + 1) * 128, :], sem="xld")
        b.wait("vector", t_x)
        t1 = emit_ln(b, xt, z1, st)
        b.wait("vector", t1)
        tln = emit_ln(b, z1, zn, st)
        prev_tln = tln
        for kc in range(8):
            pre = [tln]
            if kc == 0 and prev_pmm is not None:
                pre.append(prev_pmm)
            tt, vt = tr.emit(zn[:, kc * 128:(kc + 1) * 128], znT[:, kc * 128:(kc + 1) * 128],
                             pre_tensor_waits=pre)
        b.wait("vector", tt)
        b.wait("tensor", vt)
        for oc in range(8):
            if prev_ev is not None:
                b.wait("tensor", prev_ev)
            for kc in range(8):
                b.op("tensor", lambda e, oc=oc, kc=kc: e.matmul(
                    ps_a[:, 0:512], znT[:, kc * 128:(kc + 1) * 128],
                    wh_sb[:, kc * 4096 + oc * 512: kc * 4096 + (oc + 1) * 512],
                    start=(kc == 0), stop=False))
            tmm = b.op("tensor", lambda e, oc=oc: e.matmul(
                ps_a[:, 0:512], ones_sb[0:1, :], bias_h[0:1, oc * 512:(oc + 1) * 512],
                start=False, stop=True), inc=True)
            b.wait("scalar", tmm)
            if i > 0 and oc == 0:
                b.wait("scalar", prev_hvst)
            prev_ev = b.op("scalar", lambda e, oc=oc: e.activation(
                hv_t[:, oc * 512:(oc + 1) * 512], ps_a[:, 0:512], AF.Silu), inc=True)
        b.wait("tensor", prev_ev)
        for kc in range(8):
            b.op("tensor", lambda e, kc=kc: e.matmul(
                ps_a[:, 0:128], znT[:, kc * 128:(kc + 1) * 128], wqk_sb[:, kc * 128:(kc + 1) * 128],
                start=(kc == 0), stop=False))
        tmm = b.op("tensor", lambda e: e.matmul(
            ps_a[:, 0:128], ones_sb[0:1, :], bias_qk[0:1, :], start=False, stop=True), inc=True)
        b.wait("scalar", tmm)
        tqk = b.op("scalar", lambda e, i=i: e.activation(
            qk_all[:, i * 128:(i + 1) * 128], ps_a[:, 0:128], AF.Silu), inc=True)
        prev_ev = tqk
        b.wait("gpsimd", prev_ev)
        t_hv = b.dma("gpsimd", hv_scr[i * 128:(i + 1) * 128, :], hv_t[:, :], sem="hvst")
        prev_hvst = t_hv
        ttq, vtq = tr.emit(qk_all[:, i * 128:(i + 1) * 128], qk_fma[:, i * 128:(i + 1) * 128],
                           pre_tensor_waits=[tqk])
        qk_last = vtq
        for ec in range(4):
            b.op("tensor", lambda e, i=i, ec=ec, first=first_lin: e.matmul(
                ps_kv[:, ec * 512:(ec + 1) * 512], qk_all[:, i * 128:(i + 1) * 128],
                hv_t[:, ec * 512:(ec + 1) * 512], start=first, stop=(i == NT - 1)))
        for ec in range(16):
            b.op("tensor", lambda e, i=i, ec=ec, first=first_lin: e.matmul(
                ps_vs[:, ec:ec + 1], hv_t[:, ec * 128:(ec + 1) * 128], onesc[:, 0:1],
                start=(first and ec == 0), stop=(i == NT - 1 and ec == 15)))
        first_lin = False
        prev_pmm = b.mark("tensor")

    b.wait("vector", prev_pmm)
    tvs = b.op("vector", lambda e: e.tensor_copy(vs16[:, :], ps_vs[:, 0:16]), inc=True)
    ttv, vtv = tr.emit(vs16[:, 0:16], vsf[0:16, :], pre_tensor_waits=[tvs], src_p=128, src_f=16)
    b.wait("gpsimd", vtv)
    t_vs1 = b.dma("gpsimd", vs_scr[0:16, :], vsf[:, :], sem="vsst")
    b.wait("sync", t_vs1)
    t_vs2 = b.dma("sync", vsrow[0:1, :], vs_scr.flatten().unsqueeze(0), sem="vsld")
    tlkv = None
    for ec in range(4):
        tlkv = b.op("vector", lambda e, ec=ec: e.tensor_scalar(
            linkv[:, ec * 512:(ec + 1) * 512], ps_kv[:, ec * 512:(ec + 1) * 512],
            gam[:, 3:4], None, ALU.mult), inc=True)
        b.drain("vector")
    b.wait("tensor", tlkv)
    b.wait("tensor", t_vs2)
    tout = None
    for ec in range(4):
        tout = b.op("tensor", lambda e, ec=ec: e.matmul(
            ps_kv[:, ec * 512:(ec + 1) * 512], betr[0:1, :], vsrow[0:1, ec * 512:(ec + 1) * 512],
            start=True, stop=True), inc=(ec == 3))
    b.wait("vector", tout)
    for ec in range(4):
        b.op("vector", lambda e, ec=ec: e.tensor_tensor(
            linkv[:, ec * 512:(ec + 1) * 512], linkv[:, ec * 512:(ec + 1) * 512],
            ps_kv[:, ec * 512:(ec + 1) * 512], ALU.add))
        b.drain("vector")
    tlk2 = b.op("vector", lambda e: e.tensor_scalar(linkv2[:, :], linkv[:, :],
                                                    gam[:, 1:2], None, ALU.mult), inc=True)
    b.drain("vector")
    b.wait("tensor", tlk2)
    for ec in range(4):
        trt = b.op("tensor", lambda e, ec=ec: e.matmul(
            ps_a[0:1, 0:512], betb[:, 1:2], linkv[:, ec * 512:(ec + 1) * 512],
            start=True, stop=True), inc=True)
        b.wait("vector", trt)
        tvr = b.op("vector", lambda e, ec=ec: e.tensor_copy(
            rowt[0:1, ec * 512:(ec + 1) * 512], ps_a[0:1, 0:512]), inc=True)
        b.wait("tensor", tvr)
    rowt_ready = tvr
    b.wait("gpsimd", b.dtok("hvst"))
    scopeA.__exit__(None, None, None)

    ps_s = [b.ps(name + "_pss0", [128, 512], F32), b.ps(name + "_pss1", [128, 512], F32)]
    ps_o = [b.ps(name + "_pso0", [128, 512], F32), b.ps(name + "_pso1", [128, 512], F32)]
    ps_y = b.ps(name + "_psy", [128, 1024], F32)
    prev_store = prev_tln
    prev_simr = [None, None]
    prev_our = [None, None]
    prev_yr = None
    prev_aff = None
    b.wait("tensor", rowt_ready)
    b.wait("tensor", tlk2)
    prev_gho = None
    for g in range(16):
        if prev_gho is not None:
            b.wait("sync", prev_gho)
        hv_toks = []
        for t in range(2):
            hv_toks.append(b.dma("sync", vg_t[t][:, :],
                                 hv_scr[(g * 2 + t) * 128:(g * 2 + t + 1) * 128, 0:2048], sem="vgld"))
            hv_toks.append(b.dma("sync", gt_t[t][:, :],
                                 hv_scr[(g * 2 + t) * 128:(g * 2 + t + 1) * 128, 2048:4096], sem="vgld"))
        if prev_aff is not None:
            b.wait("vector", prev_aff)
        if prev_gho is not None:
            b.wait("vector", prev_gho)
        b.wait("vector", qk_last)
        tqq = b.op("vector", lambda e, g=g: e.tensor_scalar(
            qq[:, :], qk_fma[:, g * 256:(g + 1) * 256], gam[:, 0:1], bet[:, 0:1],
            ALU.mult, op1=ALU.add), inc=True)
        b.drain("vector")
        tkk = b.op("vector", lambda e, g=g: e.tensor_scalar(
            kk[:, :], qk_fma[:, g * 256:(g + 1) * 256], gam[:, 2:3], bet[:, 2:3],
            ALU.mult, op1=ALU.add), inc=True)
        b.drain("vector")
        b.wait("tensor", tkk)
        at_toks = []
        for jc in range(2):
            if prev_simr[jc] is not None:
                b.wait("tensor", prev_simr[jc])
            b.op("tensor", lambda e, jc=jc: e.matmul(
                ps_s[jc][:, 0:256], kk[:, jc * 128:(jc + 1) * 128], qq[:, 0:256],
                start=True, stop=False))
            tsm = b.op("tensor", lambda e, jc=jc: e.matmul(
                ps_s[jc][:, 0:256], idt[:, :], btT[:, jc * 256:(jc + 1) * 256],
                start=False, stop=True), inc=True)
            b.wait("vector", tsm)
            b.op("vector", lambda e, jc=jc: e.tensor_scalar(
                at1[:, :], ps_s[jc][:, 0:256], 0.0, None, ALU.max))
            b.drain("vector")
            tat = b.op("vector", lambda e, jc=jc: e.tensor_tensor(
                attnT[:, jc * 256:(jc + 1) * 256], at1[:, :], at1[:, :], ALU.mult), inc=True)
            b.drain("vector")
            prev_simr[jc] = tat
            at_toks.append(tat)
        prev_aff = tsm
        for t in range(2):
            b.wait("tensor", at_toks[-1])
            b.wait("tensor", hv_toks[-1])
            b.wait("tensor", qk_last)
            for ec in range(4):
                po = ps_o[ec % 2]
                if prev_our[ec % 2] is not None:
                    b.wait("tensor", prev_our[ec % 2])
                for jc in range(2):
                    b.op("tensor", lambda e, g=g, t=t, ec=ec, jc=jc, po=po: e.matmul(
                        po[:, 0:512], attnT[:, jc * 256 + t * 128: jc * 256 + (t + 1) * 128],
                        vg_t[jc][:, ec * 512:(ec + 1) * 512], start=(jc == 0), stop=False))
                b.op("tensor", lambda e, g=g, t=t, ec=ec, po=po: e.matmul(
                    po[:, 0:512], qk_fma[:, (g * 2 + t) * 128:(g * 2 + t + 1) * 128],
                    linkv2[:, ec * 512:(ec + 1) * 512], start=False, stop=False))
                tom = b.op("tensor", lambda e, ec=ec, po=po: e.matmul(
                    po[:, 0:512], ones_sb[0:1, :], rowt[0:1, ec * 512:(ec + 1) * 512],
                    start=False, stop=True), inc=True)
                b.wait("vector", tom)
                tho = b.op("vector", lambda e, t=t, ec=ec, po=po: e.tensor_tensor(
                    ho[:, ec * 512:(ec + 1) * 512], po[:, 0:512],
                    gt_t[t][:, ec * 512:(ec + 1) * 512], ALU.mult), inc=True)
                prev_our[ec % 2] = tho
            for dc in range(16):
                pre = [prev_our[(dc // 4 + 1) % 2], prev_our[dc % 2]] if dc == 0 else []
                tt2, vt2 = tr.emit(ho[:, dc * 128:(dc + 1) * 128], hoT[:, dc * 128:(dc + 1) * 128],
                                   pre_tensor_waits=pre)
            b.wait("vector", tt2)
            b.wait("tensor", vt2)
            i_t = g * 2 + t
            if prev_store is not None:
                b.wait("sync", prev_store)
            t_x = b.dma("sync", xt[:], x_in[i_t * 128:(i_t + 1) * 128, :], sem="xld2")
            b.wait("vector", t_x)
            t1 = emit_ln(b, xt, z1, st)
            if prev_yr is not None:
                b.wait("tensor", prev_yr)
            for n2 in range(2):
                for dc in range(16):
                    b.op("tensor", lambda e, n2=n2, dc=dc: e.matmul(
                        ps_y[:, n2 * 512:(n2 + 1) * 512], hoT[:, dc * 128:(dc + 1) * 128],
                        wo_sb[:, dc * 1024 + n2 * 512: dc * 1024 + (n2 + 1) * 512],
                        start=(dc == 0), stop=False))
                b.op("tensor", lambda e, n2=n2: e.matmul(
                    ps_y[:, n2 * 512:(n2 + 1) * 512], ones_sb[0:1, :],
                    bo_row[0:1, n2 * 512:(n2 + 1) * 512], start=False, stop=True))
            tym = b.mark("tensor")
            b.wait("vector", tym)
            b.wait("vector", t1)
            tvy = None
            for n2 in range(2):
                b.op("vector", lambda e, n2=n2: e.tensor_tensor(
                    xt[:, n2 * 512:(n2 + 1) * 512], xt[:, n2 * 512:(n2 + 1) * 512],
                    ps_y[:, n2 * 512:(n2 + 1) * 512], ALU.add))
                b.drain("vector")
                tvy = b.op("vector", lambda e, n2=n2: e.tensor_tensor(
                    xt[:, n2 * 512:(n2 + 1) * 512], xt[:, n2 * 512:(n2 + 1) * 512],
                    z1[:, n2 * 512:(n2 + 1) * 512], ALU.add), inc=True)
                b.drain("vector")
            prev_yr = tvy
            b.wait("gpsimd", tvy)
            prev_store = b.dma("gpsimd", x_out[i_t * 128:(i_t + 1) * 128, :], xt[:], sem="xst")
            prev_gho = tym


def build_l0_attn_launch():
    nc = bass.Bass("TRN2", target_bir_lowering=False)
    x_in = nc.dram_tensor("x", [N_TOK, D], F32, kind="ExternalInput").ap()
    norm_g = nc.dram_tensor("norm_g", [1024], F32, kind="ExternalInput").ap()
    norm_b = nc.dram_tensor("norm_b", [1024], F32, kind="ExternalInput").ap()
    w_h = nc.dram_tensor("w_h", [1024, 4096], F32, kind="ExternalInput").ap()
    b_h = nc.dram_tensor("b_h", [4096], F32, kind="ExternalInput").ap()
    w_qk = nc.dram_tensor("w_qk", [1024, 128], F32, kind="ExternalInput").ap()
    b_qk = nc.dram_tensor("b_qk", [128], F32, kind="ExternalInput").ap()
    qk_gamma = nc.dram_tensor("qk_gamma", [4, 128], F32, kind="ExternalInput").ap()
    qk_beta = nc.dram_tensor("qk_beta", [4, 128], F32, kind="ExternalInput").ap()
    rel_emb = nc.dram_tensor("rel_emb", [32, 1], F32, kind="ExternalInput").ap()
    w_o = nc.dram_tensor("w_o", [2048, 1024], F32, kind="ExternalInput").ap()
    b_o = nc.dram_tensor("b_o", [1024], F32, kind="ExternalInput").ap()
    ident = nc.dram_tensor("ident", [128, 128], BF16, kind="ExternalInput").ap()
    ones = nc.dram_tensor("ones", [1, 128], BF16, kind="ExternalInput").ap()
    onesc = nc.dram_tensor("onesc", [128, 1], BF16, kind="ExternalInput").ap()
    relmask = nc.dram_tensor("relmask", [128, 2 * 32 * 256], BF16, kind="ExternalInput").ap()
    x_out = nc.dram_tensor("x_out", [N_TOK, D], F32, kind="ExternalOutput").ap()
    hv_scr = nc.dram_tensor("hv_scr", [N_TOK, 4096], BF16).ap()
    vs_scr = nc.dram_tensor("vs_scr", [16, 128], BF16).ap()

    b = B(nc)
    emit_l0_attn(b, x_in, x_out, norm_g, norm_b, w_h, b_h, w_qk, b_qk, qk_gamma, qk_beta,
                 rel_emb, w_o, b_o, hv_scr, vs_scr, ident, ones, onesc, relmask, name="l0a")
    b.wait("gpsimd", b.dtok("xst"))
    return b.build()


def run_l0_attn(x_full, norm_g, norm_b, w_h, b_h, w_qk, b_qk, qk_gamma, qk_beta,
                rel_emb, w_o, b_o):
    nc = build_l0_attn_launch()
    in_maps = []
    for c in range(N_CORES):
        in_maps.append({
            "x": np.ascontiguousarray(x_full[c]),
            "norm_g": np.asarray(norm_g), "norm_b": np.asarray(norm_b),
            "w_h": np.asarray(w_h), "b_h": np.asarray(b_h),
            "w_qk": np.asarray(w_qk), "b_qk": np.asarray(b_qk),
            "qk_gamma": np.asarray(qk_gamma), "qk_beta": np.asarray(qk_beta),
            "rel_emb": np.asarray(rel_emb), "w_o": np.asarray(w_o), "b_o": np.asarray(b_o),
            "ident": _IDENT, "ones": _ONES, "onesc": _ONESC, "relmask": _RELMASK,
        })
    res = run_bass_kernel_spmd(nc, in_maps, core_ids=list(range(N_CORES)))
    return np.stack([res.results[c]["x_out"] for c in range(N_CORES)])



def _l2_maskadd_const():
    m = np.full((128, 128), -30000.0, dtype=bfdt)
    for g in range(16):
        m[g * 8:(g + 1) * 8, g * 8:(g + 1) * 8] = 0.0
    return m


_L2MASK = _l2_maskadd_const()


def emit_ln_only(b, x_in, xn_out, name="lno"):
    xt = b.sb(name + "_x", [128, 1024], F32)
    xn = b.sb(name + "_xn", [128, 1024], F32)
    st = {"stats": b.sb(name + "_st", [128, 12], F32),
          "mv": b.sb(name + "_mv", [128, 2], F32),
          "r1": b.sb(name + "_r1", [128, 1], F32)}
    prev = None
    prev_st = None
    for i in range(32):
        if prev is not None:
            b.wait("sync", prev)
        t_x = b.dma("sync", xt[:], x_in[i * 128:(i + 1) * 128, :], sem="xld")
        b.wait("vector", t_x)
        if prev_st is not None:
            b.wait("vector", prev_st)
        tln = emit_ln(b, xt, xn, st)
        prev = tln
        b.wait("gpsimd", tln)
        prev_st = b.dma("gpsimd", xn_out[i * 128:(i + 1) * 128, :], xn[:], sem="xst")


def build_l2_ln_launch():
    nc = bass.Bass("TRN2", target_bir_lowering=False)
    x_in = nc.dram_tensor("x", [N_TOK, D], F32, kind="ExternalInput").ap()
    xn_out = nc.dram_tensor("xn_out", [N_TOK, D], F32, kind="ExternalOutput").ap()
    b = B(nc)
    emit_ln_only(b, x_in, xn_out)
    b.wait("gpsimd", b.dtok("xst"))
    return b.build()


def emit_l2_attn_core(b, xn_in, out_hbm, wq, wk, wv, wo, bq, bk, bv, bo, gn, lq1, lk1, lq2, lk2,
                      idt_ap, ones_ap, maskadd_ap, name="l2a"):
    wq_sb = b.sb(name + "_wq", [128, 8 * 1024], BF16)
    wk_sb = b.sb(name + "_wk", [128, 8 * 1024], BF16)
    wv_sb = b.sb(name + "_wv", [128, 8 * 1024], BF16)
    wo_sb = b.sb(name + "_wo", [128, 8 * 1024], BF16)
    xt = b.sb(name + "_x", [128, 1024], BF16)
    xnT = b.sb(name + "_xnT", [128, 8 * 128], BF16)
    q_tm = b.sb(name + "_q", [128, 1024], BF16)
    k_tm = b.sb(name + "_k", [128, 1024], BF16)
    v_tm = b.sb(name + "_v", [128, 1024], BF16)
    q_fm = b.sb(name + "_qf", [128, 8 * 128], BF16)
    k_fm = b.sb(name + "_kf", [128, 8 * 128], BF16)
    brow = b.sb(name + "_brow", [1, 3 * 1024 + 1024], BF16)
    gncol = b.sb(name + "_gn", [128, 1], F32)
    g8 = b.sb(name + "_g8", [1, 128], BF16)
    lrow = b.sb(name + "_lrow", [1, 4 * 64], F32)
    lt = b.sb(name + "_lt", [1, 64], F32)
    lsc = b.sb(name + "_lsc", [1, 2], F32)
    nlam = b.sb(name + "_nlam", [1, 1], BF16)
    nlamc = b.sb(name + "_nlamc", [128, 1], F32)
    mka = b.sb(name + "_mka", [128, 128], BF16)
    sS = b.sb(name + "_sS", [128, 2 * 128], F32)
    mx = b.sb(name + "_mx", [128, 2], F32)
    sm = b.sb(name + "_sm", [128, 2], F32)
    pn = b.sb(name + "_pn", [128, 2 * 128], BF16)
    dd = b.sb(name + "_dd", [128, 128], BF16)
    ddT = b.sb(name + "_ddT", [128, 128], BF16)
    ao = b.sb(name + "_ao", [128, 1024], F32)
    aos = b.sb(name + "_aos", [128, 1024], F32)
    s2 = b.sb(name + "_s2", [128, 8], F32)
    ho = b.sb(name + "_ho", [128, 1024], BF16)
    hoT = b.sb(name + "_hoT", [128, 8 * 128], BF16)
    idt = b.sb(name + "_idt", [128, 128], BF16)
    ones_sb = b.sb(name + "_ones", [1, 128], BF16)
    tr = TransposeHelper(b, name, idt)
    ps_p = b.ps(name + "_psp", [128, 512], F32)
    ps_s = [b.ps(name + "_pss0", [128, 512], F32), b.ps(name + "_pss1", [128, 512], F32)]
    ps_o = b.ps(name + "_pso", [128, 512], F32)
    ps_y = b.ps(name + "_psy", [128, 1024], F32)

    b.dma("sync", idt[:], idt_ap[:, :], sem="wres")
    b.dma("sync", ones_sb[:], ones_ap[:, :], sem="wres")
    b.dma("sync", mka[:], maskadd_ap[:, :], sem="wres")
    for kc in range(8):
        b.dma("gpsimd", wq_sb[:, kc * 1024:(kc + 1) * 1024], wq[kc * 128:(kc + 1) * 128, :], sem="wcst")
        b.dma("gpsimd", wk_sb[:, kc * 1024:(kc + 1) * 1024], wk[kc * 128:(kc + 1) * 128, :], sem="wcst")
        b.dma("gpsimd", wv_sb[:, kc * 1024:(kc + 1) * 1024], wv[kc * 128:(kc + 1) * 128, :], sem="wcst")
        b.dma("gpsimd", wo_sb[:, kc * 1024:(kc + 1) * 1024], wo[kc * 128:(kc + 1) * 128, :], sem="wcst")
    b.dma("gpsimd", brow[0:1, 0:1024], bq.unsqueeze(0), sem="wcst")
    b.dma("gpsimd", brow[0:1, 1024:2048], bk.unsqueeze(0), sem="wcst")
    b.dma("gpsimd", brow[0:1, 2048:3072], bv.unsqueeze(0), sem="wcst")
    b.dma("gpsimd", brow[0:1, 3072:4096], bo.unsqueeze(0), sem="wcst")
    b.dma("gpsimd", g8[0:1, :], gn.unsqueeze(0), sem="wcst")
    b.dma("sync", lrow[0:1, 0:64], lq1.unsqueeze(0), sem="wres")
    b.dma("sync", lrow[0:1, 64:128], lk1.unsqueeze(0), sem="wres")
    b.dma("sync", lrow[0:1, 128:192], lq2.unsqueeze(0), sem="wres")
    b.dma("sync", lrow[0:1, 192:256], lk2.unsqueeze(0), sem="wres")
    w_tok = b.dtok("wres")
    c_tok = b.dtok("wcst")

    b.wait("vector", w_tok)
    b.wait("vector", c_tok)
    for j in range(2):
        b.op("vector", lambda e, j=j: e.tensor_tensor(
            lt[0:1, :], lrow[0:1, j * 128:j * 128 + 64], lrow[0:1, j * 128 + 64:j * 128 + 128],
            ALU.mult))
        b.drain("vector")
        b.op("vector", lambda e, j=j: e.tensor_reduce(lsc[0:1, j:j + 1], lt[0:1, :], AX.X, ALU.add))
        b.drain("vector")
    tsc = b.mark("vector")
    b.wait("scalar", tsc)
    tex = b.op("scalar", lambda e: e.activation(lsc[0:1, 0:2], lsc[0:1, 0:2], AF.Exp), inc=True)
    b.wait("vector", tex)
    b.op("vector", lambda e: e.tensor_tensor(lsc[0:1, 0:1], lsc[0:1, 1:2], lsc[0:1, 0:1],
                                             ALU.subtract))
    b.drain("vector")
    tnl = b.op("vector", lambda e: e.tensor_scalar(nlam[0:1, 0:1], lsc[0:1, 0:1], -0.8, None,
                                                   ALU.add), inc=True)
    b.wait("tensor", tnl)
    tbl = b.op("tensor", lambda e: e.matmul(ps_p[:, 0:1], ones_sb[0:1, :], nlam[0:1, 0:1],
                                            start=True, stop=True), inc=True)
    b.wait("vector", tbl)
    tnlc = b.op("vector", lambda e: e.tensor_copy(nlamc[:], ps_p[:, 0:1]), inc=True)
    ttg, vtg = tr.emit(g8[0:1, :], gncol[:, 0:1], pre_tensor_waits=[c_tok, tnlc],
                       src_p=1, src_f=128)
    b.wait("vector", vtg)
    b.drain("vector")
    tgs = b.op("vector", lambda e: e.tensor_scalar(gncol[:], gncol[:], 0.2, None, ALU.mult),
               inc=True)
    b.drain("vector")
    b.wait("vector", c_tok)
    for kc in range(8):
        b.op("vector", lambda e, kc=kc: e.tensor_scalar(
            wo_sb[:, kc * 1024:(kc + 1) * 1024], wo_sb[:, kc * 1024:(kc + 1) * 1024],
            gncol[:, 0:1], None, ALU.mult))
        b.drain("vector")
    tws = b.mark("vector")
    b.wait("tensor", tws)

    prev_store = None
    prev_pr = None
    prev_hoT = None
    for i in range(32):
        t_x = b.dma("gpsimd", xt[:], xn_in[i * 128:(i + 1) * 128, :], sem="xld")
        for kc in range(8):
            pre = [t_x]
            if kc == 0 and prev_pr is not None:
                pre.append(prev_pr)
            tt, vt = tr.emit(xt[:, kc * 128:(kc + 1) * 128], xnT[:, kc * 128:(kc + 1) * 128],
                             pre_tensor_waits=pre)
        b.wait("tensor", vt)
        evs = []
        for (wsb, dst, boff, scale) in ((wq_sb, q_tm, 0, 0.125), (wk_sb, k_tm, 1024, None),
                                        (wv_sb, v_tm, 2048, None)):
            for half in range(2):
                if evs:
                    b.wait("tensor", evs[-1])
                for kc in range(8):
                    b.op("tensor", lambda e, kc=kc, half=half, wsb=wsb: e.matmul(
                        ps_p[:, 0:512], xnT[:, kc * 128:(kc + 1) * 128],
                        wsb[:, kc * 1024 + half * 512: kc * 1024 + (half + 1) * 512],
                        start=(kc == 0), stop=False))
                tmm = b.op("tensor", lambda e, half=half, boff=boff: e.matmul(
                    ps_p[:, 0:512], ones_sb[0:1, :],
                    brow[0:1, boff + half * 512: boff + (half + 1) * 512],
                    start=False, stop=True), inc=True)
                b.wait("vector", tmm)
                if scale is None:
                    tev = b.op("vector", lambda e, half=half, dst=dst: e.tensor_copy(
                        dst[:, half * 512:(half + 1) * 512], ps_p[:, 0:512]), inc=True)
                else:
                    tev = b.op("vector", lambda e, half=half, dst=dst, scale=scale: e.tensor_scalar(
                        dst[:, half * 512:(half + 1) * 512], ps_p[:, 0:512],
                        scale, None, ALU.mult), inc=True)
                evs.append(tev)
        for kc in range(8):
            tt, vt = tr.emit(q_tm[:, kc * 128:(kc + 1) * 128], q_fm[:, kc * 128:(kc + 1) * 128],
                             pre_tensor_waits=[evs[1]] if kc == 0 else ())
            tt, vt2 = tr.emit(k_tm[:, kc * 128:(kc + 1) * 128], k_fm[:, kc * 128:(kc + 1) * 128],
                              pre_tensor_waits=[evs[3]] if kc == 0 else ())
        b.wait("tensor", vt2)
        prev_pr = b.tok("tensor")

        prev_ao = None
        for pr in range(8):
            exps = []
            for j in range(2):
                h = 2 * pr + j
                hc, par = h // 2, h % 2
                koff = par * 64
                b.op("tensor", lambda e, hc=hc, koff=koff, j=j: e.matmul(
                    ps_s[j][:, 0:128], q_fm[koff:koff + 64, hc * 128:(hc + 1) * 128],
                    k_fm[koff:koff + 64, hc * 128:(hc + 1) * 128], start=True, stop=False))
                tsm = b.op("tensor", lambda e, j=j: e.matmul(
                    ps_s[j][:, 0:128], idt[:, :], mka[:, :], start=False, stop=True), inc=True)
                b.wait("vector", tsm)
                tevs = b.op("vector", lambda e, j=j: e.tensor_copy(
                    sS[:, j * 128:(j + 1) * 128], ps_s[j][:, 0:128]), inc=True)
                b.drain("vector")
                b.op("vector", lambda e, j=j: e.tensor_reduce(
                    mx[:, j:j + 1], sS[:, j * 128:(j + 1) * 128], AX.X, ALU.max, negate=True))
                tmx = b.mark("vector")
                b.wait("scalar", tmx)
                texp = b.op("scalar", lambda e, j=j: e.activation(
                    sS[:, j * 128:(j + 1) * 128], sS[:, j * 128:(j + 1) * 128], AF.Exp,
                    bias=mx[:, j:j + 1]), inc=True)
                b.wait("vector", texp)
                b.drain("vector")
                b.op("vector", lambda e, j=j: e.tensor_reduce(
                    sm[:, j:j + 1], sS[:, j * 128:(j + 1) * 128], AX.X, ALU.add))
                b.drain("vector")
                b.op("vector", lambda e, j=j: e.reciprocal(sm[:, j:j + 1], sm[:, j:j + 1]))
                b.drain("vector")
                tpn = b.op("vector", lambda e, j=j: e.tensor_scalar(
                    pn[:, j * 128:(j + 1) * 128], sS[:, j * 128:(j + 1) * 128],
                    sm[:, j:j + 1], None, ALU.mult), inc=True)
                b.drain("vector")
                exps.append(tpn)
            tdd = b.op("vector", lambda e: e.scalar_tensor_tensor(
                dd[:, :], pn[:, 128:256], nlamc[:, 0:1], pn[:, 0:128], ALU.mult, ALU.add),
                inc=True)
            b.drain("vector")
            ttd, vtd = tr.emit(dd[:, :], ddT[:, :], pre_tensor_waits=[tdd])
            b.wait("tensor", vtd)
            if prev_ao is not None:
                b.wait("tensor", prev_ao)
            tav = b.op("tensor", lambda e, pr=pr: e.matmul(
                ps_o[:, 0:128], ddT[:, :], v_tm[:, pr * 128:(pr + 1) * 128],
                start=True, stop=True), inc=True)
            b.wait("vector", tav)
            prev_ao = b.op("vector", lambda e, pr=pr: e.tensor_copy(
                ao[:, pr * 128:(pr + 1) * 128], ps_o[:, 0:128]), inc=True)
        b.wait("scalar", prev_ao)
        if prev_store is not None:
            b.wait("scalar", prev_store)
        tsq = b.op("scalar", lambda e: e.activation(aos[:], ao[:], AF.Square), inc=True)
        b.wait("vector", tsq)
        b.drain("vector")
        b.op("vector", lambda e: e.tensor_reduce(
            s2[:], aos[:, :].rearrange("p (a c) -> p a c", c=128), AX.X, ALU.add))
        b.drain("vector")
        b.op("vector", lambda e: e.tensor_scalar(s2[:], s2[:], 1.0 / 128.0, 1e-5, ALU.mult,
                                                 op1=ALU.add))
        b.drain("vector")
        b.op("vector", lambda e: e.reciprocal(s2[:], s2[:]))
        trc2 = b.mark("vector")
        b.wait("scalar", trc2)
        trs = b.op("scalar", lambda e: e.activation(s2[:], s2[:], AF.Sqrt), inc=True)
        b.wait("vector", trs)
        tho = b.op("vector", lambda e: e.tensor_tensor(
            ho[:, :].rearrange("p (a c) -> p a c", c=128), ao[:, :].rearrange("p (a c) -> p a c", c=128),
            s2[:, :].unsqueeze(2).broadcast_to([128, 8, 128]), ALU.mult), inc=True)
        b.drain("vector")
        for kc in range(8):
            pre = [tho]
            if kc == 0 and prev_hoT is not None:
                pre.append(prev_hoT)
            tt, vt = tr.emit(ho[:, kc * 128:(kc + 1) * 128], hoT[:, kc * 128:(kc + 1) * 128],
                             pre_tensor_waits=pre if kc == 0 else ())
        b.wait("tensor", vt)
        for n2 in range(2):
            for kc in range(8):
                b.op("tensor", lambda e, n2=n2, kc=kc: e.matmul(
                    ps_y[:, n2 * 512:(n2 + 1) * 512], hoT[:, kc * 128:(kc + 1) * 128],
                    wo_sb[:, kc * 1024 + n2 * 512: kc * 1024 + (n2 + 1) * 512],
                    start=(kc == 0), stop=False))
            b.op("tensor", lambda e, n2=n2: e.matmul(
                ps_y[:, n2 * 512:(n2 + 1) * 512], ones_sb[0:1, :],
                brow[0:1, 3072 + n2 * 512: 3072 + (n2 + 1) * 512], start=False, stop=True))
        tym = b.mark("tensor")
        prev_hoT = tym
        b.wait("vector", tym)
        tvy = b.op("vector", lambda e: e.tensor_copy(aos[:], ps_y[:, 0:1024]), inc=True)
        b.wait("gpsimd", tvy)
        if prev_store is not None:
            b.wait("gpsimd", prev_store)
        prev_store = b.dma("gpsimd", out_hbm[i * 128:(i + 1) * 128, :], aos[:], sem="xst")
        b.wait("sync", tvy)
        b.wait("sync", prev_pr)


def build_l2_attn_core_launch():
    nc = bass.Bass("TRN2", target_bir_lowering=False)
    xn_in = nc.dram_tensor("xn", [N_TOK, D], F32, kind="ExternalInput").ap()
    names = ["Wq", "Wk", "Wv", "Wo"]
    w = {n: nc.dram_tensor(n, [1024, 1024], F32, kind="ExternalInput").ap() for n in names}
    bq = nc.dram_tensor("bq", [1024], F32, kind="ExternalInput").ap()
    bk = nc.dram_tensor("bk", [1024], F32, kind="ExternalInput").ap()
    bv = nc.dram_tensor("bv", [1024], F32, kind="ExternalInput").ap()
    bo = nc.dram_tensor("bo", [1024], F32, kind="ExternalInput").ap()
    gn = nc.dram_tensor("g_norm", [128], F32, kind="ExternalInput").ap()
    lq1 = nc.dram_tensor("lq1", [64], F32, kind="ExternalInput").ap()
    lk1 = nc.dram_tensor("lk1", [64], F32, kind="ExternalInput").ap()
    lq2 = nc.dram_tensor("lq2", [64], F32, kind="ExternalInput").ap()
    lk2 = nc.dram_tensor("lk2", [64], F32, kind="ExternalInput").ap()
    ident = nc.dram_tensor("ident", [128, 128], BF16, kind="ExternalInput").ap()
    ones = nc.dram_tensor("ones", [1, 128], BF16, kind="ExternalInput").ap()
    maskadd = nc.dram_tensor("maskadd", [128, 128], BF16, kind="ExternalInput").ap()
    out = nc.dram_tensor("out", [N_TOK, D], F32, kind="ExternalOutput").ap()
    wbf = {n: nc.dram_tensor(n + "_bf", [1024, 1024], BF16).ap() for n in names}

    b = B(nc)
    for n in names:
        emit_cast_pass(b, w[n], wbf[n], 1024)
    b.wait("gpsimd", b.dtok("ppcast"))
    emit_l2_attn_core(b, xn_in, out, wbf["Wq"], wbf["Wk"], wbf["Wv"], wbf["Wo"],
                      bq, bk, bv, bo, gn, lq1, lk1, lq2, lk2, ident, ones, maskadd)
    b.wait("gpsimd", b.dtok("xst"))
    return b.build()


def build_add_launch():
    nc = bass.Bass("TRN2", target_bir_lowering=False)
    x_in = nc.dram_tensor("x", [N_TOK, D], F32, kind="ExternalInput").ap()
    y_in = nc.dram_tensor("y", [N_TOK, D], F32, kind="ExternalInput").ap()
    x_out = nc.dram_tensor("x_out", [N_TOK, D], F32, kind="ExternalOutput").ap()
    b = B(nc)
    xt = b.sb("a_x", [128, 1024], F32)
    yt = b.sb("a_y", [128, 1024], F32)
    prev = None
    for i in range(32):
        if prev is not None:
            b.wait("sync", prev)
        t1 = b.dma("sync", xt[:], x_in[i * 128:(i + 1) * 128, :], sem="xld")
        t2 = b.dma("sync", yt[:], y_in[i * 128:(i + 1) * 128, :], sem="xld")
        b.wait("vector", t2)
        tv = b.op("vector", lambda e: e.tensor_tensor(xt[:], xt[:], yt[:], ALU.add), inc=True)
        b.drain("vector")
        b.wait("gpsimd", tv)
        prev = b.dma("gpsimd", x_out[i * 128:(i + 1) * 128, :], xt[:], sem="xst")
        b.wait("sync", tv)
    b.wait("gpsimd", b.dtok("xst"))
    return b.build()


def run_l2_attn(x_full, Wq, bq, Wk, bk, Wv, bv, Wo, bo, g_norm, lq1, lk1, lq2, lk2):
    nc1 = build_l2_ln_launch()
    in_maps = [{"x": np.ascontiguousarray(x_full[c])} for c in range(N_CORES)]
    r1 = run_bass_kernel_spmd(nc1, in_maps, core_ids=list(range(N_CORES)))
    xn = np.stack([r1.results[c]["xn_out"] for c in range(N_CORES)])
    xn_pm = [np.ascontiguousarray(
        xn[:, c * 512:(c + 1) * 512, :].transpose(1, 0, 2).reshape(N_TOK, D))
        for c in range(N_CORES)]
    nc2 = build_l2_attn_core_launch()
    in_maps = []
    for c in range(N_CORES):
        in_maps.append({
            "xn": xn_pm[c], "Wq": np.asarray(Wq), "Wk": np.asarray(Wk), "Wv": np.asarray(Wv),
            "Wo": np.asarray(Wo), "bq": np.asarray(bq), "bk": np.asarray(bk),
            "bv": np.asarray(bv), "bo": np.asarray(bo), "g_norm": np.asarray(g_norm),
            "lq1": np.asarray(lq1), "lk1": np.asarray(lk1), "lq2": np.asarray(lq2),
            "lk2": np.asarray(lk2),
            "ident": _IDENT, "ones": _ONES, "maskadd": _L2MASK,
        })
    r2 = run_bass_kernel_spmd(nc2, in_maps, core_ids=list(range(N_CORES)))
    outs = [r2.results[c]["out"].reshape(512, 8, D) for c in range(N_CORES)]
    out_full = np.concatenate(outs, axis=0).transpose(1, 0, 2)
    nc3 = build_add_launch()
    in_maps = [{"x": np.ascontiguousarray(x_full[c]), "y": np.ascontiguousarray(out_full[c])}
               for c in range(N_CORES)]
    r3 = run_bass_kernel_spmd(nc3, in_maps, core_ids=list(range(N_CORES)))
    return np.stack([r3.results[c]["x_out"] for c in range(N_CORES)])



def emit_l2_attn_fused(b, x_in, x_out, wq, wk, wv, wo, bq, bk, bv, bo, gn, lq1, lk1, lq2, lk2,
                       idt_ap, ones_ap, maskadd_ap, name="l2b"):
    wq_sb = b.sb(name + "_wq", [128, 8 * 1024], BF16)
    wk_sb = b.sb(name + "_wk", [128, 8 * 1024], BF16)
    wv_sb = b.sb(name + "_wv", [128, 8 * 1024], BF16)
    wo_sb = b.sb(name + "_wo", [128, 8 * 1024], BF16)
    xf = b.sb(name + "_xf", [128, 1024], F32)
    xt = b.sb(name + "_x", [128, 1024], BF16)
    xnT = b.sb(name + "_xnT", [128, 8 * 128], BF16)
    q_tm = b.sb(name + "_q", [128, 1024], BF16)
    k_tm = b.sb(name + "_k", [128, 1024], BF16)
    v_tm = b.sb(name + "_v", [128, 1024], BF16)
    q_fm = b.sb(name + "_qf", [128, 8 * 128], BF16)
    k_fm = b.sb(name + "_kf", [128, 8 * 128], BF16)
    brow = b.sb(name + "_brow", [1, 3 * 1024 + 1024], BF16)
    gncol = b.sb(name + "_gn", [128, 1], F32)
    g8 = b.sb(name + "_g8", [1, 128], BF16)
    lrow = b.sb(name + "_lrow", [1, 4 * 64], F32)
    lt = b.sb(name + "_lt", [1, 64], F32)
    lsc = b.sb(name + "_lsc", [1, 2], F32)
    nlam = b.sb(name + "_nlam", [1, 1], BF16)
    nlamc = b.sb(name + "_nlamc", [128, 1], F32)
    mka = b.sb(name + "_mka", [128, 128], BF16)
    sS = b.sb(name + "_sS", [128, 2 * 128], F32)
    mx = b.sb(name + "_mx", [128, 2], F32)
    sm = b.sb(name + "_sm", [128, 2], F32)
    pn = b.sb(name + "_pn", [128, 2 * 128], BF16)
    dd = b.sb(name + "_dd", [128, 128], BF16)
    ddT = b.sb(name + "_ddT", [128, 128], BF16)
    ao = b.sb(name + "_ao", [128, 1024], F32)
    aos = b.sb(name + "_aos", [128, 1024], F32)
    s2 = b.sb(name + "_s2", [128, 8], F32)
    ho = b.sb(name + "_ho", [128, 1024], BF16)
    hoT = b.sb(name + "_hoT", [128, 8 * 128], BF16)
    idt = b.sb(name + "_idt", [128, 128], BF16)
    ones_sb = b.sb(name + "_ones", [1, 128], BF16)
    st = {"stats": b.sb(name + "_st", [128, 12], F32),
          "mv": b.sb(name + "_mv", [128, 2], F32),
          "r1": b.sb(name + "_r1", [128, 1], F32)}
    tr = TransposeHelper(b, name, idt)
    ps_p = b.ps(name + "_psp", [128, 512], F32)
    ps_s = [b.ps(name + "_pss0", [128, 512], F32), b.ps(name + "_pss1", [128, 512], F32)]
    ps_o = b.ps(name + "_pso", [128, 512], F32)
    ps_y = b.ps(name + "_psy", [128, 1024], F32)

    b.dma("sync", idt[:], idt_ap[:, :], sem="wres")
    b.dma("sync", ones_sb[:], ones_ap[:, :], sem="wres")
    b.dma("sync", mka[:], maskadd_ap[:, :], sem="wres")
    for kc in range(8):
        b.dma("gpsimd", wq_sb[:, kc * 1024:(kc + 1) * 1024], wq[kc * 128:(kc + 1) * 128, :], sem="wcst")
        b.dma("gpsimd", wk_sb[:, kc * 1024:(kc + 1) * 1024], wk[kc * 128:(kc + 1) * 128, :], sem="wcst")
        b.dma("gpsimd", wv_sb[:, kc * 1024:(kc + 1) * 1024], wv[kc * 128:(kc + 1) * 128, :], sem="wcst")
        b.dma("gpsimd", wo_sb[:, kc * 1024:(kc + 1) * 1024], wo[kc * 128:(kc + 1) * 128, :], sem="wcst")
    b.dma("gpsimd", brow[0:1, 0:1024], bq.unsqueeze(0), sem="wcst")
    b.dma("gpsimd", brow[0:1, 1024:2048], bk.unsqueeze(0), sem="wcst")
    b.dma("gpsimd", brow[0:1, 2048:3072], bv.unsqueeze(0), sem="wcst")
    b.dma("gpsimd", brow[0:1, 3072:4096], bo.unsqueeze(0), sem="wcst")
    b.dma("gpsimd", g8[0:1, :], gn.unsqueeze(0), sem="wcst")
    b.dma("sync", lrow[0:1, 0:64], lq1.unsqueeze(0), sem="wres")
    b.dma("sync", lrow[0:1, 64:128], lk1.unsqueeze(0), sem="wres")
    b.dma("sync", lrow[0:1, 128:192], lq2.unsqueeze(0), sem="wres")
    b.dma("sync", lrow[0:1, 192:256], lk2.unsqueeze(0), sem="wres")
    w_tok = b.dtok("wres")
    c_tok = b.dtok("wcst")

    b.wait("vector", w_tok)
    b.wait("vector", c_tok)
    for j in range(2):
        b.op("vector", lambda e, j=j: e.tensor_tensor(
            lt[0:1, :], lrow[0:1, j * 128:j * 128 + 64], lrow[0:1, j * 128 + 64:j * 128 + 128],
            ALU.mult))
        b.drain("vector")
        b.op("vector", lambda e, j=j: e.tensor_reduce(lsc[0:1, j:j + 1], lt[0:1, :], AX.X, ALU.add))
        b.drain("vector")
    tsc = b.mark("vector")
    b.wait("scalar", tsc)
    tex = b.op("scalar", lambda e: e.activation(lsc[0:1, 0:2], lsc[0:1, 0:2], AF.Exp), inc=True)
    b.wait("vector", tex)
    b.op("vector", lambda e: e.tensor_tensor(lsc[0:1, 0:1], lsc[0:1, 1:2], lsc[0:1, 0:1],
                                             ALU.subtract))
    b.drain("vector")
    tnl = b.op("vector", lambda e: e.tensor_scalar(nlam[0:1, 0:1], lsc[0:1, 0:1], -0.8, None,
                                                   ALU.add), inc=True)
    b.wait("tensor", tnl)
    tbl = b.op("tensor", lambda e: e.matmul(ps_p[:, 0:1], ones_sb[0:1, :], nlam[0:1, 0:1],
                                            start=True, stop=True), inc=True)
    b.wait("vector", tbl)
    tnlc = b.op("vector", lambda e: e.tensor_copy(nlamc[:], ps_p[:, 0:1]), inc=True)
    ttg, vtg = tr.emit(g8[0:1, :], gncol[:, 0:1], pre_tensor_waits=[c_tok, tnlc],
                       src_p=1, src_f=128)
    b.wait("vector", vtg)
    b.drain("vector")
    tgs = b.op("vector", lambda e: e.tensor_scalar(gncol[:], gncol[:], 0.2, None, ALU.mult),
               inc=True)
    b.drain("vector")
    b.wait("vector", c_tok)
    for kc in range(8):
        b.op("vector", lambda e, kc=kc: e.tensor_scalar(
            wo_sb[:, kc * 1024:(kc + 1) * 1024], wo_sb[:, kc * 1024:(kc + 1) * 1024],
            gncol[:, 0:1], None, ALU.mult))
        b.drain("vector")
    tws = b.mark("vector")
    b.wait("tensor", tws)

    prev_store = None
    prev_pr = None
    prev_hoT = None
    prev_tln = None
    for i in range(32):
        if prev_tln is not None:
            b.wait("sync", prev_tln)
        t_x = b.dma("sync", xf[:], x_in[i * 128:(i + 1) * 128, :], sem="xldf")
        b.wait("vector", t_x)
        tln = emit_ln(b, xf, xt, st)
        prev_tln = tln
        for kc in range(8):
            pre = [tln]
            if kc == 0 and prev_pr is not None:
                pre.append(prev_pr)
            tt, vt = tr.emit(xt[:, kc * 128:(kc + 1) * 128], xnT[:, kc * 128:(kc + 1) * 128],
                             pre_tensor_waits=pre)
        b.wait("tensor", vt)
        evs = []
        for (wsb, dst, boff, scale) in ((wq_sb, q_tm, 0, 0.125), (wk_sb, k_tm, 1024, None),
                                        (wv_sb, v_tm, 2048, None)):
            for half in range(2):
                if evs:
                    b.wait("tensor", evs[-1])
                for kc in range(8):
                    b.op("tensor", lambda e, kc=kc, half=half, wsb=wsb: e.matmul(
                        ps_p[:, 0:512], xnT[:, kc * 128:(kc + 1) * 128],
                        wsb[:, kc * 1024 + half * 512: kc * 1024 + (half + 1) * 512],
                        start=(kc == 0), stop=False))
                tmm = b.op("tensor", lambda e, half=half, boff=boff: e.matmul(
                    ps_p[:, 0:512], ones_sb[0:1, :],
                    brow[0:1, boff + half * 512: boff + (half + 1) * 512],
                    start=False, stop=True), inc=True)
                b.wait("vector", tmm)
                if scale is None:
                    tev = b.op("vector", lambda e, half=half, dst=dst: e.tensor_copy(
                        dst[:, half * 512:(half + 1) * 512], ps_p[:, 0:512]), inc=True)
                else:
                    tev = b.op("vector", lambda e, half=half, dst=dst, scale=scale: e.tensor_scalar(
                        dst[:, half * 512:(half + 1) * 512], ps_p[:, 0:512],
                        scale, None, ALU.mult), inc=True)
                evs.append(tev)
        for kc in range(8):
            tt, vt = tr.emit(q_tm[:, kc * 128:(kc + 1) * 128], q_fm[:, kc * 128:(kc + 1) * 128],
                             pre_tensor_waits=[evs[1]] if kc == 0 else ())
            tt, vt2 = tr.emit(k_tm[:, kc * 128:(kc + 1) * 128], k_fm[:, kc * 128:(kc + 1) * 128],
                              pre_tensor_waits=[evs[3]] if kc == 0 else ())
        b.wait("tensor", vt2)
        prev_pr = b.tok("tensor")

        prev_ao = None
        for pr in range(8):
            for j in range(2):
                h = 2 * pr + j
                hc, par = h // 2, h % 2
                koff = par * 64
                b.op("tensor", lambda e, hc=hc, koff=koff, j=j: e.matmul(
                    ps_s[j][:, 0:128], q_fm[koff:koff + 64, hc * 128:(hc + 1) * 128],
                    k_fm[koff:koff + 64, hc * 128:(hc + 1) * 128], start=True, stop=False))
                tsm = b.op("tensor", lambda e, j=j: e.matmul(
                    ps_s[j][:, 0:128], idt[:, :], mka[:, :], start=False, stop=True), inc=True)
                b.wait("vector", tsm)
                tevs = b.op("vector", lambda e, j=j: e.tensor_copy(
                    sS[:, j * 128:(j + 1) * 128], ps_s[j][:, 0:128]), inc=True)
                b.drain("vector")
                b.op("vector", lambda e, j=j: e.tensor_reduce(
                    mx[:, j:j + 1], sS[:, j * 128:(j + 1) * 128], AX.X, ALU.max, negate=True))
                tmx = b.mark("vector")
                b.wait("scalar", tmx)
                texp = b.op("scalar", lambda e, j=j: e.activation(
                    sS[:, j * 128:(j + 1) * 128], sS[:, j * 128:(j + 1) * 128], AF.Exp,
                    bias=mx[:, j:j + 1]), inc=True)
                b.wait("vector", texp)
                b.drain("vector")
                b.op("vector", lambda e, j=j: e.tensor_reduce(
                    sm[:, j:j + 1], sS[:, j * 128:(j + 1) * 128], AX.X, ALU.add))
                b.drain("vector")
                b.op("vector", lambda e, j=j: e.reciprocal(sm[:, j:j + 1], sm[:, j:j + 1]))
                b.drain("vector")
                tpn = b.op("vector", lambda e, j=j: e.tensor_scalar(
                    pn[:, j * 128:(j + 1) * 128], sS[:, j * 128:(j + 1) * 128],
                    sm[:, j:j + 1], None, ALU.mult), inc=True)
                b.drain("vector")
            tdd = b.op("vector", lambda e: e.scalar_tensor_tensor(
                dd[:, :], pn[:, 128:256], nlamc[:, 0:1], pn[:, 0:128], ALU.mult, ALU.add),
                inc=True)
            b.drain("vector")
            ttd, vtd = tr.emit(dd[:, :], ddT[:, :], pre_tensor_waits=[tdd])
            b.wait("tensor", vtd)
            if prev_ao is not None:
                b.wait("tensor", prev_ao)
            tav = b.op("tensor", lambda e, pr=pr: e.matmul(
                ps_o[:, 0:128], ddT[:, :], v_tm[:, pr * 128:(pr + 1) * 128],
                start=True, stop=True), inc=True)
            b.wait("vector", tav)
            prev_ao = b.op("vector", lambda e, pr=pr: e.tensor_copy(
                ao[:, pr * 128:(pr + 1) * 128], ps_o[:, 0:128]), inc=True)
        b.wait("scalar", prev_ao)
        if prev_store is not None:
            b.wait("scalar", prev_store)
        tsq = b.op("scalar", lambda e: e.activation(aos[:], ao[:], AF.Square), inc=True)
        b.wait("vector", tsq)
        b.drain("vector")
        b.op("vector", lambda e: e.tensor_reduce(
            s2[:], aos[:, :].rearrange("p (a c) -> p a c", c=128), AX.X, ALU.add))
        b.drain("vector")
        b.op("vector", lambda e: e.tensor_scalar(s2[:], s2[:], 1.0 / 128.0, 1e-5, ALU.mult,
                                                 op1=ALU.add))
        b.drain("vector")
        b.op("vector", lambda e: e.reciprocal(s2[:], s2[:]))
        trc2 = b.mark("vector")
        b.wait("scalar", trc2)
        trs = b.op("scalar", lambda e: e.activation(s2[:], s2[:], AF.Sqrt), inc=True)
        b.wait("vector", trs)
        tho = b.op("vector", lambda e: e.tensor_tensor(
            ho[:, :].rearrange("p (a c) -> p a c", c=128), ao[:, :].rearrange("p (a c) -> p a c", c=128),
            s2[:, :].unsqueeze(2).broadcast_to([128, 8, 128]), ALU.mult), inc=True)
        b.drain("vector")
        for kc in range(8):
            pre = [tho]
            if kc == 0 and prev_hoT is not None:
                pre.append(prev_hoT)
            tt, vt = tr.emit(ho[:, kc * 128:(kc + 1) * 128], hoT[:, kc * 128:(kc + 1) * 128],
                             pre_tensor_waits=pre if kc == 0 else ())
        b.wait("tensor", vt)
        for n2 in range(2):
            for kc in range(8):
                b.op("tensor", lambda e, n2=n2, kc=kc: e.matmul(
                    ps_y[:, n2 * 512:(n2 + 1) * 512], hoT[:, kc * 128:(kc + 1) * 128],
                    wo_sb[:, kc * 1024 + n2 * 512: kc * 1024 + (n2 + 1) * 512],
                    start=(kc == 0), stop=False))
            b.op("tensor", lambda e, n2=n2: e.matmul(
                ps_y[:, n2 * 512:(n2 + 1) * 512], ones_sb[0:1, :],
                brow[0:1, 3072 + n2 * 512: 3072 + (n2 + 1) * 512], start=False, stop=True))
        tym = b.mark("tensor")
        prev_hoT = tym
        b.wait("vector", tym)
        tvy = b.op("vector", lambda e: e.tensor_tensor(aos[:], xf[:], ps_y[:, 0:1024], ALU.add),
                   inc=True)
        b.wait("gpsimd", tvy)
        if prev_store is not None:
            b.wait("gpsimd", prev_store)
        prev_store = b.dma("gpsimd", x_out[i * 128:(i + 1) * 128, :], aos[:], sem="xst")
        b.wait("sync", tvy)
        b.wait("sync", prev_pr)



def build_fused_launch1():
    nc = bass.Bass("TRN2", target_bir_lowering=False)
    x_in = nc.dram_tensor("x", [N_TOK, D], F32, kind="ExternalInput").ap()
    gw = nc.dram_tensor("l0f_gate_w", [8, 1024], F32, kind="ExternalInput").ap()
    w1 = nc.dram_tensor("l0f_w1", [8, 1024, 4096], F32, kind="ExternalInput").ap()
    b1 = nc.dram_tensor("l0f_b1", [8, 4096], F32, kind="ExternalInput").ap()
    w2 = nc.dram_tensor("l0f_w2", [8, 4096, 1024], F32, kind="ExternalInput").ap()
    b2 = nc.dram_tensor("l0f_b2", [8, 1024], F32, kind="ExternalInput").ap()
    norm_g = nc.dram_tensor("l0a_norm_g", [1024], F32, kind="ExternalInput").ap()
    norm_b = nc.dram_tensor("l0a_norm_b", [1024], F32, kind="ExternalInput").ap()
    w_h = nc.dram_tensor("l0a_w_h", [1024, 4096], F32, kind="ExternalInput").ap()
    b_h = nc.dram_tensor("l0a_b_h", [4096], F32, kind="ExternalInput").ap()
    w_qk = nc.dram_tensor("l0a_w_qk", [1024, 128], F32, kind="ExternalInput").ap()
    b_qk = nc.dram_tensor("l0a_b_qk", [128], F32, kind="ExternalInput").ap()
    qk_gamma = nc.dram_tensor("l0a_qk_gamma", [4, 128], F32, kind="ExternalInput").ap()
    qk_beta = nc.dram_tensor("l0a_qk_beta", [4, 128], F32, kind="ExternalInput").ap()
    rel_emb = nc.dram_tensor("l0a_rel_emb", [32, 1], F32, kind="ExternalInput").ap()
    w_o = nc.dram_tensor("l0a_w_o", [2048, 1024], F32, kind="ExternalInput").ap()
    b_o = nc.dram_tensor("l0a_b_o", [1024], F32, kind="ExternalInput").ap()
    wg1 = nc.dram_tensor("l1f_w_gate", [3072, 1024], I32, kind="ExternalInput").ap()
    wu1 = nc.dram_tensor("l1f_w_up", [3072, 1024], I32, kind="ExternalInput").ap()
    wd1 = nc.dram_tensor("l1f_w_down", [1024, 3072], I32, kind="ExternalInput").ap()
    wq1 = nc.dram_tensor("l1a_wq", [1024, 1024], F32, kind="ExternalInput").ap()
    wk1 = nc.dram_tensor("l1a_wk", [1024, 1024], F32, kind="ExternalInput").ap()
    wv1 = nc.dram_tensor("l1a_wv", [1024, 1024], F32, kind="ExternalInput").ap()
    ident = nc.dram_tensor("ident", [128, 128], BF16, kind="ExternalInput").ap()
    ones = nc.dram_tensor("ones", [1, 128], BF16, kind="ExternalInput").ap()
    onesc = nc.dram_tensor("onesc", [128, 1], BF16, kind="ExternalInput").ap()
    cmask = nc.dram_tensor("cmask", [128, 4 * 512], BF16, kind="ExternalInput").ap()
    relmask = nc.dram_tensor("relmask", [128, 2 * 32 * 256], BF16, kind="ExternalInput").ap()
    x_out = nc.dram_tensor("x_out", [N_TOK, D], F32, kind="ExternalOutput").ap()
    xa = nc.dram_tensor("xa", [N_TOK, D], F32).ap()
    xb = nc.dram_tensor("xb", [N_TOK, D], F32).ap()
    hv_scr = nc.dram_tensor("hv_scr", [N_TOK, 4096], BF16).ap()
    vs_scr = nc.dram_tensor("vs_scr", [16, 128], BF16).ap()
    wg_bf = nc.dram_tensor("wg_bf", [3072, 1024], BF16).ap()
    wu_bf = nc.dram_tensor("wu_bf", [3072, 1024], BF16).ap()
    wd_bf = nc.dram_tensor("wd_bf", [1024, 3072], BF16).ap()
    wgT = nc.dram_tensor("wgT", [1024, 3072], BF16).ap()
    wuT = nc.dram_tensor("wuT", [1024, 3072], BF16).ap()
    wdT = nc.dram_tensor("wdT", [3072, 1024], BF16).ap()
    wq_bf = nc.dram_tensor("wq_bf", [1024, 1024], BF16).ap()
    wk_bf = nc.dram_tensor("wk_bf", [1024, 1024], BF16).ap()
    wv_bf = nc.dram_tensor("wv_bf", [1024, 1024], BF16).ap()

    b = B(nc)
    with b.scope():
        emit_l1_ffn_prep(b, wg1, wu1, wd1, wg_bf, wu_bf, wd_bf, wgT, wuT, wdT)
        emit_cast_pass(b, wq1, wq_bf, 1024)
        emit_cast_pass(b, wk1, wk_bf, 1024)
        emit_cast_pass(b, wv1, wv_bf, 1024)
    b.barrier()
    with b.scope():
        emit_l0_ffn(b, x_in, xa, w1, w2, b1, b2, gw, ident, ones, name="f0a")
    b.barrier()
    with b.scope():
        emit_l0_attn(b, xa, xb, norm_g, norm_b, w_h, b_h, w_qk, b_qk, qk_gamma, qk_beta,
                     rel_emb, w_o, b_o, hv_scr, vs_scr, ident, ones, onesc, relmask, name="f0b")
    b.barrier()
    with b.scope():
        emit_l0_ffn(b, xb, xa, w1, w2, b1, b2, gw, ident, ones, name="f0c")
    b.barrier()
    with b.scope():
        emit_glu_ffn(b, xa, xb, wgT, wuT, wdT, HID=3072, idt_ap=ident, scale=0.5, name="f1a")
    b.barrier()
    with b.scope():
        emit_mha(b, xb, xa, wq_bf, wk_bf, wv_bf, ident, ones, onesc, cmask,
                 causal=True, name="f1b")
    b.barrier()
    with b.scope():
        emit_glu_ffn(b, xa, x_out, wgT, wuT, wdT, HID=3072, idt_ap=ident, scale=0.5, name="f1c")
    b.barrier()
    return b.build()


def build_fused_launch2():
    nc = bass.Bass("TRN2", target_bir_lowering=False)
    x_in = nc.dram_tensor("x", [N_TOK, D], F32, kind="ExternalInput").ap()
    wg = nc.dram_tensor("l2f_w_gate", [2816, 1024], F32, kind="ExternalInput").ap()
    wu = nc.dram_tensor("l2f_w_up", [2816, 1024], F32, kind="ExternalInput").ap()
    wd = nc.dram_tensor("l2f_w_down", [1024, 2816], F32, kind="ExternalInput").ap()
    names = ["Wq", "Wk", "Wv", "Wo"]
    w2a = {n: nc.dram_tensor("l2a_" + n, [1024, 1024], F32, kind="ExternalInput").ap() for n in names}
    bq = nc.dram_tensor("l2a_bq", [1024], F32, kind="ExternalInput").ap()
    bk = nc.dram_tensor("l2a_bk", [1024], F32, kind="ExternalInput").ap()
    bv = nc.dram_tensor("l2a_bv", [1024], F32, kind="ExternalInput").ap()
    bo = nc.dram_tensor("l2a_bo", [1024], F32, kind="ExternalInput").ap()
    gn = nc.dram_tensor("l2a_g_norm", [128], F32, kind="ExternalInput").ap()
    lq1 = nc.dram_tensor("l2a_lq1", [64], F32, kind="ExternalInput").ap()
    lk1 = nc.dram_tensor("l2a_lk1", [64], F32, kind="ExternalInput").ap()
    lq2 = nc.dram_tensor("l2a_lq2", [64], F32, kind="ExternalInput").ap()
    lk2 = nc.dram_tensor("l2a_lk2", [64], F32, kind="ExternalInput").ap()
    rw = nc.dram_tensor("l3f_router_w", [16, 1024], F32, kind="ExternalInput").ap()
    rb = nc.dram_tensor("l3f_router_b", [16], F32, kind="ExternalInput").ap()
    w13 = nc.dram_tensor("l3f_w1", [4096, 1024], F32, kind="ExternalInput").ap()
    b13 = nc.dram_tensor("l3f_b1", [4096], F32, kind="ExternalInput").ap()
    w23 = nc.dram_tensor("l3f_w2", [1024, 4096], F32, kind="ExternalInput").ap()
    b23 = nc.dram_tensor("l3f_b2", [1024], F32, kind="ExternalInput").ap()
    ident = nc.dram_tensor("ident", [128, 128], BF16, kind="ExternalInput").ap()
    ones = nc.dram_tensor("ones", [1, 128], BF16, kind="ExternalInput").ap()
    maskadd = nc.dram_tensor("maskadd", [128, 128], BF16, kind="ExternalInput").ap()
    x_out = nc.dram_tensor("x_out", [N_TOK, D], F32, kind="ExternalOutput").ap()
    xa = nc.dram_tensor("xa", [N_TOK, D], F32).ap()
    xb = nc.dram_tensor("xb", [N_TOK, D], F32).ap()
    wgq_bf = nc.dram_tensor("wgq_bf", [2816, 1024], BF16).ap()
    wuq_bf = nc.dram_tensor("wuq_bf", [2816, 1024], BF16).ap()
    wdq_bf = nc.dram_tensor("wdq_bf", [1024, 2816], BF16).ap()
    wgqT = nc.dram_tensor("wgqT", [1024, 2816], BF16).ap()
    wuqT = nc.dram_tensor("wuqT", [1024, 2816], BF16).ap()
    wdqT = nc.dram_tensor("wdqT", [2816, 1024], BF16).ap()
    wbf = {n: nc.dram_tensor("l2a_" + n + "_bf", [1024, 1024], BF16).ap() for n in names}
    w1_bf = nc.dram_tensor("w1_bf", [4096, 1024], BF16).ap()
    w2_bf = nc.dram_tensor("w2_bf", [1024, 4096], BF16).ap()
    rw_bf = nc.dram_tensor("rw_bf", [16, 1024], BF16).ap()
    w1T = nc.dram_tensor("w1T", [1024, 4096], BF16).ap()
    w2T = nc.dram_tensor("w2T", [4096, 1024], BF16).ap()
    rwT = nc.dram_tensor("rwT", [1024, 16], BF16).ap()

    b = B(nc)
    with b.scope():
        qz = Fp4Quant(b, "pp", 2816)
        emit_fp4_weight_quant(b, wg, wgq_bf, 2816, 1024, qz, "ppg")
        emit_fp4_weight_quant(b, wu, wuq_bf, 2816, 1024, qz, "ppu")
        emit_fp4_weight_quant(b, wd, wdq_bf, 1024, 2816, qz, "ppd")
        for nm in ("ppg", "ppu", "ppd"):
            b.wait("sync", b.dtok(nm + "st"))
        tb = b.sb("pp_tb2", [128, 4096], BF16)
        emit_transpose_pass(b, wgq_bf, wgqT, 2816, 1024, tb)
        emit_transpose_pass(b, wuq_bf, wuqT, 2816, 1024, tb)
        emit_transpose_pass(b, wdq_bf, wdqT, 1024, 2816, tb)
        for n in names:
            emit_cast_pass(b, w2a[n], wbf[n], 1024)
        emit_cast_pass(b, w13, w1_bf, 4096)
        emit_cast_pass(b, w23, w2_bf, 1024)
        emit_cast_pass(b, rw, rw_bf, 16)
        b.wait("sync", b.dtok("ppcast"))
        emit_transpose_pass(b, w1_bf, w1T, 4096, 1024, tb)
        emit_transpose_pass(b, w2_bf, w2T, 1024, 4096, tb)
        emit_transpose_pass(b, rw_bf, rwT, 16, 1024, tb)
    b.barrier()
    with b.scope():
        emit_l2_ffn(b, x_in, xa, wgqT, wuqT, wdqT, ident, name="f2a")
    b.barrier()
    with b.scope():
        emit_l2_attn_fused(b, xa, xb, wbf["Wq"], wbf["Wk"], wbf["Wv"], wbf["Wo"],
                           bq, bk, bv, bo, gn, lq1, lk1, lq2, lk2, ident, ones, maskadd,
                           name="f2b")
    b.barrier()
    with b.scope():
        emit_l2_ffn(b, xb, xa, wgqT, wuqT, wdqT, ident, name="f2c")
    b.barrier()
    with b.scope():
        emit_l3_ffn(b, xa, x_out, w1T, w2T, rwT, b13, b23, rb, ident, ones, name="f3a")
    b.barrier()
    return b.build()


def build_fused_launch3():
    nc = bass.Bass("TRN2", target_bir_lowering=False)
    x_in = nc.dram_tensor("x", [N_TOK, D], F32, kind="ExternalInput").ap()
    wqkv = nc.dram_tensor("l3a_w_qkv", [3072, 1024], F32, kind="ExternalInput").ap()
    rw = nc.dram_tensor("l3f_router_w", [16, 1024], F32, kind="ExternalInput").ap()
    rb = nc.dram_tensor("l3f_router_b", [16], F32, kind="ExternalInput").ap()
    w13 = nc.dram_tensor("l3f_w1", [4096, 1024], F32, kind="ExternalInput").ap()
    b13 = nc.dram_tensor("l3f_b1", [4096], F32, kind="ExternalInput").ap()
    w23 = nc.dram_tensor("l3f_w2", [1024, 4096], F32, kind="ExternalInput").ap()
    b23 = nc.dram_tensor("l3f_b2", [1024], F32, kind="ExternalInput").ap()
    ident = nc.dram_tensor("ident", [128, 128], BF16, kind="ExternalInput").ap()
    ones = nc.dram_tensor("ones", [1, 128], BF16, kind="ExternalInput").ap()
    onesc = nc.dram_tensor("onesc", [128, 1], BF16, kind="ExternalInput").ap()
    cmask = nc.dram_tensor("cmask", [128, 4 * 512], BF16, kind="ExternalInput").ap()
    x_out = nc.dram_tensor("x_out", [N_TOK, D], F32, kind="ExternalOutput").ap()
    xa = nc.dram_tensor("xa", [N_TOK, D], F32).ap()
    qkv_bf = nc.dram_tensor("qkv_bf", [3072, 1024], BF16).ap()
    qkvT = nc.dram_tensor("qkvT", [1024, 3072], BF16).ap()
    w1_bf = nc.dram_tensor("w1_bf", [4096, 1024], BF16).ap()
    w2_bf = nc.dram_tensor("w2_bf", [1024, 4096], BF16).ap()
    rw_bf = nc.dram_tensor("rw_bf", [16, 1024], BF16).ap()
    w1T = nc.dram_tensor("w1T", [1024, 4096], BF16).ap()
    w2T = nc.dram_tensor("w2T", [4096, 1024], BF16).ap()
    rwT = nc.dram_tensor("rwT", [1024, 16], BF16).ap()

    b = B(nc)
    with b.scope():
        emit_cast_pass(b, wqkv, qkv_bf, 3072)
        emit_cast_pass(b, w13, w1_bf, 4096)
        emit_cast_pass(b, w23, w2_bf, 1024)
        emit_cast_pass(b, rw, rw_bf, 16)
        b.wait("sync", b.dtok("ppcast"))
        tb = b.sb("pp_tb4", [128, 4096], BF16)
        emit_transpose_pass(b, qkv_bf, qkvT, 3072, 1024, tb)
        emit_transpose_pass(b, w1_bf, w1T, 4096, 1024, tb)
        emit_transpose_pass(b, w2_bf, w2T, 1024, 4096, tb)
        emit_transpose_pass(b, rw_bf, rwT, 16, 1024, tb)
    b.barrier()
    with b.scope():
        emit_mha(b, x_in, xa, qkvT[:, 0:1024], qkvT[:, 1024:2048], qkvT[:, 2048:3072],
                 ident, ones, onesc, cmask, causal=False, name="f3b")
    b.barrier()
    with b.scope():
        emit_l3_ffn(b, xa, x_out, w1T, w2T, rwT, b13, b23, rb, ident, ones, name="f3c")
    b.barrier()
    return b.build()



def trigger_boundary(b, nc, cc_name, gin, g):
    cc = b.es.enter_context(nc.semaphore(cc_name))

    def cl(eng, gin=gin, g=g, cc=cc):
        eng.collective_compute(
            "AllGather", ALU.bypass,
            replica_groups=[list(range(N_CORES))],
            ins=[gin.opt()], outs=[g.opt()]).then_inc(cc)
    b.ops["gpsimd"].append(cl)
    return cc


def wait_boundary(b, cc):
    for e in DMA_QUEUES:
        b.ops[e].append(lambda eng, cc=cc: eng.wait_ge(cc, 1))


def emit_boundary(b, nc, cc_name, gin, g):
    wait_boundary(b, trigger_boundary(b, nc, cc_name, gin, g))


def build_megakernel():
    nc = bass.Bass("TRN2", target_bir_lowering=False)
    ET = mybir.EngineType
    pid = nc.partition_id(engines=[ET.SP, ET.Activation, ET.Pool])

    x_in = nc.dram_tensor("x", [N_TOK, D], F32, kind="ExternalInput").ap()
    gw = nc.dram_tensor("l0f_gate_w", [8, 1024], F32, kind="ExternalInput").ap()
    w1 = nc.dram_tensor("l0f_w1", [8, 1024, 4096], F32, kind="ExternalInput").ap()
    b1 = nc.dram_tensor("l0f_b1", [8, 4096], F32, kind="ExternalInput").ap()
    w2 = nc.dram_tensor("l0f_w2", [8, 4096, 1024], F32, kind="ExternalInput").ap()
    b2 = nc.dram_tensor("l0f_b2", [8, 1024], F32, kind="ExternalInput").ap()
    norm_g = nc.dram_tensor("l0a_norm_g", [1024], F32, kind="ExternalInput").ap()
    norm_b = nc.dram_tensor("l0a_norm_b", [1024], F32, kind="ExternalInput").ap()
    w_h = nc.dram_tensor("l0a_w_h", [1024, 4096], F32, kind="ExternalInput").ap()
    b_h = nc.dram_tensor("l0a_b_h", [4096], F32, kind="ExternalInput").ap()
    w_qk = nc.dram_tensor("l0a_w_qk", [1024, 128], F32, kind="ExternalInput").ap()
    b_qk = nc.dram_tensor("l0a_b_qk", [128], F32, kind="ExternalInput").ap()
    qk_gamma = nc.dram_tensor("l0a_qk_gamma", [4, 128], F32, kind="ExternalInput").ap()
    qk_beta = nc.dram_tensor("l0a_qk_beta", [4, 128], F32, kind="ExternalInput").ap()
    rel_emb = nc.dram_tensor("l0a_rel_emb", [32, 1], F32, kind="ExternalInput").ap()
    w_o = nc.dram_tensor("l0a_w_o", [2048, 1024], F32, kind="ExternalInput").ap()
    b_o = nc.dram_tensor("l0a_b_o", [1024], F32, kind="ExternalInput").ap()
    wg1 = nc.dram_tensor("l1f_w_gate", [3072, 1024], I32, kind="ExternalInput").ap()
    wu1 = nc.dram_tensor("l1f_w_up", [3072, 1024], I32, kind="ExternalInput").ap()
    wd1 = nc.dram_tensor("l1f_w_down", [1024, 3072], I32, kind="ExternalInput").ap()
    wq1 = nc.dram_tensor("l1a_wq", [1024, 1024], F32, kind="ExternalInput").ap()
    wk1 = nc.dram_tensor("l1a_wk", [1024, 1024], F32, kind="ExternalInput").ap()
    wv1 = nc.dram_tensor("l1a_wv", [1024, 1024], F32, kind="ExternalInput").ap()
    wg2 = nc.dram_tensor("l2f_w_gate", [2816, 1024], F32, kind="ExternalInput").ap()
    wu2 = nc.dram_tensor("l2f_w_up", [2816, 1024], F32, kind="ExternalInput").ap()
    wd2 = nc.dram_tensor("l2f_w_down", [1024, 2816], F32, kind="ExternalInput").ap()
    names = ["Wq", "Wk", "Wv", "Wo"]
    w2a = {n: nc.dram_tensor("l2a_" + n, [1024, 1024], F32, kind="ExternalInput").ap() for n in names}
    bq2 = nc.dram_tensor("l2a_bq", [1024], F32, kind="ExternalInput").ap()
    bk2 = nc.dram_tensor("l2a_bk", [1024], F32, kind="ExternalInput").ap()
    bv2 = nc.dram_tensor("l2a_bv", [1024], F32, kind="ExternalInput").ap()
    bo2 = nc.dram_tensor("l2a_bo", [1024], F32, kind="ExternalInput").ap()
    gn2 = nc.dram_tensor("l2a_g_norm", [128], F32, kind="ExternalInput").ap()
    lq1 = nc.dram_tensor("l2a_lq1", [64], F32, kind="ExternalInput").ap()
    lk1 = nc.dram_tensor("l2a_lk1", [64], F32, kind="ExternalInput").ap()
    lq2 = nc.dram_tensor("l2a_lq2", [64], F32, kind="ExternalInput").ap()
    lk2 = nc.dram_tensor("l2a_lk2", [64], F32, kind="ExternalInput").ap()
    wqkv = nc.dram_tensor("l3a_w_qkv", [3072, 1024], F32, kind="ExternalInput").ap()
    rw = nc.dram_tensor("l3f_router_w", [16, 1024], F32, kind="ExternalInput").ap()
    rb = nc.dram_tensor("l3f_router_b", [16], F32, kind="ExternalInput").ap()
    w13 = nc.dram_tensor("l3f_w1", [4096, 1024], F32, kind="ExternalInput").ap()
    b13 = nc.dram_tensor("l3f_b1", [4096], F32, kind="ExternalInput").ap()
    w23 = nc.dram_tensor("l3f_w2", [1024, 4096], F32, kind="ExternalInput").ap()
    b23 = nc.dram_tensor("l3f_b2", [1024], F32, kind="ExternalInput").ap()
    ident = nc.dram_tensor("ident", [128, 128], BF16, kind="ExternalInput").ap()
    ones = nc.dram_tensor("ones", [1, 128], BF16, kind="ExternalInput").ap()
    onesc = nc.dram_tensor("onesc", [128, 1], BF16, kind="ExternalInput").ap()
    cmask = nc.dram_tensor("cmask", [128, 4 * 512], BF16, kind="ExternalInput").ap()
    relmask = nc.dram_tensor("relmask", [128, 2 * 32 * 256], BF16, kind="ExternalInput").ap()
    maskadd = nc.dram_tensor("maskadd", [128, 128], BF16, kind="ExternalInput").ap()
    x_out = nc.dram_tensor("x_out", [N_TOK, D], F32, kind="ExternalOutput").ap()

    xa = nc.dram_tensor("xa", [N_TOK, D], F32).ap()
    xb = nc.dram_tensor("xb", [N_TOK, D], F32).ap()
    hv_scr = nc.dram_tensor("hv_scr", [N_TOK, 4096], BF16).ap()
    vs_scr = nc.dram_tensor("vs_scr", [16, 128], BF16).ap()
    wg_bf = nc.dram_tensor("wg_bf", [3072, 1024], BF16).ap()
    wu_bf = nc.dram_tensor("wu_bf", [3072, 1024], BF16).ap()
    wd_bf = nc.dram_tensor("wd_bf", [1024, 3072], BF16).ap()
    wgT = nc.dram_tensor("wgT", [1024, 3072], BF16).ap()
    wuT = nc.dram_tensor("wuT", [1024, 3072], BF16).ap()
    wdT = nc.dram_tensor("wdT", [3072, 1024], BF16).ap()
    wq_bf = nc.dram_tensor("wq_bf", [1024, 1024], BF16).ap()
    wk_bf = nc.dram_tensor("wk_bf", [1024, 1024], BF16).ap()
    wv_bf = nc.dram_tensor("wv_bf", [1024, 1024], BF16).ap()
    wgq_bf = nc.dram_tensor("wgq_bf", [2816, 1024], BF16).ap()
    wuq_bf = nc.dram_tensor("wuq_bf", [2816, 1024], BF16).ap()
    wdq_bf = nc.dram_tensor("wdq_bf", [1024, 2816], BF16).ap()
    wgqT = nc.dram_tensor("wgqT", [1024, 2816], BF16).ap()
    wuqT = nc.dram_tensor("wuqT", [1024, 2816], BF16).ap()
    wdqT = nc.dram_tensor("wdqT", [2816, 1024], BF16).ap()
    wbf = {n: nc.dram_tensor("l2a_" + n + "_bf", [1024, 1024], BF16).ap() for n in names}
    w1_bf = nc.dram_tensor("w1_bf", [4096, 1024], BF16).ap()
    w2_bf = nc.dram_tensor("w2_bf", [1024, 4096], BF16).ap()
    rw_bf = nc.dram_tensor("rw_bf", [16, 1024], BF16).ap()
    w1T = nc.dram_tensor("w1T", [1024, 4096], BF16).ap()
    w2T = nc.dram_tensor("w2T", [4096, 1024], BF16).ap()
    rwT = nc.dram_tensor("rwT", [1024, 16], BF16).ap()
    qkv_bf = nc.dram_tensor("qkv_bf", [3072, 1024], BF16).ap()
    qkvT = nc.dram_tensor("qkvT", [1024, 3072], BF16).ap()
    gin1 = nc.dram_tensor("gin1", [N_TOK, D], F32).ap()
    g1t = nc.dram_tensor("g1", [N_CORES * N_TOK, D], F32)
    xpm = nc.dram_tensor("xpm", [N_TOK, D], F32)
    gin2 = nc.dram_tensor("gin2", [N_TOK, D], F32).ap()
    g2t = nc.dram_tensor("g2", [N_CORES * N_TOK, D], F32)
    xbm = nc.dram_tensor("xbm", [N_TOK, D], F32).ap()

    b = B(nc)
    with b.scope():
        emit_l1_ffn_prep(b, wg1, wu1, wd1, wg_bf, wu_bf, wd_bf, wgT, wuT, wdT)
        emit_cast_pass(b, wq1, wq_bf, 1024)
        emit_cast_pass(b, wk1, wk_bf, 1024)
        emit_cast_pass(b, wv1, wv_bf, 1024)
    b.barrier()
    with b.scope():
        emit_l0_ffn(b, x_in, xa, w1, w2, b1, b2, gw, ident, ones, name="f0a")
    b.barrier()
    with b.scope():
        emit_l0_attn(b, xa, xb, norm_g, norm_b, w_h, b_h, w_qk, b_qk, qk_gamma, qk_beta,
                     rel_emb, w_o, b_o, hv_scr, vs_scr, ident, ones, onesc, relmask, name="f0b")
    b.barrier()
    with b.scope():
        emit_l0_ffn(b, xb, xa, w1, w2, b1, b2, gw, ident, ones, name="f0c")
    b.barrier()
    with b.scope():
        emit_glu_ffn(b, xa, xb, wgT, wuT, wdT, HID=3072, idt_ap=ident, scale=0.5, name="f1a")
    b.barrier()
    with b.scope():
        emit_mha(b, xb, xa, wq_bf, wk_bf, wv_bf, ident, ones, onesc, cmask,
                 causal=True, name="f1b")
    b.barrier()
    with b.scope():
        emit_glu_ffn(b, xa, gin1, wgT, wuT, wdT, HID=3072, idt_ap=ident, scale=0.5, name="f1c")
    b.barrier()
    cc1 = trigger_boundary(b, nc, "cc1", gin1, g1t.ap())
    xpm_ap = xpm.ap()
    with b.scope():
        qz = Fp4Quant(b, "pp", 2816)
        emit_fp4_weight_quant(b, wg2, wgq_bf, 2816, 1024, qz, "ppg")
        emit_fp4_weight_quant(b, wu2, wuq_bf, 2816, 1024, qz, "ppu")
        emit_fp4_weight_quant(b, wd2, wdq_bf, 1024, 2816, qz, "ppd")
        for nm in ("ppg", "ppu", "ppd"):
            b.wait("sync", b.dtok(nm + "st"))
        tb = b.sb("pp_tb2", [128, 4096], BF16)
        emit_transpose_pass(b, wgq_bf, wgqT, 2816, 1024, tb)
        emit_transpose_pass(b, wuq_bf, wuqT, 2816, 1024, tb)
        emit_transpose_pass(b, wdq_bf, wdqT, 1024, 2816, tb)
        for n in names:
            emit_cast_pass(b, w2a[n], wbf[n], 1024)
        emit_cast_pass(b, w13, w1_bf, 4096)
        emit_cast_pass(b, w23, w2_bf, 1024)
        emit_cast_pass(b, rw, rw_bf, 16)
        emit_cast_pass(b, wqkv, qkv_bf, 3072)
        b.wait("sync", b.dtok("ppcast"))
        emit_transpose_pass(b, w1_bf, w1T, 4096, 1024, tb)
        emit_transpose_pass(b, w2_bf, w2T, 1024, 4096, tb)
        emit_transpose_pass(b, rw_bf, rwT, 16, 1024, tb)
        emit_transpose_pass(b, qkv_bf, qkvT, 3072, 1024, tb)
    wait_boundary(b, cc1)
    g1v = g1t.ap().rearrange("(m q) d -> q m d", m=N_CORES)
    xpmv = xpm.ap().rearrange("(p m) d -> p m d", m=N_CORES)
    for m in range(N_CORES):
        q = DMA_QUEUES[m % len(DMA_QUEUES)]
        b.dma(q, xpmv[:, m:m + 1, :], g1v[ds(pid * 512, 512), m:m + 1, :], sem="bnd1")
    b.barrier()
    with b.scope():
        emit_l2_ffn(b, xpm_ap, xa, wgqT, wuqT, wdqT, ident, name="f2a")
    b.barrier()
    with b.scope():
        emit_l2_attn_fused(b, xa, xb, wbf["Wq"], wbf["Wk"], wbf["Wv"], wbf["Wo"],
                           bq2, bk2, bv2, bo2, gn2, lq1, lk1, lq2, lk2, ident, ones, maskadd,
                           name="f2b")
    b.barrier()
    with b.scope():
        emit_l2_ffn(b, xb, xa, wgqT, wuqT, wdqT, ident, name="f2c")
    b.barrier()
    with b.scope():
        emit_l3_ffn(b, xa, gin2, w1T, w2T, rwT, b13, b23, rb, ident, ones, name="f3a")
    b.barrier()
    emit_boundary(b, nc, "cc2", gin2, g2t.ap())
    g2v = g2t.ap().rearrange("(q m) d -> q m d", m=N_CORES)
    NSPL = 4
    CH = N_TOK // NSPL
    for k in range(NSPL):
        q = DMA_QUEUES[k % len(DMA_QUEUES)]
        src = g2v[k * CH:(k + 1) * CH, ds(pid, 1), :].rearrange("q m d -> (q m) d")
        b.dma(q, xbm[k * CH:(k + 1) * CH, :], src, sem="bnd2")
    b.barrier()
    with b.scope():
        emit_mha(b, xbm, xa, qkvT[:, 0:1024], qkvT[:, 1024:2048], qkvT[:, 2048:3072],
                 ident, ones, onesc, cmask, causal=False, name="f3b")
    b.barrier()
    with b.scope():
        emit_l3_ffn(b, xa, x_out, w1T, w2T, rwT, b13, b23, rb, ident, ones, name="f3c")
    b.barrier()
    return b.build()


def kernel(**inputs):
    I = inputs
    x = np.ascontiguousarray(np.asarray(I["L0_attn_x"], dtype=np.float32))
    nc = build_megakernel()
    in_maps = []
    for c in range(N_CORES):
        in_maps.append({
            "x": np.ascontiguousarray(x[c]),
            "l0f_gate_w": np.asarray(I["L0_ffn_gate_w"]), "l0f_w1": np.asarray(I["L0_ffn_w1"]),
            "l0f_b1": np.asarray(I["L0_ffn_b1"]), "l0f_w2": np.asarray(I["L0_ffn_w2"]),
            "l0f_b2": np.asarray(I["L0_ffn_b2"]),
            "l0a_norm_g": np.asarray(I["L0_attn_norm_g"]), "l0a_norm_b": np.asarray(I["L0_attn_norm_b"]),
            "l0a_w_h": np.asarray(I["L0_attn_W_h"]), "l0a_b_h": np.asarray(I["L0_attn_b_h"]),
            "l0a_w_qk": np.asarray(I["L0_attn_W_qk"]), "l0a_b_qk": np.asarray(I["L0_attn_b_qk"]),
            "l0a_qk_gamma": np.asarray(I["L0_attn_qk_gamma"]), "l0a_qk_beta": np.asarray(I["L0_attn_qk_beta"]),
            "l0a_rel_emb": np.asarray(I["L0_attn_rel_emb"]), "l0a_w_o": np.asarray(I["L0_attn_W_o"]),
            "l0a_b_o": np.asarray(I["L0_attn_b_o"]),
            "l1f_w_gate": np.asarray(I["L1_ffn_w_gate"]), "l1f_w_up": np.asarray(I["L1_ffn_w_up"]),
            "l1f_w_down": np.asarray(I["L1_ffn_w_down"]),
            "l1a_wq": np.asarray(I["L1_attn_Wq"]), "l1a_wk": np.asarray(I["L1_attn_Wk"]),
            "l1a_wv": np.asarray(I["L1_attn_Wv"]),
            "l2f_w_gate": np.asarray(I["L2_ffn_w_gate"]), "l2f_w_up": np.asarray(I["L2_ffn_w_up"]),
            "l2f_w_down": np.asarray(I["L2_ffn_w_down"]),
            "l2a_Wq": np.asarray(I["L2_attn_Wq"]), "l2a_Wk": np.asarray(I["L2_attn_Wk"]),
            "l2a_Wv": np.asarray(I["L2_attn_Wv"]), "l2a_Wo": np.asarray(I["L2_attn_Wo"]),
            "l2a_bq": np.asarray(I["L2_attn_bq"]), "l2a_bk": np.asarray(I["L2_attn_bk"]),
            "l2a_bv": np.asarray(I["L2_attn_bv"]), "l2a_bo": np.asarray(I["L2_attn_bo"]),
            "l2a_g_norm": np.asarray(I["L2_attn_g_norm"]),
            "l2a_lq1": np.asarray(I["L2_attn_lq1"]), "l2a_lk1": np.asarray(I["L2_attn_lk1"]),
            "l2a_lq2": np.asarray(I["L2_attn_lq2"]), "l2a_lk2": np.asarray(I["L2_attn_lk2"]),
            "l3a_w_qkv": np.asarray(I["L3_attn_W_qkv"]),
            "l3f_router_w": np.asarray(I["L3_ffn_router_w"]), "l3f_router_b": np.asarray(I["L3_ffn_router_b"]),
            "l3f_w1": np.asarray(I["L3_ffn_w1"]), "l3f_b1": np.asarray(I["L3_ffn_b1"]),
            "l3f_w2": np.asarray(I["L3_ffn_w2"]), "l3f_b2": np.asarray(I["L3_ffn_b2"]),
            "ident": _IDENT, "ones": _ONES, "onesc": _ONESC, "cmask": _CMASK,
            "relmask": _RELMASK, "maskadd": _L2MASK,
        })
    res = run_bass_kernel_spmd(nc, in_maps, core_ids=list(range(N_CORES)))
    return np.stack([res.results[c]["x_out"] for c in range(N_CORES)]).astype(np.float32)


def kernel_3launch(**inputs):
    I = inputs
    x = np.ascontiguousarray(np.asarray(I["L0_attn_x"], dtype=np.float32))
    consts = {"ident": _IDENT, "ones": _ONES, "onesc": _ONESC, "cmask": _CMASK,
              "relmask": _RELMASK}
    nc1 = build_fused_launch1()
    in_maps = []
    for c in range(N_CORES):
        in_maps.append({
            "x": np.ascontiguousarray(x[c]),
            "l0f_gate_w": np.asarray(I["L0_ffn_gate_w"]), "l0f_w1": np.asarray(I["L0_ffn_w1"]),
            "l0f_b1": np.asarray(I["L0_ffn_b1"]), "l0f_w2": np.asarray(I["L0_ffn_w2"]),
            "l0f_b2": np.asarray(I["L0_ffn_b2"]),
            "l0a_norm_g": np.asarray(I["L0_attn_norm_g"]), "l0a_norm_b": np.asarray(I["L0_attn_norm_b"]),
            "l0a_w_h": np.asarray(I["L0_attn_W_h"]), "l0a_b_h": np.asarray(I["L0_attn_b_h"]),
            "l0a_w_qk": np.asarray(I["L0_attn_W_qk"]), "l0a_b_qk": np.asarray(I["L0_attn_b_qk"]),
            "l0a_qk_gamma": np.asarray(I["L0_attn_qk_gamma"]), "l0a_qk_beta": np.asarray(I["L0_attn_qk_beta"]),
            "l0a_rel_emb": np.asarray(I["L0_attn_rel_emb"]), "l0a_w_o": np.asarray(I["L0_attn_W_o"]),
            "l0a_b_o": np.asarray(I["L0_attn_b_o"]),
            "l1f_w_gate": np.asarray(I["L1_ffn_w_gate"]), "l1f_w_up": np.asarray(I["L1_ffn_w_up"]),
            "l1f_w_down": np.asarray(I["L1_ffn_w_down"]),
            "l1a_wq": np.asarray(I["L1_attn_Wq"]), "l1a_wk": np.asarray(I["L1_attn_Wk"]),
            "l1a_wv": np.asarray(I["L1_attn_Wv"]),
            **consts,
        })
    r1 = run_bass_kernel_spmd(nc1, in_maps, core_ids=list(range(N_CORES)))
    x = np.stack([r1.results[c]["x_out"] for c in range(N_CORES)])
    x_pm = [np.ascontiguousarray(
        x[:, c * 512:(c + 1) * 512, :].transpose(1, 0, 2).reshape(N_TOK, D))
        for c in range(N_CORES)]
    nc2 = build_fused_launch2()
    in_maps = []
    for c in range(N_CORES):
        in_maps.append({
            "x": x_pm[c],
            "l2f_w_gate": np.asarray(I["L2_ffn_w_gate"]), "l2f_w_up": np.asarray(I["L2_ffn_w_up"]),
            "l2f_w_down": np.asarray(I["L2_ffn_w_down"]),
            "l2a_Wq": np.asarray(I["L2_attn_Wq"]), "l2a_Wk": np.asarray(I["L2_attn_Wk"]),
            "l2a_Wv": np.asarray(I["L2_attn_Wv"]), "l2a_Wo": np.asarray(I["L2_attn_Wo"]),
            "l2a_bq": np.asarray(I["L2_attn_bq"]), "l2a_bk": np.asarray(I["L2_attn_bk"]),
            "l2a_bv": np.asarray(I["L2_attn_bv"]), "l2a_bo": np.asarray(I["L2_attn_bo"]),
            "l2a_g_norm": np.asarray(I["L2_attn_g_norm"]),
            "l2a_lq1": np.asarray(I["L2_attn_lq1"]), "l2a_lk1": np.asarray(I["L2_attn_lk1"]),
            "l2a_lq2": np.asarray(I["L2_attn_lq2"]), "l2a_lk2": np.asarray(I["L2_attn_lk2"]),
            "l3f_router_w": np.asarray(I["L3_ffn_router_w"]), "l3f_router_b": np.asarray(I["L3_ffn_router_b"]),
            "l3f_w1": np.asarray(I["L3_ffn_w1"]), "l3f_b1": np.asarray(I["L3_ffn_b1"]),
            "l3f_w2": np.asarray(I["L3_ffn_w2"]), "l3f_b2": np.asarray(I["L3_ffn_b2"]),
            "ident": _IDENT, "ones": _ONES, "maskadd": _L2MASK,
        })
    r2 = run_bass_kernel_spmd(nc2, in_maps, core_ids=list(range(N_CORES)))
    outs = [r2.results[c]["x_out"].reshape(512, 8, D) for c in range(N_CORES)]
    x = np.ascontiguousarray(np.concatenate(outs, axis=0).transpose(1, 0, 2))
    nc3 = build_fused_launch3()
    in_maps = []
    for c in range(N_CORES):
        in_maps.append({
            "x": np.ascontiguousarray(x[c]),
            "l3a_w_qkv": np.asarray(I["L3_attn_W_qkv"]),
            "l3f_router_w": np.asarray(I["L3_ffn_router_w"]), "l3f_router_b": np.asarray(I["L3_ffn_router_b"]),
            "l3f_w1": np.asarray(I["L3_ffn_w1"]), "l3f_b1": np.asarray(I["L3_ffn_b1"]),
            "l3f_w2": np.asarray(I["L3_ffn_w2"]), "l3f_b2": np.asarray(I["L3_ffn_b2"]),
            "ident": _IDENT, "ones": _ONES, "onesc": _ONESC, "cmask": _CMASK,
        })
    r3 = run_bass_kernel_spmd(nc3, in_maps, core_ids=list(range(N_CORES)))
    return np.stack([r3.results[c]["x_out"] for c in range(N_CORES)]).astype(np.float32)


# version 11
# speedup vs baseline: 1.3714x; 1.3714x over previous
import numpy as np
import ml_dtypes
from contextlib import ExitStack
import concourse.bass as bass
import concourse.mybir as mybir
from concourse.bass_utils import run_bass_kernel_spmd

F32 = mybir.dt.float32
BF16 = mybir.dt.bfloat16
I32 = mybir.dt.int32
U32 = mybir.dt.uint32
AF = mybir.ActivationFunctionType
ALU = mybir.AluOpType
AX = mybir.AxisListType
bfdt = ml_dtypes.bfloat16
ds = bass.ds

ENGINES = ("sync", "tensor", "vector", "scalar", "gpsimd")
DMA_QUEUES = ("sync", "gpsimd", "scalar")

N_TOK = 4096
D = 1024
N_CORES = 8


class B:

    def __init__(self, nc):
        self.nc = nc
        self.es = ExitStack()
        self.ops = {e: [] for e in ENGINES}
        self.pc = {e: 0 for e in ENGINES}
        self.psem = {e: self.es.enter_context(nc.semaphore(f"p_{e}")) for e in ENGINES}
        self.dsem = {}
        self.dc = {}

    def sb(self, name, shape, dtype):
        st = getattr(self, "_scope_stack", None) or self.es
        return st.enter_context(self.nc.sbuf_tensor(name, shape, dtype))

    def ps(self, name, shape, dtype=F32):
        st = getattr(self, "_scope_stack", None) or self.es
        return st.enter_context(self.nc.psum_tensor(name, shape, dtype))

    def scope(self):
        b = self

        class _S:
            def __enter__(self):
                self.prev = getattr(b, "_scope_stack", None)
                self.stack = ExitStack()
                b._scope_stack = self.stack
                return self

            def __exit__(self, *a):
                self.stack.close()
                b._scope_stack = self.prev
                return False
        return _S()

    def op(self, e, fn, inc=False):
        if inc:
            self.pc[e] += 1
            sem = self.psem[e]

            def cl(eng, fn=fn, sem=sem):
                fn(eng).then_inc(sem, 1)
            cl._is_op = True
            self.ops[e].append(cl)
            return (e, "p", self.pc[e])

        def cl0(eng, fn=fn):
            return fn(eng)
        cl0._is_op = True
        self.ops[e].append(cl0)
        return None

    def mark(self, e):
        self.pc[e] += 1
        sem = self.psem[e]
        ops = self.ops[e]
        if ops and getattr(ops[-1], "_is_op", False):
            last = ops.pop()

            def cl(eng, last=last, sem=sem):
                inst = last(eng)
                inst.then_inc(sem, 1)
            cl._is_op = False
            ops.append(cl)
        else:
            def cl(eng, sem=sem):
                eng.drain().then_inc(sem, 1)
            ops.append(cl)
        return (e, "p", self.pc[e])

    def tok(self, e):
        return (e, "p", self.pc[e])

    def _dsem(self, name):
        if name not in self.dsem:
            self.dsem[name] = self.es.enter_context(self.nc.semaphore(f"d_{name}"))
            self.dc[name] = 0
        return self.dsem[name]

    def dma(self, q, out, in_, transpose=False, sem=None):
        name = sem or q
        s = self._dsem(name)
        self.dc[name] += 16
        if transpose:
            def cl(eng, out=out, in_=in_, s=s):
                eng.dma_start_transpose(out, in_).then_inc(s, 16)
        else:
            def cl(eng, out=out, in_=in_, s=s):
                eng.dma_start(out=out, in_=in_).then_inc(s, 16)
        self.ops[q].append(cl)
        return (name, "d", self.dc[name])

    def dtok(self, name):
        self._dsem(name)
        return (name, "d", self.dc[name])

    def wait(self, e, tok):
        if tok is None:
            return
        src, kind, val = tok
        if kind == "p" and src == e:
            self.drain(e)
            return
        key = (e, kind, src)
        if not hasattr(self, "_wmax"):
            self._wmax = {}
        if self._wmax.get(key, -1) >= val:
            return
        self._wmax[key] = val
        sem = self.psem[src] if kind == "p" else self.dsem[src]

        def cl(eng, sem=sem, val=val):
            eng.wait_ge(sem, val)
        self.ops[e].append(cl)

    def drain(self, e):
        self.ops[e].append(lambda eng: eng.drain())

    def barrier_tokens(self):
        toks = [(e, "p", self.pc[e]) for e in ENGINES if self.pc[e] > 0]
        toks += [(n, "d", c) for n, c in self.dc.items() if c > 0]
        return toks

    def barrier(self):
        toks = self.barrier_tokens()
        for e in ENGINES:
            for t in toks:
                if not (t[1] == "p" and t[0] == e):
                    self.wait(e, t)

    def build(self):
        with self.es:
            block = self.es.enter_context(self.nc.Block())

            def run(eng, oplist):
                for f in oplist:
                    f(eng)

            @block.sync
            def _(sync):
                run(sync, self.ops["sync"])

            @block.tensor
            def _(tensor):
                run(tensor, self.ops["tensor"])

            @block.vector
            def _(vector):
                run(vector, self.ops["vector"])

            @block.scalar
            def _(scalar):
                run(scalar, self.ops["scalar"])

            @block.gpsimd
            def _(gpsimd):
                run(gpsimd, self.ops["gpsimd"])
        return self.nc



def emit_ln(b, xt, xn_out, st, eps=1e-5, ncols=1024):
    v, s = "vector", "scalar"
    h = ncols // 2
    b.op(v, lambda e: e.bn_stats(st["stats"][:, 0:6], xt[:, 0:h]))
    b.op(v, lambda e: e.bn_stats(st["stats"][:, 6:12], xt[:, h:ncols]))
    b.drain(v)
    b.op(v, lambda e: e.bn_aggr(st["mv"][:], st["stats"][:]))
    b.drain(v)
    b.op(v, lambda e: e.tensor_scalar(st["r1"][:], st["mv"][:, 1:2], eps, None, ALU.add))
    b.drain(v)
    tv = b.op(v, lambda e: e.reciprocal(st["r1"][:], st["r1"][:]), inc=True)
    b.wait(s, tv)
    ts = b.op(s, lambda e: e.activation(st["r1"][:], st["r1"][:], AF.Sqrt), inc=True)
    b.wait(v, ts)
    tok = b.op(v, lambda e: e.tensor_scalar(xn_out[:], xt[:], st["mv"][:, 0:1], st["r1"][:],
                                            ALU.subtract, op1=ALU.mult), inc=True)
    return tok


class TransposeHelper:

    def __init__(self, b, name, idt):
        self.b = b
        self.idt = idt
        self.ps = [b.ps(name + "_t0", [128, 1024], BF16), b.ps(name + "_t1", [128, 1024], BF16)]
        self.vt = [None, None]
        self.i = 0

    def emit(self, src_tile, dst_tile, pre_tensor_waits=(), src_p=128, src_f=128):
        b = self.b
        slot = self.i % 2
        self.i += 1
        for t in pre_tensor_waits:
            b.wait("tensor", t)
        if self.vt[slot] is not None:
            b.wait("tensor", self.vt[slot])
        tt = b.op("tensor", lambda e, slot=slot, src=src_tile, p=src_p, f=src_f: e.transpose(
            self.ps[slot][0:f, 0:p], src, self.idt[0:p, 0:p]), inc=True)
        b.wait("vector", tt)
        vt = b.op("vector", lambda e, slot=slot, dst=dst_tile, p=src_p, f=src_f: e.tensor_copy(
            dst, self.ps[slot][0:f, 0:p]), inc=True)
        self.vt[slot] = vt
        return tt, vt



def emit_l1_ffn_prep(b, wg, wu, wd, wg_bf, wu_bf, wd_bf, wgT, wuT, wdT):
    i32 = b.sb("pp_i32", [128, 3072], I32)
    bft = b.sb("pp_bf", [128, 3072], BF16)
    SC = 1.0 / 127.0
    for (w, w_bf, nrow, ncol) in ((wg, wg_bf, 3072, 1024), (wu, wu_bf, 3072, 1024), (wd, wd_bf, 1024, 3072)):
        for r in range(nrow // 128):
            t_in = b.dma("sync", i32[:, 0:ncol], w[r * 128:(r + 1) * 128, :], sem="ppld")
            b.wait("vector", t_in)
            tv = b.op("vector", lambda e, ncol=ncol: e.tensor_scalar(
                bft[:, 0:ncol], i32[:, 0:ncol], SC, None, ALU.mult), inc=True)
            b.wait("gpsimd", tv)
            t_out = b.dma("gpsimd", w_bf[r * 128:(r + 1) * 128, :], bft[:, 0:ncol], sem="ppst")
            b.wait("sync", tv)
            b.wait("vector", t_out)
    tb = b.sb("pp_tb", [128, 3072], BF16)
    b.wait("sync", b.dtok("ppst"))
    for (w_bf, wT, nrow, ncol) in ((wg_bf, wgT, 3072, 1024), (wu_bf, wuT, 3072, 1024), (wd_bf, wdT, 1024, 3072)):
        for kc in range(ncol // 128):
            t_in = b.dma("sync", tb[:, 0:nrow], w_bf[0:nrow, kc * 128:(kc + 1) * 128], transpose=True, sem="pptr")
            b.wait("gpsimd", t_in)
            t_out = b.dma("gpsimd", wT[kc * 128:(kc + 1) * 128, 0:nrow], tb[:, 0:nrow], sem="ppst2")
            b.wait("sync", t_out)



def emit_glu_ffn(b, x_in, x_out, wgT, wuT, wdT, HID, idt_ap, scale=0.5, name="l1f"):
    NOC = HID // 128
    wg_sb = b.sb(name + "_wg", [128, 8 * HID], BF16)
    wu_sb = b.sb(name + "_wu", [128, 8 * HID], BF16)
    wd_sb = b.sb(name + "_wd", [128, 2 * 1024], BF16)
    xt = [b.sb(name + f"_x{i}", [128, 1024], F32) for i in range(4)]
    xn_tm = b.sb(name + "_xntm", [128, 1024], BF16)
    xnfm = b.sb(name + "_xnfm", [128, 8 * 512], BF16)
    h_sb = b.sb(name + "_h", [128, NOC * 512], BF16)
    sil = b.sb(name + "_sil", [128, 512], BF16)
    idt = b.sb(name + "_idt", [128, 128], BF16)
    st = {"stats": b.sb(name + "_st", [128, 12], F32),
          "mv": b.sb(name + "_mv", [128, 2], F32),
          "r1": b.sb(name + "_r1", [128, 1], F32)}
    ps_g = b.ps(name + "_psg", [128, 512], F32)
    ps_u = b.ps(name + "_psu", [128, 512], F32)
    ps_y = [b.ps(name + f"_psy{i}", [128, 512], F32) for i in range(4)]

    t_id = b.dma("sync", idt[:], idt_ap[:, :], sem="wres")
    tr = TransposeHelper(b, name, idt)

    for kc in range(8):
        b.dma("sync", wg_sb[:, kc * HID:(kc + 1) * HID], wgT[kc * 128:(kc + 1) * 128, :], sem="wres")
        b.dma("sync", wu_sb[:, kc * HID:(kc + 1) * HID], wuT[kc * 128:(kc + 1) * 128, :], sem="wres")
    t_id = b.dtok("wres")
    w_toks = [t_id]

    prev_store = [None] * 4
    wd_slot_tok = [None, None]
    prev_sil_read = None

    for tc in range(8):
        ln_evac_toks = []
        x_toks = []
        for i in range(4):
            if prev_store[i] is not None:
                b.wait("sync", prev_store[3])
            x_toks.append(b.dma("sync", xt[i][:], x_in[(tc * 4 + i) * 128:(tc * 4 + i + 1) * 128, :], sem="xld"))
        for i in range(4):
            b.wait("vector", x_toks[3])
            tln = emit_ln(b, xt[i], xn_tm, st)
            last_vt = None
            for kc in range(8):
                pre = [tln, t_id] if kc == 0 else []
                tt, vt = tr.emit(xn_tm[:, kc * 128:(kc + 1) * 128],
                                 xnfm[:, kc * 512 + i * 128: kc * 512 + (i + 1) * 128],
                                 pre_tensor_waits=pre)
                last_vt = vt
            b.wait("vector", tt)
            ln_evac_toks.append(last_vt)
        xnfm_ready = ln_evac_toks[-1]

        b.wait("tensor", xnfm_ready)
        for t in w_toks:
            b.wait("tensor", t)
        w_toks = []
        ev_toks = []
        for oc in range(NOC):
            if oc >= 1:
                b.wait("tensor", ev_toks[oc - 1][1])
            for kc in range(8):
                b.op("tensor", lambda e, oc=oc, kc=kc: e.matmul(
                    ps_g[:], wg_sb[:, kc * HID + oc * 128: kc * HID + (oc + 1) * 128],
                    xnfm[:, kc * 512:(kc + 1) * 512], start=(kc == 0), stop=(kc == 7)))
            tmm_g = b.mark("tensor")
            for kc in range(8):
                b.op("tensor", lambda e, oc=oc, kc=kc: e.matmul(
                    ps_u[:], wu_sb[:, kc * HID + oc * 128: kc * HID + (oc + 1) * 128],
                    xnfm[:, kc * 512:(kc + 1) * 512], start=(kc == 0), stop=(kc == 7)))
            tmm_u = b.mark("tensor")
            b.wait("scalar", tmm_g)
            if prev_sil_read is not None:
                b.wait("scalar", prev_sil_read)
            tsil = b.op("scalar", lambda e: e.activation(sil[:], ps_g[:], AF.Silu), inc=True)
            b.wait("vector", tsil)
            b.wait("vector", tmm_u)
            thv = b.op("vector", lambda e, oc=oc: e.tensor_tensor(
                h_sb[:, oc * 512:(oc + 1) * 512], sil[:], ps_u[:], ALU.mult), inc=True)
            prev_sil_read = thv
            ev_toks.append((tsil, thv))

        st_toks = []
        for half in range(2):
            for dc in range(NOC):
                slot = dc % 2
                if wd_slot_tok[slot] is not None:
                    b.wait("sync", wd_slot_tok[slot])
                t_w = b.dma("sync", wd_sb[:, slot * 1024:(slot + 1) * 1024],
                            wdT[dc * 128:(dc + 1) * 128, :], sem=f"wd{slot}")
                b.wait("tensor", t_w)
                b.wait("tensor", ev_toks[dc][1])
                last = None
                for t in range(2):
                    t128 = half * 2 + t
                    for n2 in range(2):
                        last = b.op("tensor", lambda e, dc=dc, t128=t128, t=t, n2=n2, slot=slot: e.matmul(
                            ps_y[t * 2 + n2][:],
                            h_sb[:, dc * 512 + t128 * 128: dc * 512 + (t128 + 1) * 128],
                            wd_sb[:, slot * 1024 + n2 * 512: slot * 1024 + (n2 + 1) * 512],
                            start=(dc == 0), stop=(dc == NOC - 1)),
                            inc=(True if (t == 1 and n2 == 1) else False))
                wd_slot_tok[slot] = last
            dn_done = b.tok("tensor")
            for t in range(2):
                t128 = half * 2 + t
                b.wait("vector", dn_done)
                for n2 in range(2):
                    tvy = b.op("vector", lambda e, t128=t128, t=t, n2=n2: e.scalar_tensor_tensor(
                        xt[t128][:, n2 * 512:(n2 + 1) * 512], ps_y[t * 2 + n2][:], scale,
                        xt[t128][:, n2 * 512:(n2 + 1) * 512], ALU.mult, ALU.add), inc=True)
                    b.wait("gpsimd", tvy)
                st_toks.append(b.dma("gpsimd", x_out[(tc * 4 + t128) * 128:(tc * 4 + t128 + 1) * 128, :],
                                     xt[t128][:], sem="xst"))
            if half == 0:
                b.wait("tensor", b.tok("vector"))
        prev_store = st_toks



def build_l1_ffn_launch():
    nc = bass.Bass("TRN2", target_bir_lowering=False)
    x_in = nc.dram_tensor("x", [N_TOK, D], F32, kind="ExternalInput").ap()
    wg = nc.dram_tensor("w_gate", [3072, 1024], I32, kind="ExternalInput").ap()
    wu = nc.dram_tensor("w_up", [3072, 1024], I32, kind="ExternalInput").ap()
    wd = nc.dram_tensor("w_down", [1024, 3072], I32, kind="ExternalInput").ap()
    ident = nc.dram_tensor("ident", [128, 128], BF16, kind="ExternalInput").ap()
    x_out = nc.dram_tensor("x_out", [N_TOK, D], F32, kind="ExternalOutput").ap()
    wg_bf = nc.dram_tensor("wg_bf", [3072, 1024], BF16).ap()
    wu_bf = nc.dram_tensor("wu_bf", [3072, 1024], BF16).ap()
    wd_bf = nc.dram_tensor("wd_bf", [1024, 3072], BF16).ap()
    wgT = nc.dram_tensor("wgT", [1024, 3072], BF16).ap()
    wuT = nc.dram_tensor("wuT", [1024, 3072], BF16).ap()
    wdT = nc.dram_tensor("wdT", [3072, 1024], BF16).ap()

    b = B(nc)
    with b.scope():
        emit_l1_ffn_prep(b, wg, wu, wd, wg_bf, wu_bf, wd_bf, wgT, wuT, wdT)
    b.barrier()
    emit_glu_ffn(b, x_in, x_out, wgT, wuT, wdT, HID=3072, idt_ap=ident, scale=0.5, name="l1f")
    b.wait("gpsimd", b.dtok("xst"))
    return b.build()


_IDENT = np.eye(128, dtype=bfdt)


def run_l1_ffn(x_full, w_gate, w_up, w_down):
    nc = build_l1_ffn_launch()
    in_maps = []
    for c in range(N_CORES):
        in_maps.append({
            "x": np.ascontiguousarray(x_full[c]),
            "w_gate": np.asarray(w_gate), "w_up": np.asarray(w_up), "w_down": np.asarray(w_down),
            "ident": _IDENT,
        })
    res = run_bass_kernel_spmd(nc, in_maps, core_ids=list(range(N_CORES)))
    return np.stack([res.results[c]["x_out"] for c in range(N_CORES)])



class Fp4Quant:
    def __init__(self, b, name, F):
        self.b = b
        self.F = F
        self.A = b.sb(name + "_qA", [128, F], F32)
        self.Q = b.sb(name + "_qQ", [128, F], BF16)
        self.V = b.sb(name + "_qV", [128, F], BF16)
        self.G = b.sb(name + "_qG", [128, F], BF16)
        self.S = b.sb(name + "_qS", [128, F // 16], F32)
        self.S2 = b.sb(name + "_qS2", [128, F // 16], F32)
        self.prev = None
        self.prev_g = None

    def emit(self, src_ap, dst_ap, F=None, wait_toks=()):
        b = self.b
        F = F or self.F
        A, Q, V, G, S, S2 = self.A, self.Q, self.V, self.G, self.S, self.S2
        v, s = "vector", "scalar"
        for t in wait_toks:
            b.wait(v, t)
            b.wait(s, t)
        if self.prev is not None:
            b.wait(s, self.prev)
        ts_abs = b.op(s, lambda e: e.activation(A[:, 0:F], src_ap, AF.Abs), inc=True)
        ts_sgn = b.op(s, lambda e: e.activation(G[:, 0:F], src_ap, AF.Sign), inc=True)
        b.wait(v, ts_abs)
        b.op(v, lambda e: e.tensor_reduce(S[:, 0:F // 16],
             A[:, 0:F].rearrange("p (c s) -> p c s", s=16), AX.X, ALU.max))
        b.drain(v)
        b.op(v, lambda e: e.tensor_scalar(S[:, 0:F // 16], S[:, 0:F // 16],
             1.0 / 6.0, 1e-12, ALU.mult, op1=ALU.max))
        b.drain(v)
        b.op(v, lambda e: e.tensor_scalar(S2[:, 0:F // 16], S[:, 0:F // 16],
             0.5, None, ALU.mult))
        b.drain(v)
        b.op(v, lambda e: e.reciprocal(S[:, 0:F // 16], S[:, 0:F // 16]))
        b.drain(v)
        tA = b.op(v, lambda e: e.tensor_tensor(
            A[:, 0:F].rearrange("p (c s) -> p c s", s=16),
            A[:, 0:F].rearrange("p (c s) -> p c s", s=16),
            S[:, 0:F // 16].unsqueeze(2).broadcast_to([128, F // 16, 16]), ALU.mult),
            inc=True)
        b.drain(v)
        b.op(v, lambda e: e.tensor_scalar(Q[:, 0:F], A[:, 0:F], 0.25, None, ALU.is_gt))
        b.drain(v)
        for t in (0.75, 1.25, 1.75):
            b.op(v, lambda e, t=t: e.scalar_tensor_tensor(
                Q[:, 0:F], A[:, 0:F], t, Q[:, 0:F], ALU.is_gt, ALU.add))
            b.drain(v)
        b.op(v, lambda e: e.tensor_scalar(V[:, 0:F], A[:, 0:F], 2.5, None, ALU.is_gt))
        b.drain(v)
        b.op(v, lambda e: e.scalar_tensor_tensor(V[:, 0:F], A[:, 0:F], 3.5, V[:, 0:F], ALU.is_gt, ALU.add))
        b.drain(v)
        b.op(v, lambda e: e.scalar_tensor_tensor(Q[:, 0:F], V[:, 0:F], 2.0, Q[:, 0:F], ALU.mult, ALU.add))
        b.drain(v)
        b.op(v, lambda e: e.tensor_scalar(V[:, 0:F], A[:, 0:F], 5.0, 4.0, ALU.is_gt, op1=ALU.mult))
        b.drain(v)
        b.op(v, lambda e: e.tensor_tensor(Q[:, 0:F], Q[:, 0:F], V[:, 0:F], ALU.add))
        b.drain(v)
        b.op(v, lambda e: e.tensor_tensor(
            Q[:, 0:F].rearrange("p (c s) -> p c s", s=16),
            Q[:, 0:F].rearrange("p (c s) -> p c s", s=16),
            S2[:, 0:F // 16].unsqueeze(2).broadcast_to([128, F // 16, 16]), ALU.mult))
        b.drain(v)
        b.wait(v, ts_sgn)
        tok = b.op(v, lambda e: e.tensor_tensor(dst_ap, Q[:, 0:F], G[:, 0:F], ALU.mult), inc=True)
        b.drain(v)
        self.prev = tok
        return tok



def emit_fp4_weight_quant(b, w, wq_bf, nrow, ncol, qz, name):
    wt = b.sb(name + "_wt", [128, ncol], F32)
    wq = b.sb(name + "_wq", [128, ncol], BF16)
    prev_q = None
    for r in range(nrow // 128):
        if prev_q is not None:
            b.wait("sync", prev_q)
        t_in = b.dma("sync", wt[:, 0:ncol], w[r * 128:(r + 1) * 128, :], sem=name + "ld")
        tq = qz.emit(wt[:, 0:ncol], wq[:, 0:ncol], F=ncol, wait_toks=[t_in])
        prev_q = tq
        b.wait("gpsimd", tq)
        t_out = b.dma("gpsimd", wq_bf[r * 128:(r + 1) * 128, :], wq[:, 0:ncol], sem=name + "st")
        b.wait("vector", t_out)


def emit_transpose_pass(b, w_bf, wT, nrow, ncol, tbuf):
    for kc in range(ncol // 128):
        t_in = b.dma("sync", tbuf[:, 0:nrow], w_bf[0:nrow, kc * 128:(kc + 1) * 128], transpose=True, sem="pptr")
        b.wait("gpsimd", t_in)
        t_out = b.dma("gpsimd", wT[kc * 128:(kc + 1) * 128, 0:nrow], tbuf[:, 0:nrow], sem="ppst")
        b.wait("sync", t_out)


def emit_l2_ffn(b, x_in, x_out, wgqT, wuqT, wdqT, idt_ap, name="l2f"):
    HID = 2816
    NB = HID // 128
    OCS = [(i * 512, 512) for i in range(5)] + [(2560, 256)]
    wgq_sb = b.sb(name + "_wg", [128, 8 * HID], BF16)
    wuq_sb = b.sb(name + "_wu", [128, 8 * HID], BF16)
    wdq_sb = b.sb(name + "_wd", [128, NB * 1024], BF16)
    xt = b.sb(name + "_x", [128, 1024], F32)
    xn = b.sb(name + "_xn", [128, 1024], F32)
    xq = b.sb(name + "_xq", [128, 1024], BF16)
    xqT = b.sb(name + "_xqT", [128, 8 * 128], BF16)
    h_tm = b.sb(name + "_h", [128, HID], F32)
    hq = b.sb(name + "_hq", [128, HID], BF16)
    hqT = b.sb(name + "_hqT", [128, NB * 128], BF16)
    sil = b.sb(name + "_sil", [128, 512], F32)
    idt = b.sb(name + "_idt", [128, 128], BF16)
    st = {"stats": b.sb(name + "_st", [128, 12], F32),
          "mv": b.sb(name + "_mv", [128, 2], F32),
          "r1": b.sb(name + "_r1", [128, 1], F32)}
    qz = Fp4Quant(b, name, HID)
    ps_g2 = [b.ps(name + "_psg", [128, 512], F32), b.ps(name + "_psg1", [128, 512], F32)]
    ps_u2 = [b.ps(name + "_psu", [128, 512], F32), b.ps(name + "_psu1", [128, 512], F32)]
    ps_y = [b.ps(name + "_psy0", [128, 512], F32), b.ps(name + "_psy1", [128, 512], F32)]
    tr = TransposeHelper(b, name, idt)

    t_id = b.dma("sync", idt[:], idt_ap[:, :], sem="wres")
    for kc in range(8):
        b.dma("sync", wgq_sb[:, kc * HID:(kc + 1) * HID], wgqT[kc * 128:(kc + 1) * 128, :], sem="wres")
        b.dma("sync", wuq_sb[:, kc * HID:(kc + 1) * HID], wuqT[kc * 128:(kc + 1) * 128, :], sem="wres")
    for dc in range(NB):
        b.dma("sync", wdq_sb[:, dc * 1024:(dc + 1) * 1024], wdqT[dc * 128:(dc + 1) * 128, :], sem="wres")
    t_id = b.dtok("wres")
    w_toks = [t_id]

    prev_store = None
    prev_gu = None
    prev_dn = None
    first = True
    for i in range(32):
        if prev_store is not None:
            b.wait("sync", prev_store)
        t_x = b.dma("sync", xt[:], x_in[i * 128:(i + 1) * 128, :], sem="xld")
        b.wait("vector", t_x)
        tln = emit_ln(b, xt, xn, st)
        tq = qz.emit(xn[:], xq[:], F=1024, wait_toks=[tln])
        for kc in range(8):
            pre = [tq, t_id] if kc == 0 else []
            if kc == 0 and prev_gu is not None:
                pre.append(prev_gu)
            tt, vt = tr.emit(xq[:, kc * 128:(kc + 1) * 128], xqT[:, kc * 128:(kc + 1) * 128],
                             pre_tensor_waits=pre)
        xqT_ready = vt
        b.wait("vector", tt)

        b.wait("tensor", xqT_ready)
        if first:
            for t in w_toks:
                b.wait("tensor", t)
            w_toks = []
        ev = []
        for ocix, (off, n) in enumerate(OCS):
            ps_g = ps_g2[ocix % 2]
            ps_u = ps_u2[ocix % 2]
            if ocix >= 2:
                b.wait("tensor", ev[ocix - 2][1])
            for kc in range(8):
                b.op("tensor", lambda e, off=off, n=n, kc=kc, ps_g=ps_g: e.matmul(
                    ps_g[:, 0:n], xqT[:, kc * 128:(kc + 1) * 128],
                    wgq_sb[:, kc * HID + off: kc * HID + off + n], start=(kc == 0), stop=(kc == 7)))
            tg = b.mark("tensor")
            for kc in range(8):
                b.op("tensor", lambda e, off=off, n=n, kc=kc, ps_u=ps_u: e.matmul(
                    ps_u[:, 0:n], xqT[:, kc * 128:(kc + 1) * 128],
                    wuq_sb[:, kc * HID + off: kc * HID + off + n], start=(kc == 0), stop=(kc == 7)))
            tu = b.mark("tensor")
            b.wait("scalar", tg)
            if ev:
                b.wait("scalar", ev[-1][1])
            tsil = b.op("scalar", lambda e, n=n, ps_g=ps_g: e.activation(sil[:, 0:n], ps_g[:, 0:n], AF.Silu), inc=True)
            b.wait("vector", tsil)
            b.wait("vector", tu)
            thv = b.op("vector", lambda e, off=off, n=n, ps_u=ps_u: e.tensor_tensor(
                h_tm[:, off:off + n], sil[:, 0:n], ps_u[:, 0:n], ALU.mult), inc=True)
            ev.append((tsil, thv))
        gu_done = b.tok("tensor")
        prev_gu = gu_done

        tqh = qz.emit(h_tm[:], hq[:], F=HID, wait_toks=[ev[-1][1]])
        for dc in range(NB):
            pre = [tqh] if dc == 0 else []
            if dc == 0 and prev_dn is not None:
                pre.append(prev_dn)
            tt2, vt2 = tr.emit(hq[:, dc * 128:(dc + 1) * 128], hqT[:, dc * 128:(dc + 1) * 128],
                               pre_tensor_waits=pre)
        hqT_ready = vt2
        b.wait("vector", tt2)

        b.wait("tensor", hqT_ready)
        last = None
        for n2 in range(2):
            for dc in range(NB):
                last = b.op("tensor", lambda e, n2=n2, dc=dc: e.matmul(
                    ps_y[n2][:], hqT[:, dc * 128:(dc + 1) * 128],
                    wdq_sb[:, dc * 1024 + n2 * 512: dc * 1024 + (n2 + 1) * 512],
                    start=(dc == 0), stop=(dc == NB - 1)),
                    inc=(n2 == 1 and dc == NB - 1))
        prev_dn = last
        b.wait("vector", last)
        for n2 in range(2):
            tvy = b.op("vector", lambda e, n2=n2: e.scalar_tensor_tensor(
                xt[:, n2 * 512:(n2 + 1) * 512], ps_y[n2][:], 0.5,
                xt[:, n2 * 512:(n2 + 1) * 512], ALU.mult, ALU.add), inc=True)
            b.wait("gpsimd", tvy)
        prev_store = b.dma("gpsimd", x_out[i * 128:(i + 1) * 128, :], xt[:], sem="xst")
        first = False


def build_l2_ffn_launch():
    nc = bass.Bass("TRN2", target_bir_lowering=False)
    x_in = nc.dram_tensor("x", [N_TOK, D], F32, kind="ExternalInput").ap()
    wg = nc.dram_tensor("w_gate", [2816, 1024], F32, kind="ExternalInput").ap()
    wu = nc.dram_tensor("w_up", [2816, 1024], F32, kind="ExternalInput").ap()
    wd = nc.dram_tensor("w_down", [1024, 2816], F32, kind="ExternalInput").ap()
    ident = nc.dram_tensor("ident", [128, 128], BF16, kind="ExternalInput").ap()
    x_out = nc.dram_tensor("x_out", [N_TOK, D], F32, kind="ExternalOutput").ap()
    wgq_bf = nc.dram_tensor("wgq_bf", [2816, 1024], BF16).ap()
    wuq_bf = nc.dram_tensor("wuq_bf", [2816, 1024], BF16).ap()
    wdq_bf = nc.dram_tensor("wdq_bf", [1024, 2816], BF16).ap()
    wgqT = nc.dram_tensor("wgqT", [1024, 2816], BF16).ap()
    wuqT = nc.dram_tensor("wuqT", [1024, 2816], BF16).ap()
    wdqT = nc.dram_tensor("wdqT", [2816, 1024], BF16).ap()

    b = B(nc)
    with b.scope():
        qz = Fp4Quant(b, "pp", 2816)
        emit_fp4_weight_quant(b, wg, wgq_bf, 2816, 1024, qz, "ppg")
        emit_fp4_weight_quant(b, wu, wuq_bf, 2816, 1024, qz, "ppu")
        emit_fp4_weight_quant(b, wd, wdq_bf, 1024, 2816, qz, "ppd")
        for nm in ("ppg", "ppu", "ppd"):
            b.wait("sync", b.dtok(nm + "st"))
        tb = b.sb("pp_tb2", [128, 2816], BF16)
        emit_transpose_pass(b, wgq_bf, wgqT, 2816, 1024, tb)
        emit_transpose_pass(b, wuq_bf, wuqT, 2816, 1024, tb)
        emit_transpose_pass(b, wdq_bf, wdqT, 1024, 2816, tb)
    b.barrier()
    emit_l2_ffn(b, x_in, x_out, wgqT, wuqT, wdqT, ident, name="l2f")
    b.wait("gpsimd", b.dtok("xst"))
    return b.build()


def run_l2_ffn(x_full, w_gate, w_up, w_down):
    nc = build_l2_ffn_launch()
    in_maps = []
    for c in range(N_CORES):
        in_maps.append({
            "x": np.ascontiguousarray(x_full[c]),
            "w_gate": np.asarray(w_gate), "w_up": np.asarray(w_up), "w_down": np.asarray(w_down),
            "ident": _IDENT,
        })
    res = run_bass_kernel_spmd(nc, in_maps, core_ids=list(range(N_CORES)))
    return np.stack([res.results[c]["x_out"] for c in range(N_CORES)])



def emit_cast_pass(b, w, w_bf, nrow, sem="ppcast"):
    step = 2048
    for r0 in range(0, nrow, step):
        r1 = min(nrow, r0 + step)
        t = b.dma("gpsimd", w_bf[r0:r1, :], w[r0:r1, :], sem=sem)
    return t



def emit_l3_ffn(b, x_in, x_out, w1T, w2T, rwT, b1_ap, b2_ap, rb_ap, idt_ap, ones_ap, name="l3f"):
    w1_sb = b.sb(name + "_w1", [128, 8 * 4096], BF16)
    w2_sb = b.sb(name + "_w2", [128, 32 * 1024], BF16)
    rw_sb = b.sb(name + "_rw", [128, 8 * 16], BF16)
    b1_sb = b.sb(name + "_b1", [1, 4096], BF16)
    b2_sb = b.sb(name + "_b2", [1, 1024], BF16)
    rb_sb = b.sb(name + "_rb", [1, 16], BF16)
    ones_sb = b.sb(name + "_ones", [1, 128], BF16)
    idt = b.sb(name + "_idt", [128, 128], BF16)
    xt = b.sb(name + "_x", [128, 1024], F32)
    xn = b.sb(name + "_xn", [128, 1024], BF16)
    xnT = b.sb(name + "_xnT", [128, 8 * 128], BF16)
    h_tm = b.sb(name + "_h", [128, 4096], BF16)
    hT = b.sb(name + "_hT", [128, 32 * 128], BF16)
    lg = b.sb(name + "_lg", [128, 16], F32)
    topv = b.sb(name + "_topv", [128, 8], F32)
    mask = b.sb(name + "_mask", [128, 16], F32)
    st = {"stats": b.sb(name + "_st", [128, 12], F32),
          "mv": b.sb(name + "_mv", [128, 2], F32),
          "r1": b.sb(name + "_r1", [128, 1], F32)}
    ps_r = b.ps(name + "_psr", [128, 512], F32)
    ps_h = [b.ps(name + "_psh0", [128, 512], F32), b.ps(name + "_psh1", [128, 512], F32)]
    ps_y = [b.ps(name + "_psy0", [128, 512], F32), b.ps(name + "_psy1", [128, 512], F32)]
    tr = TransposeHelper(b, name, idt)

    b.dma("sync", idt[:], idt_ap[:, :], sem="wres")
    b.dma("sync", ones_sb[:], ones_ap[:, :], sem="wres")
    for kc in range(8):
        b.dma("sync", w1_sb[:, kc * 4096:(kc + 1) * 4096], w1T[kc * 128:(kc + 1) * 128, :], sem="wres")
        b.dma("sync", rw_sb[:, kc * 16:(kc + 1) * 16], rwT[kc * 128:(kc + 1) * 128, :], sem="wres")
    for dc in range(32):
        b.dma("sync", w2_sb[:, dc * 1024:(dc + 1) * 1024], w2T[dc * 128:(dc + 1) * 128, :], sem="wres")
    w_tok = b.dtok("wres")
    b.dma("gpsimd", b1_sb[:], b1_ap.unsqueeze(0), sem="wres2")
    b.dma("gpsimd", b2_sb[:], b2_ap.unsqueeze(0), sem="wres2")
    b.dma("gpsimd", rb_sb[:], rb_ap.unsqueeze(0), sem="wres2")
    w_tok2 = b.dtok("wres2")

    prev_store = None
    prev_hmm = None
    prev_dn = None
    first = True
    for i in range(32):
        if prev_store is not None:
            b.wait("sync", prev_store)
        t_x = b.dma("sync", xt[:], x_in[i * 128:(i + 1) * 128, :], sem="xld")
        b.wait("vector", t_x)
        tln = emit_ln(b, xt, xn, st)
        for kc in range(8):
            pre = [tln] + ([w_tok] if first else [])
            if kc == 0 and prev_hmm is not None:
                pre.append(prev_hmm)
            tt, vt = tr.emit(xn[:, kc * 128:(kc + 1) * 128], xnT[:, kc * 128:(kc + 1) * 128],
                             pre_tensor_waits=pre)
        xnT_ready = vt
        b.wait("vector", tt)

        b.wait("tensor", xnT_ready)
        if first:
            b.wait("tensor", w_tok2)
        for kc in range(8):
            b.op("tensor", lambda e, kc=kc: e.matmul(
                ps_r[:, 0:16], xnT[:, kc * 128:(kc + 1) * 128],
                rw_sb[:, kc * 16:(kc + 1) * 16], start=(kc == 0), stop=False))
        trt = b.op("tensor", lambda e: e.matmul(
            ps_r[:, 0:16], ones_sb[0:1, :], rb_sb[0:1, :], start=False, stop=True), inc=True)
        b.wait("vector", trt)
        b.op("vector", lambda e: e.tensor_copy(lg[:], ps_r[:, 0:16]))
        b.drain("vector")
        b.op("vector", lambda e: e.max(topv[:], lg[:]))
        b.drain("vector")
        tmask = b.op("vector", lambda e: e.tensor_scalar(
            mask[:], lg[:], topv[:, 3:4], None, ALU.is_ge), inc=True)

        hev = []
        for oc in range(8):
            ph = ps_h[oc % 2]
            if oc >= 2:
                b.wait("tensor", hev[oc - 2])
            for kc in range(8):
                b.op("tensor", lambda e, oc=oc, kc=kc, ph=ph: e.matmul(
                    ph[:], xnT[:, kc * 128:(kc + 1) * 128],
                    w1_sb[:, kc * 4096 + oc * 512: kc * 4096 + (oc + 1) * 512],
                    start=(kc == 0), stop=False))
            th = b.op("tensor", lambda e, oc=oc, ph=ph: e.matmul(
                ph[:], ones_sb[0:1, :], b1_sb[0:1, oc * 512:(oc + 1) * 512],
                start=False, stop=True), inc=True)
            b.wait("vector", th)
            if oc == 0:
                b.wait("vector", tmask)
            for j in range(2):
                blk = oc * 2 + j
                tev = b.op("vector", lambda e, oc=oc, j=j, blk=blk, ph=ph: e.tensor_scalar(
                    h_tm[:, oc * 512 + j * 256: oc * 512 + (j + 1) * 256],
                    ph[:, j * 256:(j + 1) * 256], 0.0, mask[:, blk:blk + 1],
                    ALU.max, op1=ALU.mult), inc=True)
            hev.append(tev)
        hmm_done = b.tok("tensor")
        prev_hmm = hmm_done

        for dc in range(32):
            pre = [hev[dc // 4]] if dc % 4 == 0 else []
            if dc == 0 and prev_dn is not None:
                pre.append(prev_dn)
            tt2, vt2 = tr.emit(h_tm[:, dc * 128:(dc + 1) * 128], hT[:, dc * 128:(dc + 1) * 128],
                               pre_tensor_waits=pre)
        hT_ready = vt2
        b.wait("vector", tt2)

        b.wait("tensor", hT_ready)
        last = None
        for n2 in range(2):
            for dc in range(32):
                b.op("tensor", lambda e, n2=n2, dc=dc: e.matmul(
                    ps_y[n2][:], hT[:, dc * 128:(dc + 1) * 128],
                    w2_sb[:, dc * 1024 + n2 * 512: dc * 1024 + (n2 + 1) * 512],
                    start=(dc == 0), stop=False))
            last = b.op("tensor", lambda e, n2=n2: e.matmul(
                ps_y[n2][:], ones_sb[0:1, :], b2_sb[0:1, n2 * 512:(n2 + 1) * 512],
                start=False, stop=True), inc=True)
        prev_dn = last
        b.wait("vector", last)
        for n2 in range(2):
            tvy = b.op("vector", lambda e, n2=n2: e.scalar_tensor_tensor(
                xt[:, n2 * 512:(n2 + 1) * 512], ps_y[n2][:], 0.5,
                xt[:, n2 * 512:(n2 + 1) * 512], ALU.mult, ALU.add), inc=True)
            b.wait("gpsimd", tvy)
        prev_store = b.dma("gpsimd", x_out[i * 128:(i + 1) * 128, :], xt[:], sem="xst")
        first = False


def build_l3_ffn_launch():
    nc = bass.Bass("TRN2", target_bir_lowering=False)
    x_in = nc.dram_tensor("x", [N_TOK, D], F32, kind="ExternalInput").ap()
    rw = nc.dram_tensor("router_w", [16, 1024], F32, kind="ExternalInput").ap()
    rb = nc.dram_tensor("router_b", [16], F32, kind="ExternalInput").ap()
    w1 = nc.dram_tensor("w1", [4096, 1024], F32, kind="ExternalInput").ap()
    b1 = nc.dram_tensor("b1", [4096], F32, kind="ExternalInput").ap()
    w2 = nc.dram_tensor("w2", [1024, 4096], F32, kind="ExternalInput").ap()
    b2 = nc.dram_tensor("b2", [1024], F32, kind="ExternalInput").ap()
    ident = nc.dram_tensor("ident", [128, 128], BF16, kind="ExternalInput").ap()
    ones = nc.dram_tensor("ones", [1, 128], BF16, kind="ExternalInput").ap()
    x_out = nc.dram_tensor("x_out", [N_TOK, D], F32, kind="ExternalOutput").ap()
    w1_bf = nc.dram_tensor("w1_bf", [4096, 1024], BF16).ap()
    w2_bf = nc.dram_tensor("w2_bf", [1024, 4096], BF16).ap()
    rw_bf = nc.dram_tensor("rw_bf", [16, 1024], BF16).ap()
    w1T = nc.dram_tensor("w1T", [1024, 4096], BF16).ap()
    w2T = nc.dram_tensor("w2T", [4096, 1024], BF16).ap()
    rwT = nc.dram_tensor("rwT", [1024, 16], BF16).ap()

    b = B(nc)
    with b.scope():
        emit_cast_pass(b, w1, w1_bf, 4096)
        emit_cast_pass(b, w2, w2_bf, 1024)
        emit_cast_pass(b, rw, rw_bf, 16)
        b.wait("sync", b.dtok("ppcast"))
        tb = b.sb("pp_tb3", [128, 4096], BF16)
        emit_transpose_pass(b, w1_bf, w1T, 4096, 1024, tb)
        emit_transpose_pass(b, w2_bf, w2T, 1024, 4096, tb)
        emit_transpose_pass(b, rw_bf, rwT, 16, 1024, tb)
    b.barrier()
    emit_l3_ffn(b, x_in, x_out, w1T, w2T, rwT, b1, b2, rb, ident, ones, name="l3f")
    b.wait("gpsimd", b.dtok("xst"))
    return b.build()


_ONES = np.ones((1, 128), dtype=bfdt)


def run_l3_ffn(x_full, rw, rb, w1, b1, w2, b2):
    nc = build_l3_ffn_launch()
    in_maps = []
    for c in range(N_CORES):
        in_maps.append({
            "x": np.ascontiguousarray(x_full[c]),
            "router_w": np.asarray(rw), "router_b": np.asarray(rb),
            "w1": np.asarray(w1), "b1": np.asarray(b1),
            "w2": np.asarray(w2), "b2": np.asarray(b2),
            "ident": _IDENT, "ones": _ONES,
        })
    res = run_bass_kernel_spmd(nc, in_maps, core_ids=list(range(N_CORES)))
    return np.stack([res.results[c]["x_out"] for c in range(N_CORES)])



def emit_l0_ffn(b, x_in, x_out, w1, w2, b1_ap, b2_ap, gw_ap, idt_ap, ones_ap, name="l0f"):
    E, DFF = 8, 4096
    xr = b.sb(name + "_xr", [128, 16 * 1024], F32)
    xnfm = b.sb(name + "_xnfm", [128, 8 * 2048], BF16)
    coef = b.sb(name + "_coef", [128, 16 * 8], F32)
    xn_tm = b.sb(name + "_xn", [128, 1024], BF16)
    gw_sb = b.sb(name + "_gw", [128, 8 * 8], BF16)
    w1s = [b.sb(name + f"_w1s{i}", [128, 8 * 1024], BF16) for i in range(2)]
    w2s = [b.sb(name + f"_w2s{i}", [128, 8 * 1024], BF16) for i in range(2)]
    h_fm = b.sb(name + "_h", [128, 8 * 512], BF16)
    b1c = b.sb(name + "_b1c", [32, 128], BF16)
    b1f = b.sb(name + "_b1f", [128, 32], F32)
    b2r = b.sb(name + "_b2r", [1, 1024], BF16)
    ones_sb = b.sb(name + "_ones", [1, 128], BF16)
    idt = b.sb(name + "_idt", [128, 128], BF16)
    lg = b.sb(name + "_lg", [128, 8], F32)
    mm_ = b.sb(name + "_mm", [128, 1], F32)
    nm_ = b.sb(name + "_nm", [128, 1], F32)
    ex_ = b.sb(name + "_ex", [128, 8], F32)
    ss_ = b.sb(name + "_ss", [128, 1], F32)
    msk = b.sb(name + "_msk", [128, 8], F32)
    st = {"stats": b.sb(name + "_st", [128, 12], F32),
          "mv": b.sb(name + "_mv", [128, 2], F32),
          "r1": b.sb(name + "_r1", [128, 1], F32)}
    ps_h = [b.ps(name + "_psh0", [128, 512], F32), b.ps(name + "_psh1", [128, 512], F32)]
    ps_y2 = [b.ps(name + "_psy", [128, 1024], F32), b.ps(name + "_psyb", [128, 1024], F32)]
    ps_m = ps_h[0]
    tr = TransposeHelper(b, name, idt)
    gwbf = b.sb(name + "_gwbf", [8, 1024], BF16)
    b.dma("sync", idt[:], idt_ap[:, :], sem="wres")
    b.dma("sync", ones_sb[:], ones_ap[:, :], sem="wres")
    b.dma("gpsimd", gwbf[:, :], gw_ap[:, :], sem="gwld")
    b.wait("tensor", b.dtok("wres"))
    w_tok0 = b.dtok("gwld")
    for kc in range(8):
        tt0, vt0 = tr.emit(gwbf[0:8, kc * 128:(kc + 1) * 128], gw_sb[:, kc * 8:(kc + 1) * 8],
                           pre_tensor_waits=[w_tok0] if kc == 0 else (), src_p=8)
    w_tok = vt0

    st_toks = [None] * 16
    slot_tok = [None, None]
    prev_apply = None
    for half in range(2):
        base = half * 2048
        x_toks = []
        for i in range(16):
            if st_toks[i] is not None:
                b.wait("sync", st_toks[15])
            x_toks.append(b.dma("sync", xr[:, i * 1024:(i + 1) * 1024],
                                x_in[base + i * 128: base + (i + 1) * 128, :], sem="xld"))
        gate_toks = []
        for i in range(16):
            b.wait("vector", x_toks[15])
            tln = emit_ln(b, xr[:, i * 1024:(i + 1) * 1024], xn_tm, st)
            for kc in range(8):
                pre = [tln] + ([w_tok] if (half == 0 and i == 0) else [])
                if kc == 0 and prev_apply is not None and i == 0:
                    pre.append(prev_apply)
                tt, vt = tr.emit(xn_tm[:, kc * 128:(kc + 1) * 128],
                                 xnfm[:, kc * 2048 + i * 128: kc * 2048 + (i + 1) * 128],
                                 pre_tensor_waits=pre)
            b.wait("vector", tt)
            xnT_done = vt
            b.wait("tensor", xnT_done)
            tg = None
            for kc in range(8):
                tg = b.op("tensor", lambda e, kc=kc, i=i: e.matmul(
                    ps_m[:, 0:8], xnfm[:, kc * 2048 + i * 128: kc * 2048 + (i + 1) * 128],
                    gw_sb[:, kc * 8:(kc + 1) * 8], start=(kc == 0), stop=(kc == 7)),
                    inc=(kc == 7))
            b.wait("vector", tg)
            b.op("vector", lambda e: e.tensor_copy(lg[:], ps_m[:, 0:8]))
            b.drain("vector")
            b.op("vector", lambda e: e.tensor_reduce(mm_[:], lg[:], AX.X, ALU.max))
            b.drain("vector")
            tnm = b.op("vector", lambda e: e.tensor_scalar(nm_[:], mm_[:], -1.0, None, ALU.mult), inc=True)
            b.wait("scalar", tnm)
            tex = b.op("scalar", lambda e: e.activation(ex_[:], lg[:], AF.Exp, bias=nm_[:],
                                                        accum_out=ss_[:]), inc=True)
            b.wait("vector", tex)
            b.op("vector", lambda e: e.reciprocal(ss_[:], ss_[:]))
            b.drain("vector")
            b.op("vector", lambda e: e.tensor_scalar(msk[:], lg[:], mm_[:, 0:1], None, ALU.is_equal))
            b.drain("vector")
            tcoef = b.op("vector", lambda e, i=i: e.tensor_scalar(
                coef[:, i * 8:(i + 1) * 8], msk[:], ss_[:, 0:1], 0.5, ALU.mult, op1=ALU.mult), inc=True)
            gate_toks.append(tcoef)
        phaseA_v = gate_toks[-1]

        ap_toks = [phaseA_v]
        last_hev = None
        prev_ttb = None
        tap_par = [None, None]
        for e in range(E):
            if e > 0:
                b.wait("gpsimd", prev_ttb)
            tb1 = b.dma("gpsimd", b1c[:, :], b1_ap[e].rearrange("(c p) -> c p", p=128), sem="b1ld")
            pre = [tb1]
            if last_hev is not None:
                b.wait("vector", last_hev)
            prev_ttb, tb1f = tr.emit(b1c[:, :], b1f[:], pre_tensor_waits=pre, src_p=32)
            if e > 0:
                b.wait("gpsimd", ap_toks[-1])
            tb2 = b.dma("gpsimd", b2r[:], b2_ap[e].unsqueeze(0), sem="b2ld")
            for dc in range(4):
                k = e * 4 + dc
                slot = k % 2
                if slot_tok[slot] is not None:
                    b.wait("gpsimd", slot_tok[slot])
                for kc in range(8):
                    b.dma("gpsimd", w1s[slot][:, kc * 1024:(kc + 1) * 1024],
                          w1[e, kc * 128:(kc + 1) * 128, dc * 1024:(dc + 1) * 1024],
                          sem=f"w1s{slot}")
                for kc in range(8):
                    b.dma("gpsimd", w2s[slot][:, kc * 1024:(kc + 1) * 1024],
                          w2[e, dc * 1024 + kc * 128: dc * 1024 + (kc + 1) * 128, :],
                          sem=f"w2s{slot}")
                tw1 = b.dtok(f"w1s{slot}")
                tw2 = b.dtok(f"w2s{slot}")
                for tc in range(4):
                    hev = []
                    for hc in range(8):
                        ph = ps_h[hc % 2]
                        if hc >= 2:
                            b.wait("tensor", hev[hc - 2])
                        if hc == 0 and tc == 0:
                            b.wait("tensor", tw1)
                            b.wait("tensor", tw2)
                            if dc == 0:
                                b.wait("tensor", tb1f)
                        for kc in range(8):
                            b.op("tensor", lambda e_, hc=hc, kc=kc, tc=tc, slot=slot, ph=ph: e_.matmul(
                                ph[:], w1s[slot][:, kc * 1024 + hc * 128: kc * 1024 + (hc + 1) * 128],
                                xnfm[:, kc * 2048 + tc * 512: kc * 2048 + (tc + 1) * 512],
                                start=(kc == 0), stop=(kc == 7)))
                        tmm = b.mark("tensor")
                        b.wait("scalar", tmm)
                        thc = b.op("scalar", lambda e_, hc=hc, dc=dc, ph=ph: e_.activation(
                            h_fm[:, hc * 512:(hc + 1) * 512], ph[:], AF.Relu,
                            bias=b1f[:, dc * 8 + hc: dc * 8 + hc + 1]), inc=True)
                        hev.append(thc)
                    last_hev = hev[-1]
                    for t128 in range(4):
                        ti = tc * 4 + t128
                        psy = ps_y2[t128 % 2]
                        b.wait("tensor", hev[7])
                        if dc == 0 and tc == 0 and t128 == 0:
                            b.wait("tensor", tb2)
                        if tap_par[t128 % 2] is not None:
                            b.wait("tensor", tap_par[t128 % 2])
                        for hc in range(8):
                            for n2 in range(2):
                                stop = (dc != 0) and (hc == 7)
                                b.op("tensor", lambda e_, hc=hc, n2=n2, t128=t128, slot=slot, stop=stop, psy=psy: e_.matmul(
                                    psy[:, n2 * 512:(n2 + 1) * 512],
                                    h_fm[:, hc * 512 + t128 * 128: hc * 512 + (t128 + 1) * 128],
                                    w2s[slot][:, hc * 1024 + n2 * 512: hc * 1024 + (n2 + 1) * 512],
                                    start=(hc == 0), stop=stop))
                        if dc == 0:
                            for n2 in range(2):
                                b.op("tensor", lambda e_, n2=n2, psy=psy: e_.matmul(
                                    psy[:, n2 * 512:(n2 + 1) * 512], ones_sb[0:1, :],
                                    b2r[0:1, n2 * 512:(n2 + 1) * 512], start=False, stop=True))
                        last_mm = b.mark("tensor")
                        b.wait("vector", last_mm)
                        for n2 in range(2):
                            tap = b.op("vector", lambda e_, ti=ti, n2=n2, e=e, psy=psy: e_.scalar_tensor_tensor(
                                xr[:, ti * 1024 + n2 * 512: ti * 1024 + (n2 + 1) * 512],
                                psy[:, n2 * 512:(n2 + 1) * 512], coef[:, ti * 8 + e: ti * 8 + e + 1],
                                xr[:, ti * 1024 + n2 * 512: ti * 1024 + (n2 + 1) * 512],
                                ALU.mult, ALU.add), inc=True)
                        ap_toks.append(tap)
                        tap_par[t128 % 2] = tap
                slot_tok[slot] = b.tok("tensor")
        prev_apply = ap_toks[-1]
        b.wait("gpsimd", ap_toks[-1])
        for i in range(16):
            st_toks[i] = b.dma("gpsimd", x_out[base + i * 128: base + (i + 1) * 128, :],
                               xr[:, i * 1024:(i + 1) * 1024], sem="xst")


def build_l0_ffn_launch():
    nc = bass.Bass("TRN2", target_bir_lowering=False)
    x_in = nc.dram_tensor("x", [N_TOK, D], F32, kind="ExternalInput").ap()
    gw = nc.dram_tensor("gate_w", [8, 1024], F32, kind="ExternalInput").ap()
    w1 = nc.dram_tensor("w1", [8, 1024, 4096], F32, kind="ExternalInput").ap()
    b1 = nc.dram_tensor("b1", [8, 4096], F32, kind="ExternalInput").ap()
    w2 = nc.dram_tensor("w2", [8, 4096, 1024], F32, kind="ExternalInput").ap()
    b2 = nc.dram_tensor("b2", [8, 1024], F32, kind="ExternalInput").ap()
    ident = nc.dram_tensor("ident", [128, 128], BF16, kind="ExternalInput").ap()
    ones = nc.dram_tensor("ones", [1, 128], BF16, kind="ExternalInput").ap()
    x_out = nc.dram_tensor("x_out", [N_TOK, D], F32, kind="ExternalOutput").ap()

    b = B(nc)
    emit_l0_ffn(b, x_in, x_out, w1, w2, b1, b2, gw, ident, ones, name="l0f")
    b.wait("gpsimd", b.dtok("xst"))
    return b.build()


def run_l0_ffn(x_full, gw, w1, b1, w2, b2):
    nc = build_l0_ffn_launch()
    in_maps = []
    for c in range(N_CORES):
        in_maps.append({
            "x": np.ascontiguousarray(x_full[c]),
            "gate_w": np.asarray(gw), "w1": np.asarray(w1), "b1": np.asarray(b1),
            "w2": np.asarray(w2), "b2": np.asarray(b2),
            "ident": _IDENT, "ones": _ONES,
        })
    res = run_bass_kernel_spmd(nc, in_maps, core_ids=list(range(N_CORES)))
    return np.stack([res.results[c]["x_out"] for c in range(N_CORES)])



def emit_mha(b, x_in, x_out, w_q, w_k, w_v, idt_ap, ones_ap, onesc_ap, cmask_ap,
             causal, name, ntok=N_TOK, debug_stop=None):
    NT = ntok // 128
    NQC = ntok // 512
    fuse_dn = True
    VW = 1040
    HW = VW // 16
    k_fm = b.sb(name + "_kfm", [128, 8 * ntok], BF16)
    v_sb = b.sb(name + "_v", [128, NT * VW], BF16)
    q_fm = b.sb(name + "_qfm", [128, 8 * 512], BF16)
    w_sb = b.sb(name + "_w", [128, 16 * 1024], BF16)
    xt = [b.sb(name + f"_x{i}", [128, 1024], F32) for i in range(4)]
    xn = b.sb(name + "_xn", [128, 2 * 1024], BF16)
    xnT = b.sb(name + "_xnT", [128, 8 * 128], BF16)
    ktv = b.sb(name + "_ktv", [128, 1024], BF16)
    q2 = b.sb(name + "_q2", [128, 1024], BF16)
    qn2 = b.sb(name + "_qn2", [128, 16], BF16)
    qn2f = b.sb(name + "_qn2f", [128, 16], F32)
    kn2 = b.sb(name + "_kn2", [128, 16], BF16)
    qnF = b.sb(name + "_qnF", [16, 512], BF16)
    knF = b.sb(name + "_knF", [16, 128], BF16)
    kntmp = b.sb(name + "_kntmp", [16, 1], BF16)
    kmax = b.sb(name + "_kmax", [16, 1], BF16)
    qmx = b.sb(name + "_qmx", [16, 1], BF16)
    crow = b.sb(name + "_crow", [1, 16], BF16)
    negc = b.sb(name + "_negc", [128, 16], F32)
    pt_t = b.sb(name + "_pt", [128, 3 * 1024], BF16)
    dcol = b.sb(name + "_dcol", [128, 4], BF16)
    dcolf = b.sb(name + "_dcolf", [128, 4], F32)
    o_fm = ktv[0:64, 0:512]
    drow = ktv[0:1, 512:1024]
    o_tm = b.sb(name + "_otm", [128, 65], F32)
    idt = b.sb(name + "_idt", [128, 128], BF16)
    ones_sb = b.sb(name + "_ones", [1, 128], BF16)
    onesc = b.sb(name + "_onesc", [128, 1], BF16)
    cmask = b.sb(name + "_cmask", [128, 4 * 512], BF16)
    st = {"stats": b.sb(name + "_st", [128, 12], F32),
          "mv": b.sb(name + "_mv", [128, 2], F32),
          "r1": b.sb(name + "_r1", [128, 1], F32)}
    ps_p = [b.ps(name + "_psp0", [128, 512], F32), b.ps(name + "_psp1", [128, 512], F32)]
    ps_s = b.ps(name + "_pss", [128, 1024], F32)
    ps_o = b.ps(name + "_pso", [128, 512], F32)
    ps_d = b.ps(name + "_psd", [128, 512], F32)
    tr = TransposeHelper(b, name, idt)

    b.dma("sync", idt[:], idt_ap[:, :], sem="wres")
    b.dma("sync", ones_sb[:], ones_ap[:, :], sem="wres")
    b.dma("sync", onesc[:], onesc_ap[:, :], sem="wres")
    if causal:
        b.dma("sync", cmask[:], cmask_ap[:, :], sem="wres")
    for kc in range(8):
        b.dma("sync", w_sb[:, kc * 1024:(kc + 1) * 1024], w_k[kc * 128:(kc + 1) * 128, :], sem="wres")
    for kc in range(8):
        b.dma("gpsimd", w_sb[:, 8192 + kc * 1024: 8192 + (kc + 1) * 1024],
              w_v[kc * 128:(kc + 1) * 128, :], sem="wvres")
    w_tok = b.dtok("wres")
    wv_tok = b.dtok("wvres")
    if fuse_dn:
        vones = v_sb[:, :].rearrange("p (c d) -> p c d", d=HW)[:, :, 64:65]
        b.op("vector", lambda e: e.memset(vones, 1.0))
        b.drain("vector")

    prev_evacs = []
    prev_tkn = None
    tknm = None
    ln_toks = [None] * (NT + 8)
    x_toks = [None] * (NT + 8)
    xn_guard = [None, None]

    def p1_load(i):
        if i < NT:
            if i >= 4:
                b.wait("sync", ln_toks[i - 4])
            x_toks[i] = b.dma("sync", xt[i % 4][:], x_in[i * 128:(i + 1) * 128, :], sem="xld")
    for i in range(3):
        p1_load(i)
    for i in range(NT):
        p1_load(i + 3)
        b.wait("vector", x_toks[i])
        if xn_guard[i % 2] is not None:
            b.wait("vector", xn_guard[i % 2])
        xns = xn[:, (i % 2) * 1024:(i % 2 + 1) * 1024]
        tln = emit_ln(b, xt[i % 4], xns, st)
        ln_toks[i] = tln
        for kc in range(8):
            pre = [tln] + ([w_tok, wv_tok] if (i == 0 and kc == 0) else [])
            tt, vt = tr.emit(xns[:, kc * 128:(kc + 1) * 128], xnT[:, kc * 128:(kc + 1) * 128],
                             pre_tensor_waits=pre)
        xn_guard[i % 2] = tt
        b.wait("tensor", vt)
        tev = tsq = None
        k_evac = []
        for half in range(2):
            pp = ps_p[half]
            for t_ in prev_evacs:
                b.wait("tensor", t_)
            for kc in range(8):
                b.op("tensor", lambda e, kc=kc, half=half, pp=pp: e.matmul(
                    pp[:], xnT[:, kc * 128:(kc + 1) * 128],
                    w_sb[:, kc * 1024 + half * 512: kc * 1024 + (half + 1) * 512],
                    start=(kc == 0), stop=(kc == 7)))
            tpk = b.mark("tensor")
            b.wait("vector", tpk)
            tev = b.op("vector", lambda e, half=half, pp=pp: e.tensor_copy(
                ktv[:, half * 512:(half + 1) * 512], pp[:]), inc=True)
            b.wait("scalar", tpk)
            b.wait("scalar", tev)
            if prev_tkn is not None:
                b.wait("scalar", prev_tkn)
            tsq = b.op("scalar", lambda e, half=half, pp=pp: e.activation(
                q2[:, half * 512:(half + 1) * 512], pp[:], AF.Square), inc=True)
            k_evac.append((tev, tsq))
        prev_evacs = [k_evac[0][0], k_evac[0][1], k_evac[1][0], k_evac[1][1]]
        for half in range(2):
            pp = ps_p[half]
            for t_ in prev_evacs:
                b.wait("tensor", t_)
            prev_evacs = []
            for kc in range(8):
                b.op("tensor", lambda e, kc=kc, half=half, pp=pp: e.matmul(
                    pp[:], xnT[:, kc * 128:(kc + 1) * 128],
                    w_sb[:, 8192 + kc * 1024 + half * 512: 8192 + kc * 1024 + (half + 1) * 512],
                    start=(kc == 0), stop=(kc == 7)))
            tpv = b.mark("tensor")
            b.wait("vector", tpv)
            if fuse_dn:
                dst3 = v_sb[:, i * VW + half * 520: i * VW + half * 520 + 520].rearrange(
                    "p (h d) -> p h d", d=65)[:, :, 0:64]
                tvv = b.op("vector", lambda e, dst3=dst3, pp=pp: e.tensor_copy(
                    dst3, pp[:].rearrange("p (h d) -> p h d", d=64)), inc=True)
            else:
                tvv = b.op("vector", lambda e, half=half, i=i, pp=pp: e.tensor_copy(
                    v_sb[:, i * 1024 + half * 512: i * 1024 + (half + 1) * 512], pp[:]), inc=True)
            prev_evacs.append(tvv)
        b.wait("vector", tsq)
        b.drain("vector")
        b.op("vector", lambda e: e.tensor_reduce(
            qn2f[:], q2[:, :].rearrange("p (h d) -> p h d", d=64), AX.X, ALU.add))
        b.drain("vector")
        tkn = b.op("vector", lambda e: e.tensor_copy(kn2[:], qn2f[:]), inc=True)
        prev_tkn = tkn
        for kc in range(8):
            tt, vt = tr.emit(ktv[:, kc * 128:(kc + 1) * 128],
                             k_fm[:, kc * ntok + i * 128: kc * ntok + (i + 1) * 128],
                             pre_tensor_waits=[k_evac[1][0]] if kc == 0 else ())
        tt_k, vt_k = tr.emit(kn2[:, 0:16], knF[:, 0:128],
                             pre_tensor_waits=[tkn], src_p=128, src_f=16)
        b.wait("vector", tt_k)
        b.wait("vector", vt_k)
        b.drain("vector")
        b.op("vector", lambda e: e.tensor_reduce(kntmp[0:16, 0:1], knF[0:16, 0:128], AX.X, ALU.max))
        b.drain("vector")
        if i == 0:
            tknm = b.op("vector", lambda e: e.tensor_copy(kmax[0:16, 0:1], kntmp[0:16, 0:1]), inc=True)
        else:
            tknm = b.op("vector", lambda e: e.tensor_tensor(
                kmax[0:16, 0:1], kmax[0:16, 0:1], kntmp[0:16, 0:1], ALU.max), inc=True)
        b.drain("vector")
    if debug_stop in ("p1a", "p1a1"):
        return

    b.wait("sync", b.tok("tensor"))
    for kc in range(8):
        b.dma("sync", w_sb[:, kc * 1024:(kc + 1) * 1024], w_q[kc * 128:(kc + 1) * 128, :], sem="wres")
    wq_tok = b.dtok("wres")
    prev_tln = ln_toks[NT - 1]

    if debug_stop in ("phase1", "p1a1x"):
        return
    prev_store = [None] * 4
    prev_qmm = None
    prev_tad = None
    prev_tqn = None
    for qc in range(NQC):
        x_toks = []
        for t in range(4):
            if prev_store[t] is not None:
                b.wait("sync", prev_store[3])
            else:
                b.wait("sync", ln_toks[NT - 4 + t])
            x_toks.append(b.dma("sync", xt[t][:],
                                x_in[(qc * 4 + t) * 128:(qc * 4 + t + 1) * 128, :], sem="xld"))
        vtqn = None
        for t in range(4):
            b.wait("vector", x_toks[3])
            tln = emit_ln(b, xt[t], xn[:, 0:1024], st)
            for kc in range(8):
                pre = [tln]
                if kc == 0 and prev_qmm is not None and t == 0:
                    pre.append(prev_qmm)
                if qc == 0 and t == 0 and kc == 0:
                    pre.append(wq_tok)
                tt, vt = tr.emit(xn[:, kc * 128:(kc + 1) * 128], xnT[:, kc * 128:(kc + 1) * 128],
                                 pre_tensor_waits=pre)
            b.wait("vector", tt)
            b.wait("tensor", vt)
            tqe = tq2 = None
            for half in range(2):
                for t_ in prev_evacs:
                    b.wait("tensor", t_)
                prev_evacs = []
                pp = ps_p[half]
                for kc in range(8):
                    b.op("tensor", lambda e, kc=kc, half=half, pp=pp: e.matmul(
                        pp[:], xnT[:, kc * 128:(kc + 1) * 128],
                        w_sb[:, kc * 1024 + half * 512: kc * 1024 + (half + 1) * 512],
                        start=(kc == 0), stop=(kc == 7)))
                tpq = b.mark("tensor")
                b.wait("vector", tpq)
                tqe = b.op("vector", lambda e, half=half, pp=pp: e.tensor_scalar(
                    ktv[:, half * 512:(half + 1) * 512], pp[:], 0.125, None, ALU.mult), inc=True)
                b.wait("scalar", tpq)
                b.wait("scalar", tqe)
                if prev_tqn is not None:
                    b.wait("scalar", prev_tqn)
                tq2 = b.op("scalar", lambda e, half=half, pp=pp: e.activation(
                    q2[:, half * 512:(half + 1) * 512], pp[:], AF.Square, scale=0.125), inc=True)
                prev_evacs = [tqe, tq2]
            b.wait("vector", tq2)
            b.drain("vector")
            b.op("vector", lambda e: e.tensor_reduce(
                qn2f[:], q2[:, :].rearrange("p (h d) -> p h d", d=64), AX.X, ALU.add))
            b.drain("vector")
            tqn = b.op("vector", lambda e: e.tensor_copy(qn2[:], qn2f[:]), inc=True)
            prev_tqn = tqn
            for kc in range(8):
                tt, vt = tr.emit(ktv[:, kc * 128:(kc + 1) * 128],
                                 q_fm[:, kc * 512 + t * 128: kc * 512 + (t + 1) * 128],
                                 pre_tensor_waits=[tqe] if kc == 0 else ())
            ttqn, vtqn = tr.emit(qn2[:, 0:16], qnF[:, t * 128:(t + 1) * 128],
                                 pre_tensor_waits=[tqn], src_p=128, src_f=16)
            b.wait("vector", ttqn)
        b.wait("vector", vtqn)
        if qc == 0:
            b.wait("vector", tknm)
        b.drain("vector")
        b.op("vector", lambda e: e.tensor_reduce(qmx[0:16, 0:1], qnF[0:16, 0:512], AX.X, ALU.max))
        b.drain("vector")
        tcb = b.op("vector", lambda e: e.tensor_tensor(qmx[0:16, 0:1], qmx[0:16, 0:1],
                                                       kmax[0:16, 0:1], ALU.mult), inc=True)
        b.wait("scalar", tcb)
        tcs = b.op("scalar", lambda e: e.activation(qmx[0:16, 0:1], qmx[0:16, 0:1], AF.Sqrt), inc=True)
        ttc, vtc = tr.emit(qmx[0:16, 0:1], crow[0:1, 0:16], pre_tensor_waits=[tcs],
                           src_p=16, src_f=1)
        b.drain("vector")
        tneg = b.op("vector", lambda e: e.tensor_scalar(crow[0:1, 0:16], crow[0:1, 0:16],
                                                        -1.05, None, ALU.mult), inc=True)
        b.wait("tensor", tneg)
        if prev_tad is not None:
            b.wait("tensor", prev_tad)
        tbc = b.op("tensor", lambda e: e.matmul(ps_d[:, 0:16], ones_sb[0:1, :], crow[0:1, 0:16],
                                                start=True, stop=True), inc=True)
        b.wait("vector", tbc)
        tnegc = b.op("vector", lambda e: e.tensor_copy(negc[:], ps_d[:, 0:16]), inc=True)
        b.wait("tensor", tnegc)
        b.wait("scalar", tnegc)

        if debug_stop == "bound":
            continue
        nheads = 1 if debug_stop == "head1" else 16

        def make_epilogue(h, tdn):
            def epi():
                nonlocal prev_tad
                b.wait("vector", tdn)
                if fuse_dn:
                    tor = b.op("vector", lambda e: e.tensor_copy(
                        ktv[0:65, 0:512], ps_o[0:65, :]), inc=True)
                    tad = None
                    for t in range(4):
                        tto, vto = tr.emit(ktv[0:65, t * 128:(t + 1) * 128], o_tm[:, 0:65],
                                           pre_tensor_waits=[tor] if t == 0 else (),
                                           src_p=65, src_f=128)
                        b.wait("vector", vto)
                        b.drain("vector")
                        trc = b.op("vector", lambda e, t=t: e.reciprocal(
                            dcolf[:, t:t + 1], o_tm[:, 64:65]), inc=True)
                        b.drain("vector")
                        tad = b.op("vector", lambda e, t=t, h=h: e.scalar_tensor_tensor(
                            xt[t][:, h * 64:(h + 1) * 64], o_tm[:, 0:64], dcolf[:, t:t + 1],
                            xt[t][:, h * 64:(h + 1) * 64], ALU.mult, ALU.add), inc=True)
                        b.drain("vector")
                    prev_tad = tad
                    return tad, tor
                tdr = b.op("vector", lambda e: e.tensor_copy(drow[:], ps_d[0:1, 0:512]), inc=True)
                tor = b.op("vector", lambda e: e.tensor_copy(o_fm[:], ps_o[0:64, :]), inc=True)
                vtd = None
                for t in range(4):
                    ttd, vtd = tr.emit(drow[0:1, t * 128:(t + 1) * 128], dcol[:, t:t + 1],
                                       pre_tensor_waits=[tdr] if t == 0 else (), src_p=1, src_f=128)
                b.wait("vector", vtd)
                b.drain("vector")
                trc = b.op("vector", lambda e: e.reciprocal(dcolf[:], dcol[:]), inc=True)
                b.drain("vector")
                tad = None
                for t in range(4):
                    tto, vto = tr.emit(o_fm[0:64, t * 128:(t + 1) * 128], o_tm[:, 0:64],
                                       pre_tensor_waits=[tor] if t == 0 else (), src_p=64, src_f=128)
                    b.wait("vector", vto)
                    tad = b.op("vector", lambda e, t=t, h=h: e.scalar_tensor_tensor(
                        xt[t][:, h * 64:(h + 1) * 64], o_tm[:, 0:64], dcolf[:, t:t + 1],
                        xt[t][:, h * 64:(h + 1) * 64], ALU.mult, ALU.add), inc=True)
                    b.drain("vector")
                prev_tad = tad
                return tdr, tor
            return epi

        pending_epi = None
        epi_guard = None
        for h in range(nheads):
            hc, par = h // 2, h % 2
            koff = par * 64
            ndiag = min((qc + 1) * 4, NT) if causal else NT
            nrounds = NT // 2
            texp_r = [[None, None] for _ in range(nrounds)]
            cons = [[None, None] for _ in range(3)]
            for r in range(nrounds + 2):
                if r < nrounds:
                    slot = r % 3
                    for j in range(2):
                        kt = r * 2 + j
                        if r > 0 and texp_r[r - 1][j] is not None:
                            b.wait("tensor", texp_r[r - 1][j])
                        b.op("tensor", lambda e, hc=hc, koff=koff, kt=kt, j=j: e.matmul(
                            ps_s[:, j * 512:(j + 1) * 512],
                            k_fm[koff:koff + 64, hc * ntok + kt * 128: hc * ntok + (kt + 1) * 128],
                            q_fm[koff:koff + 64, hc * 512:(hc + 1) * 512],
                            start=True, stop=True))
                        tsm = b.mark("tensor")
                        b.wait("scalar", tsm)
                        if cons[slot][j] is not None:
                            b.wait("scalar", cons[slot][j])
                        texp_r[r][j] = b.op("scalar", lambda e, slot=slot, j=j, h=h: e.activation(
                            pt_t[:, slot * 1024 + j * 512: slot * 1024 + j * 512 + 512],
                            ps_s[:, j * 512:(j + 1) * 512], AF.Exp, bias=negc[:, h:h + 1]),
                            inc=True)
                if r == 2 and pending_epi is not None:
                    epi_guard = pending_epi()
                    pending_epi = None
                if r >= 2:
                    rc = r - 2
                    slot_c = rc % 3
                    for j in range(2):
                        kt = rc * 2 + j
                        po = slot_c * 1024 + j * 512
                        b.wait("tensor", texp_r[rc][j])
                        diag0 = qc * 4 if causal else NT
                        if kt == 0 and epi_guard is not None:
                            b.wait("tensor", epi_guard[1])
                        if kt < diag0:
                            tav = b.op("tensor", lambda e, kt=kt, h=h, po=po: e.matmul(
                                ps_o[0:65, :], v_sb[:, kt * VW + h * 65: kt * VW + h * 65 + 65],
                                pt_t[:, po:po + 512],
                                start=(kt == 0),
                                stop=(not causal and kt == NT - 1)), inc=True)
                            cons[slot_c][j] = tav
                            continue
                        tdn_mm = b.op("tensor", lambda e, kt=kt, po=po: e.matmul(
                            ps_o[64:65, :], onesc[:, 0:1], pt_t[:, po:po + 512],
                            start=(kt == 0), stop=(kt == NT - 1)), inc=True)
                        last_cons = tdn_mm
                        dg = kt - qc * 4
                        if dg < 4:
                            b.wait("vector", tdn_mm)
                            tpt = b.op("vector", lambda e, dg=dg, po=po: e.tensor_tensor(
                                pt_t[:, po:po + 512], pt_t[:, po:po + 512],
                                cmask[:, dg * 512:(dg + 1) * 512], ALU.mult), inc=True)
                            b.wait("tensor", tpt)
                            tav = b.op("tensor", lambda e, kt=kt, h=h, po=po, ndiag=ndiag: e.matmul(
                                ps_o[0:64, :], v_sb[:, kt * VW + h * 65: kt * VW + h * 65 + 64],
                                pt_t[:, po:po + 512],
                                start=(kt == 0), stop=(kt == ndiag - 1)), inc=True)
                            last_cons = tav
                        cons[slot_c][j] = last_cons
            tdn = b.tok("tensor")
            pending_epi = make_epilogue(h, tdn)
        epi_guard = pending_epi()
        pending_epi = None
        b.wait("gpsimd", prev_tad)
        prev_store = []
        for t in range(4):
            prev_store.append(b.dma("gpsimd", x_out[(qc * 4 + t) * 128:(qc * 4 + t + 1) * 128, :],
                                    xt[t][:], sem="xst"))
        prev_qmm = b.tok("tensor")


def build_mha_launch(causal, ntok=N_TOK, debug_stop=None):
    nc = bass.Bass("TRN2", target_bir_lowering=False)
    x_in = nc.dram_tensor("x", [ntok, D], F32, kind="ExternalInput").ap()
    wq = nc.dram_tensor("wq", [1024, 1024], F32, kind="ExternalInput").ap()
    wk = nc.dram_tensor("wk", [1024, 1024], F32, kind="ExternalInput").ap()
    wv = nc.dram_tensor("wv", [1024, 1024], F32, kind="ExternalInput").ap()
    ident = nc.dram_tensor("ident", [128, 128], BF16, kind="ExternalInput").ap()
    ones = nc.dram_tensor("ones", [1, 128], BF16, kind="ExternalInput").ap()
    onesc = nc.dram_tensor("onesc", [128, 1], BF16, kind="ExternalInput").ap()
    cmask = nc.dram_tensor("cmask", [128, 4 * 512], BF16, kind="ExternalInput").ap()
    x_out = nc.dram_tensor("x_out", [ntok, D], F32, kind="ExternalOutput").ap()
    wq_bf = nc.dram_tensor("wq_bf", [1024, 1024], BF16).ap()
    wk_bf = nc.dram_tensor("wk_bf", [1024, 1024], BF16).ap()
    wv_bf = nc.dram_tensor("wv_bf", [1024, 1024], BF16).ap()

    b = B(nc)
    emit_cast_pass(b, wq, wq_bf, 1024)
    emit_cast_pass(b, wk, wk_bf, 1024)
    emit_cast_pass(b, wv, wv_bf, 1024)
    b.wait("sync", b.dtok("ppcast"))
    b.wait("gpsimd", b.dtok("ppcast"))
    if debug_stop != "cast":
        emit_mha(b, x_in, x_out, wq_bf, wk_bf, wv_bf, ident, ones, onesc, cmask,
                 causal=causal, name="mha", ntok=ntok, debug_stop=debug_stop)
    if b.dc.get("xst"):
        b.wait("gpsimd", b.dtok("xst"))
    return b.build()


_ONESC = np.ones((128, 1), dtype=bfdt)


def _causal_mask_const():
    m = np.zeros((128, 4 * 512), dtype=bfdt)
    for dg in range(4):
        for p in range(128):
            qq = np.arange(512)
            m[p, dg * 512:(dg + 1) * 512] = (dg * 128 + p <= qq).astype(bfdt)
    return m


_CMASK = _causal_mask_const()


def run_mha(x_full, wq, wk, wv, causal, ntok=N_TOK, debug_stop=None):
    nc = build_mha_launch(causal, ntok, debug_stop)
    in_maps = []
    for c in range(N_CORES):
        in_maps.append({
            "x": np.ascontiguousarray(x_full[c]),
            "wq": np.asarray(wq), "wk": np.asarray(wk), "wv": np.asarray(wv),
            "ident": _IDENT, "ones": _ONES, "onesc": _ONESC, "cmask": _CMASK,
        })
    res = run_bass_kernel_spmd(nc, in_maps, core_ids=list(range(N_CORES)))
    return np.stack([res.results[c]["x_out"] for c in range(N_CORES)])


def build_l3_attn_launch(ntok=N_TOK):
    nc = bass.Bass("TRN2", target_bir_lowering=False)
    x_in = nc.dram_tensor("x", [ntok, D], F32, kind="ExternalInput").ap()
    wqkv = nc.dram_tensor("w_qkv", [3072, 1024], F32, kind="ExternalInput").ap()
    ident = nc.dram_tensor("ident", [128, 128], BF16, kind="ExternalInput").ap()
    ones = nc.dram_tensor("ones", [1, 128], BF16, kind="ExternalInput").ap()
    onesc = nc.dram_tensor("onesc", [128, 1], BF16, kind="ExternalInput").ap()
    cmask = nc.dram_tensor("cmask", [128, 4 * 512], BF16, kind="ExternalInput").ap()
    x_out = nc.dram_tensor("x_out", [ntok, D], F32, kind="ExternalOutput").ap()
    qkv_bf = nc.dram_tensor("qkv_bf", [3072, 1024], BF16).ap()
    qkvT = nc.dram_tensor("qkvT", [1024, 3072], BF16).ap()

    b = B(nc)
    with b.scope():
        emit_cast_pass(b, wqkv, qkv_bf, 3072)
        b.wait("sync", b.dtok("ppcast"))
        tb = b.sb("pp_tb4", [128, 3072], BF16)
        emit_transpose_pass(b, qkv_bf, qkvT, 3072, 1024, tb)
    b.barrier()
    emit_mha(b, x_in, x_out, qkvT[:, 0:1024], qkvT[:, 1024:2048], qkvT[:, 2048:3072],
             ident, ones, onesc, cmask, causal=False, name="mha", ntok=ntok)
    b.wait("gpsimd", b.dtok("xst"))
    return b.build()


def run_l3_attn(x_full, w_qkv, ntok=N_TOK):
    nc = build_l3_attn_launch(ntok)
    in_maps = []
    for c in range(N_CORES):
        in_maps.append({
            "x": np.ascontiguousarray(x_full[c]), "w_qkv": np.asarray(w_qkv),
            "ident": _IDENT, "ones": _ONES, "onesc": _ONESC, "cmask": _CMASK,
        })
    res = run_bass_kernel_spmd(nc, in_maps, core_ids=list(range(N_CORES)))
    return np.stack([res.results[c]["x_out"] for c in range(N_CORES)])



GSZ = 256


def _l0_bucket():
    import math
    pos = np.arange(GSZ)
    n = -(pos[None, :] - pos[:, None])
    nb = 16
    ret = (n < 0).astype(np.int64) * nb
    n = np.abs(n)
    max_exact = nb // 2
    is_small = n < max_exact
    val_large = max_exact + (np.log(np.maximum(n, 1).astype(np.float64) / max_exact)
                             / math.log(128 / max_exact) * (nb - max_exact)).astype(np.int64)
    val_large = np.minimum(val_large, nb - 1)
    return ret + np.where(is_small, n, val_large)


def _l0_relmask_const():
    bucket = _l0_bucket()
    m = np.zeros((128, 2 * 32 * 256), dtype=bfdt)
    for jc in range(2):
        for k in range(32):
            blk = (bucket[:, jc * 128:(jc + 1) * 128].T == k).astype(bfdt)
            m[:, (jc * 32 + k) * 256:(jc * 32 + k + 1) * 256] = blk
    return m


_RELMASK = _l0_relmask_const()


def emit_l0_attn(b, x_in, x_out, norm_g, norm_b, w_h, b_h, w_qk, b_qk, qk_gamma, qk_beta,
                 rel_emb, w_o, b_o, hv_scr, vs_scr, idt_ap, ones_ap, onesc_ap, relmask_ap,
                 name="l0a"):
    NT = 32
    wh_sb = b.sb(name + "_wh", [128, 8 * 4096], BF16)
    wqk_sb = b.sb(name + "_wqk", [128, 8 * 128], BF16)
    wo_sb = b.sb(name + "_wo", [128, 16 * 1024], BF16)
    whf = b.sb(name + "_whf", [128, 2048], F32)
    qk_all = b.sb(name + "_qka", [128, 32 * 128], BF16)
    qk_fma = b.sb(name + "_qkf", [128, 32 * 128], BF16)
    xt = b.sb(name + "_x", [128, 1024], F32)
    z1 = b.sb(name + "_z1", [128, 1024], F32)
    zn = b.sb(name + "_zn", [128, 1024], BF16)
    znT = b.sb(name + "_znT", [128, 8 * 128], BF16)
    hv_t = b.sb(name + "_hv", [128, 4096], BF16)
    gcol = b.sb(name + "_gcol", [128, 8], F32)
    bcol = b.sb(name + "_bcol", [128, 8], BF16)
    g8 = b.sb(name + "_g8", [8, 128], BF16)
    bias_h = b.sb(name + "_bh", [1, 4096], BF16)
    bias_qk = b.sb(name + "_bqk", [1, 128], BF16)
    bhrow = b.sb(name + "_bhrow", [1, 4096], BF16)
    bqkrow = b.sb(name + "_bqkrow", [1, 128], BF16)
    bo_row = b.sb(name + "_borow", [1, 1024], BF16)
    gam = b.sb(name + "_gam", [128, 4], F32)
    bet = b.sb(name + "_bet", [128, 4], F32)
    betb = b.sb(name + "_betb", [128, 4], BF16)
    gb4 = b.sb(name + "_gb4", [4, 128], BF16)
    betr = b.sb(name + "_betr", [1, 128], BF16)
    linkv = b.sb(name + "_lkv", [128, 2048], BF16)
    linkv2 = b.sb(name + "_lkv2", [128, 2048], BF16)
    vsrow = b.sb(name + "_vs", [1, 2048], BF16)
    vs16 = b.sb(name + "_vs16", [128, 16], BF16)
    vsf = b.sb(name + "_vsf", [16, 128], BF16)
    rowt = b.sb(name + "_rowt", [1, 2048], BF16)
    relr = b.sb(name + "_relr", [1, 32], BF16)
    relbc = b.sb(name + "_relbc", [128, 32], F32)
    btT = b.sb(name + "_btT", [128, 2 * 256], BF16)
    qq = b.sb(name + "_qq", [128, 256], BF16)
    kk = b.sb(name + "_kk", [128, 256], BF16)
    attnT = b.sb(name + "_attnT", [128, 2 * 256], BF16)
    vg_t = [b.sb(name + f"_vg{i}", [128, 2048], BF16) for i in range(2)]
    gt_t = [b.sb(name + f"_gt{i}", [128, 2048], BF16) for i in range(2)]
    ho = b.sb(name + "_ho", [128, 2048], BF16)
    hoT = b.sb(name + "_hoT", [128, 16 * 128], BF16)
    rm = b.sb(name + "_rm", [128, 256], BF16)
    acc = b.sb(name + "_acc", [128, 256], F32)
    at1 = acc
    idt = b.sb(name + "_idt", [128, 128], BF16)
    ones_sb = b.sb(name + "_ones", [1, 128], BF16)
    onesc = b.sb(name + "_onesc", [128, 1], BF16)
    st = {"stats": b.sb(name + "_st", [128, 12], F32),
          "mv": b.sb(name + "_mv", [128, 2], F32),
          "r1": b.sb(name + "_r1", [128, 1], F32)}
    tr = TransposeHelper(b, name, idt)
    scopeA = b.scope()
    scopeA.__enter__()
    ps_a = b.ps(name + "_psa", [128, 512], F32)
    ps_kv = b.ps(name + "_pskv", [128, 2048], F32)
    ps_vs = b.ps(name + "_psvs", [128, 512], F32)

    b.dma("sync", idt[:], idt_ap[:, :], sem="wres")
    b.dma("sync", ones_sb[:], ones_ap[:, :], sem="wres")
    b.dma("sync", onesc[:], onesc_ap[:, :], sem="wres")
    w_tok = b.dtok("wres")
    b.dma("gpsimd", g8[:, :], norm_g.rearrange("(c p) -> c p", p=128), sem="cst")
    b.dma("gpsimd", relr[:, :], rel_emb.flatten().unsqueeze(0), sem="cst")
    b.dma("gpsimd", gb4[:, :], qk_gamma[:, :], sem="cst")
    c_tok = b.dtok("cst")

    ttg, vtg = tr.emit(g8[0:8, :], gcol[:, 0:8], pre_tensor_waits=[w_tok, c_tok], src_p=8)
    b.wait("gpsimd", vtg)
    b.dma("gpsimd", g8[:, :], norm_b.rearrange("(c p) -> c p", p=128), sem="cst")
    c2 = b.dtok("cst")
    ttb_, vtb_ = tr.emit(g8[0:8, :], bcol[:, 0:8], pre_tensor_waits=[c2], src_p=8)
    ttg2, vtg2 = tr.emit(gb4[0:4, :], gam[:, 0:4], pre_tensor_waits=[c_tok], src_p=4)
    b.wait("gpsimd", vtg2)
    b.dma("gpsimd", gb4[:, :], qk_beta[:, :], sem="cst")
    c3 = b.dtok("cst")
    ttb2, vtb2 = tr.emit(gb4[0:4, :], bet[:, 0:4], pre_tensor_waits=[c3], src_p=4)
    b.wait("vector", vtb2)
    b.drain("vector")
    for buf in (gam, bet):
        for (c, s) in ((0, 1.0 / 16), (2, 1.0 / 16), (3, 1.0 / 4096)):
            b.op("vector", lambda e, buf=buf, c=c, s=s: e.tensor_scalar(
                buf[:, c:c + 1], buf[:, c:c + 1], s, None, ALU.mult))
            b.drain("vector")
    b.op("vector", lambda e: e.tensor_copy(betb[:, :], bet[:, :]))
    b.drain("vector")
    tcst = b.mark("vector")
    tbr, vbr = tr.emit(betb[:, 3:4], betr[0:1, :], pre_tensor_waits=[tcst], src_p=128, src_f=1)

    prev_cast = None
    for kc in range(8):
        for hh in range(2):
            if prev_cast is not None:
                b.wait("sync", prev_cast)
            t_w = b.dma("sync", whf[:, :], w_h[kc * 128:(kc + 1) * 128, hh * 2048:(hh + 1) * 2048],
                        sem="whld")
            b.wait("vector", t_w)
            prev_cast = b.op("vector", lambda e, kc=kc, hh=hh: e.tensor_copy(
                wh_sb[:, kc * 4096 + hh * 2048: kc * 4096 + (hh + 1) * 2048], whf[:, :]), inc=True)
            b.drain("vector")
    b.wait("sync", prev_cast)
    t_w = b.dma("sync", whf[:, 0:1024].rearrange("p (c q) -> p c q", q=128),
                w_qk.rearrange("(c p) q -> p c q", p=128), sem="whld")
    b.wait("vector", t_w)
    tqkw = b.op("vector", lambda e: e.tensor_copy(wqk_sb[:, :], whf[:, 0:1024]), inc=True)
    b.drain("vector")
    for kc in range(16):
        b.dma("gpsimd", wo_sb[:, kc * 1024:(kc + 1) * 1024], w_o[kc * 128:(kc + 1) * 128, :], sem="wold")
    b.dma("gpsimd", bhrow[:, :], b_h.unsqueeze(0), sem="wold")
    b.dma("gpsimd", bqkrow[:, :], b_qk.unsqueeze(0), sem="wold")
    b.dma("gpsimd", bo_row[:, :], b_o.unsqueeze(0), sem="wold")
    two = b.dtok("wold")

    b.wait("tensor", tqkw)
    b.wait("tensor", vbr)
    b.wait("tensor", two)
    for oc in range(8):
        for kc in range(8):
            b.op("tensor", lambda e, oc=oc, kc=kc: e.matmul(
                ps_a[0:1, 0:512], bcol[:, kc:kc + 1],
                wh_sb[:, kc * 4096 + oc * 512: kc * 4096 + (oc + 1) * 512],
                start=(kc == 0), stop=(kc == 7)))
        tb_ = b.mark("tensor")
        b.wait("vector", tb_)
        tvb = b.op("vector", lambda e, oc=oc: e.tensor_copy(
            bias_h[0:1, oc * 512:(oc + 1) * 512], ps_a[0:1, 0:512]), inc=True)
        b.wait("tensor", tvb)
    b.drain("vector")
    tbh = b.op("vector", lambda e: e.tensor_tensor(bias_h[0:1, :], bias_h[0:1, :],
                                                   bhrow[0:1, :], ALU.add), inc=True)
    for kc in range(8):
        b.op("tensor", lambda e, kc=kc: e.matmul(
            ps_a[0:1, 0:128], bcol[:, kc:kc + 1], wqk_sb[:, kc * 128:(kc + 1) * 128],
            start=(kc == 0), stop=(kc == 7)))
    tb_ = b.mark("tensor")
    b.wait("vector", tb_)
    b.op("vector", lambda e: e.tensor_copy(bias_qk[0:1, :], ps_a[0:1, 0:128]))
    b.drain("vector")
    tbqk = b.op("vector", lambda e: e.tensor_tensor(bias_qk[0:1, :], bias_qk[0:1, :],
                                                    bqkrow[0:1, :], ALU.add), inc=True)
    b.wait("tensor", tbqk)
    for kc in range(8):
        b.op("vector", lambda e, kc=kc: e.tensor_scalar(
            wh_sb[:, kc * 4096:(kc + 1) * 4096], wh_sb[:, kc * 4096:(kc + 1) * 4096],
            gcol[:, kc:kc + 1], None, ALU.mult))
        b.drain("vector")
        b.op("vector", lambda e, kc=kc: e.tensor_scalar(
            wqk_sb[:, kc * 128:(kc + 1) * 128], wqk_sb[:, kc * 128:(kc + 1) * 128],
            gcol[:, kc:kc + 1], None, ALU.mult))
        b.drain("vector")
    tsc = b.mark("vector")
    b.wait("tensor", tsc)

    trel = b.op("tensor", lambda e: e.matmul(ps_a[:, 0:32], ones_sb[0:1, :], relr[0:1, :],
                                             start=True, stop=True), inc=True)
    b.wait("vector", trel)
    b.op("vector", lambda e: e.tensor_scalar(relbc[:], ps_a[:, 0:32], float(np.sqrt(128.0)),
                                             None, ALU.mult))
    b.drain("vector")
    prev_rm = None
    for jc in range(2):
        for k in range(32):
            if prev_rm is not None:
                b.wait("sync", prev_rm)
            t_m = b.dma("sync", rm[:, :], relmask_ap[:, (jc * 32 + k) * 256:(jc * 32 + k + 1) * 256],
                        sem="rmld")
            b.wait("vector", t_m)
            if k == 0:
                prev_rm = b.op("vector", lambda e: e.tensor_scalar(
                    acc[:], rm[:], relbc[:, 0:1], None, ALU.mult), inc=True)
            else:
                prev_rm = b.op("vector", lambda e, k=k: e.scalar_tensor_tensor(
                    acc[:], rm[:], relbc[:, k:k + 1], acc[:], ALU.mult, ALU.add), inc=True)
            b.drain("vector")
        b.op("vector", lambda e, jc=jc: e.tensor_copy(btT[:, jc * 256:(jc + 1) * 256], acc[:]))
        b.drain("vector")
    tbt = b.mark("vector")
    b.wait("tensor", tbt)

    prev_tln = None
    prev_pmm = None
    prev_ev = None
    first_lin = True
    qk_last = None
    for i in range(NT):
        if prev_tln is not None:
            b.wait("sync", prev_tln)
        t_x = b.dma("sync", xt[:], x_in[i * 128:(i + 1) * 128, :], sem="xld")
        b.wait("vector", t_x)
        t1 = emit_ln(b, xt, z1, st)
        b.wait("vector", t1)
        tln = emit_ln(b, z1, zn, st)
        prev_tln = tln
        for kc in range(8):
            pre = [tln]
            if kc == 0 and prev_pmm is not None:
                pre.append(prev_pmm)
            tt, vt = tr.emit(zn[:, kc * 128:(kc + 1) * 128], znT[:, kc * 128:(kc + 1) * 128],
                             pre_tensor_waits=pre)
        b.wait("vector", tt)
        b.wait("tensor", vt)
        for oc in range(8):
            if prev_ev is not None:
                b.wait("tensor", prev_ev)
            for kc in range(8):
                b.op("tensor", lambda e, oc=oc, kc=kc: e.matmul(
                    ps_a[:, 0:512], znT[:, kc * 128:(kc + 1) * 128],
                    wh_sb[:, kc * 4096 + oc * 512: kc * 4096 + (oc + 1) * 512],
                    start=(kc == 0), stop=False))
            tmm = b.op("tensor", lambda e, oc=oc: e.matmul(
                ps_a[:, 0:512], ones_sb[0:1, :], bias_h[0:1, oc * 512:(oc + 1) * 512],
                start=False, stop=True), inc=True)
            b.wait("scalar", tmm)
            if i > 0 and oc == 0:
                b.wait("scalar", prev_hvst)
            prev_ev = b.op("scalar", lambda e, oc=oc: e.activation(
                hv_t[:, oc * 512:(oc + 1) * 512], ps_a[:, 0:512], AF.Silu), inc=True)
        b.wait("tensor", prev_ev)
        for kc in range(8):
            b.op("tensor", lambda e, kc=kc: e.matmul(
                ps_a[:, 0:128], znT[:, kc * 128:(kc + 1) * 128], wqk_sb[:, kc * 128:(kc + 1) * 128],
                start=(kc == 0), stop=False))
        tmm = b.op("tensor", lambda e: e.matmul(
            ps_a[:, 0:128], ones_sb[0:1, :], bias_qk[0:1, :], start=False, stop=True), inc=True)
        b.wait("scalar", tmm)
        tqk = b.op("scalar", lambda e, i=i: e.activation(
            qk_all[:, i * 128:(i + 1) * 128], ps_a[:, 0:128], AF.Silu), inc=True)
        prev_ev = tqk
        b.wait("gpsimd", prev_ev)
        t_hv = b.dma("gpsimd", hv_scr[i * 128:(i + 1) * 128, :], hv_t[:, :], sem="hvst")
        prev_hvst = t_hv
        ttq, vtq = tr.emit(qk_all[:, i * 128:(i + 1) * 128], qk_fma[:, i * 128:(i + 1) * 128],
                           pre_tensor_waits=[tqk])
        qk_last = vtq
        for ec in range(4):
            b.op("tensor", lambda e, i=i, ec=ec, first=first_lin: e.matmul(
                ps_kv[:, ec * 512:(ec + 1) * 512], qk_all[:, i * 128:(i + 1) * 128],
                hv_t[:, ec * 512:(ec + 1) * 512], start=first, stop=(i == NT - 1)))
        for ec in range(16):
            b.op("tensor", lambda e, i=i, ec=ec, first=first_lin: e.matmul(
                ps_vs[:, ec:ec + 1], hv_t[:, ec * 128:(ec + 1) * 128], onesc[:, 0:1],
                start=(first and ec == 0), stop=(i == NT - 1 and ec == 15)))
        first_lin = False
        prev_pmm = b.mark("tensor")

    b.wait("vector", prev_pmm)
    tvs = b.op("vector", lambda e: e.tensor_copy(vs16[:, :], ps_vs[:, 0:16]), inc=True)
    ttv, vtv = tr.emit(vs16[:, 0:16], vsf[0:16, :], pre_tensor_waits=[tvs], src_p=128, src_f=16)
    b.wait("gpsimd", vtv)
    t_vs1 = b.dma("gpsimd", vs_scr[0:16, :], vsf[:, :], sem="vsst")
    b.wait("sync", t_vs1)
    t_vs2 = b.dma("sync", vsrow[0:1, :], vs_scr.flatten().unsqueeze(0), sem="vsld")
    tlkv = None
    for ec in range(4):
        tlkv = b.op("vector", lambda e, ec=ec: e.tensor_scalar(
            linkv[:, ec * 512:(ec + 1) * 512], ps_kv[:, ec * 512:(ec + 1) * 512],
            gam[:, 3:4], None, ALU.mult), inc=True)
        b.drain("vector")
    b.wait("tensor", tlkv)
    b.wait("tensor", t_vs2)
    tout = None
    for ec in range(4):
        tout = b.op("tensor", lambda e, ec=ec: e.matmul(
            ps_kv[:, ec * 512:(ec + 1) * 512], betr[0:1, :], vsrow[0:1, ec * 512:(ec + 1) * 512],
            start=True, stop=True), inc=(ec == 3))
    b.wait("vector", tout)
    for ec in range(4):
        b.op("vector", lambda e, ec=ec: e.tensor_tensor(
            linkv[:, ec * 512:(ec + 1) * 512], linkv[:, ec * 512:(ec + 1) * 512],
            ps_kv[:, ec * 512:(ec + 1) * 512], ALU.add))
        b.drain("vector")
    tlk2 = b.op("vector", lambda e: e.tensor_scalar(linkv2[:, :], linkv[:, :],
                                                    gam[:, 1:2], None, ALU.mult), inc=True)
    b.drain("vector")
    b.wait("tensor", tlk2)
    for ec in range(4):
        trt = b.op("tensor", lambda e, ec=ec: e.matmul(
            ps_a[0:1, 0:512], betb[:, 1:2], linkv[:, ec * 512:(ec + 1) * 512],
            start=True, stop=True), inc=True)
        b.wait("vector", trt)
        tvr = b.op("vector", lambda e, ec=ec: e.tensor_copy(
            rowt[0:1, ec * 512:(ec + 1) * 512], ps_a[0:1, 0:512]), inc=True)
        b.wait("tensor", tvr)
    rowt_ready = tvr
    b.wait("gpsimd", b.dtok("hvst"))
    scopeA.__exit__(None, None, None)

    ps_s = [b.ps(name + "_pss0", [128, 512], F32), b.ps(name + "_pss1", [128, 512], F32)]
    ps_o = [b.ps(name + "_pso0", [128, 512], F32), b.ps(name + "_pso1", [128, 512], F32)]
    ps_y = b.ps(name + "_psy", [128, 1024], F32)
    prev_store = prev_tln
    prev_simr = [None, None]
    prev_our = [None, None]
    prev_yr = None
    prev_aff = None
    b.wait("tensor", rowt_ready)
    b.wait("tensor", tlk2)
    prev_gho = None
    for g in range(16):
        if prev_gho is not None:
            b.wait("sync", prev_gho)
        hv_toks = []
        for t in range(2):
            hv_toks.append(b.dma("sync", vg_t[t][:, :],
                                 hv_scr[(g * 2 + t) * 128:(g * 2 + t + 1) * 128, 0:2048], sem="vgld"))
            hv_toks.append(b.dma("sync", gt_t[t][:, :],
                                 hv_scr[(g * 2 + t) * 128:(g * 2 + t + 1) * 128, 2048:4096], sem="vgld"))
        if prev_aff is not None:
            b.wait("vector", prev_aff)
        if prev_gho is not None:
            b.wait("vector", prev_gho)
        b.wait("vector", qk_last)
        tqq = b.op("vector", lambda e, g=g: e.tensor_scalar(
            qq[:, :], qk_fma[:, g * 256:(g + 1) * 256], gam[:, 0:1], bet[:, 0:1],
            ALU.mult, op1=ALU.add), inc=True)
        b.drain("vector")
        tkk = b.op("vector", lambda e, g=g: e.tensor_scalar(
            kk[:, :], qk_fma[:, g * 256:(g + 1) * 256], gam[:, 2:3], bet[:, 2:3],
            ALU.mult, op1=ALU.add), inc=True)
        b.drain("vector")
        b.wait("tensor", tkk)
        at_toks = []
        for jc in range(2):
            if prev_simr[jc] is not None:
                b.wait("tensor", prev_simr[jc])
            b.op("tensor", lambda e, jc=jc: e.matmul(
                ps_s[jc][:, 0:256], kk[:, jc * 128:(jc + 1) * 128], qq[:, 0:256],
                start=True, stop=False))
            tsm = b.op("tensor", lambda e, jc=jc: e.matmul(
                ps_s[jc][:, 0:256], idt[:, :], btT[:, jc * 256:(jc + 1) * 256],
                start=False, stop=True), inc=True)
            b.wait("vector", tsm)
            b.op("vector", lambda e, jc=jc: e.tensor_scalar(
                at1[:, :], ps_s[jc][:, 0:256], 0.0, None, ALU.max))
            b.drain("vector")
            tat = b.op("vector", lambda e, jc=jc: e.tensor_tensor(
                attnT[:, jc * 256:(jc + 1) * 256], at1[:, :], at1[:, :], ALU.mult), inc=True)
            b.drain("vector")
            prev_simr[jc] = tat
            at_toks.append(tat)
        prev_aff = tsm
        for t in range(2):
            b.wait("tensor", at_toks[-1])
            b.wait("tensor", hv_toks[-1])
            b.wait("tensor", qk_last)
            for ec in range(4):
                po = ps_o[ec % 2]
                if prev_our[ec % 2] is not None:
                    b.wait("tensor", prev_our[ec % 2])
                for jc in range(2):
                    b.op("tensor", lambda e, g=g, t=t, ec=ec, jc=jc, po=po: e.matmul(
                        po[:, 0:512], attnT[:, jc * 256 + t * 128: jc * 256 + (t + 1) * 128],
                        vg_t[jc][:, ec * 512:(ec + 1) * 512], start=(jc == 0), stop=False))
                b.op("tensor", lambda e, g=g, t=t, ec=ec, po=po: e.matmul(
                    po[:, 0:512], qk_fma[:, (g * 2 + t) * 128:(g * 2 + t + 1) * 128],
                    linkv2[:, ec * 512:(ec + 1) * 512], start=False, stop=False))
                tom = b.op("tensor", lambda e, ec=ec, po=po: e.matmul(
                    po[:, 0:512], ones_sb[0:1, :], rowt[0:1, ec * 512:(ec + 1) * 512],
                    start=False, stop=True), inc=True)
                b.wait("vector", tom)
                tho = b.op("vector", lambda e, t=t, ec=ec, po=po: e.tensor_tensor(
                    ho[:, ec * 512:(ec + 1) * 512], po[:, 0:512],
                    gt_t[t][:, ec * 512:(ec + 1) * 512], ALU.mult), inc=True)
                prev_our[ec % 2] = tho
            for dc in range(16):
                pre = [prev_our[(dc // 4 + 1) % 2], prev_our[dc % 2]] if dc == 0 else []
                tt2, vt2 = tr.emit(ho[:, dc * 128:(dc + 1) * 128], hoT[:, dc * 128:(dc + 1) * 128],
                                   pre_tensor_waits=pre)
            b.wait("vector", tt2)
            b.wait("tensor", vt2)
            i_t = g * 2 + t
            if prev_store is not None:
                b.wait("sync", prev_store)
            t_x = b.dma("sync", xt[:], x_in[i_t * 128:(i_t + 1) * 128, :], sem="xld2")
            b.wait("vector", t_x)
            t1 = emit_ln(b, xt, z1, st)
            if prev_yr is not None:
                b.wait("tensor", prev_yr)
            for n2 in range(2):
                for dc in range(16):
                    b.op("tensor", lambda e, n2=n2, dc=dc: e.matmul(
                        ps_y[:, n2 * 512:(n2 + 1) * 512], hoT[:, dc * 128:(dc + 1) * 128],
                        wo_sb[:, dc * 1024 + n2 * 512: dc * 1024 + (n2 + 1) * 512],
                        start=(dc == 0), stop=False))
                b.op("tensor", lambda e, n2=n2: e.matmul(
                    ps_y[:, n2 * 512:(n2 + 1) * 512], ones_sb[0:1, :],
                    bo_row[0:1, n2 * 512:(n2 + 1) * 512], start=False, stop=True))
            tym = b.mark("tensor")
            b.wait("vector", tym)
            b.wait("vector", t1)
            tvy = None
            for n2 in range(2):
                b.op("vector", lambda e, n2=n2: e.tensor_tensor(
                    xt[:, n2 * 512:(n2 + 1) * 512], xt[:, n2 * 512:(n2 + 1) * 512],
                    ps_y[:, n2 * 512:(n2 + 1) * 512], ALU.add))
                b.drain("vector")
                tvy = b.op("vector", lambda e, n2=n2: e.tensor_tensor(
                    xt[:, n2 * 512:(n2 + 1) * 512], xt[:, n2 * 512:(n2 + 1) * 512],
                    z1[:, n2 * 512:(n2 + 1) * 512], ALU.add), inc=True)
                b.drain("vector")
            prev_yr = tvy
            b.wait("gpsimd", tvy)
            prev_store = b.dma("gpsimd", x_out[i_t * 128:(i_t + 1) * 128, :], xt[:], sem="xst")
            prev_gho = tym


def build_l0_attn_launch():
    nc = bass.Bass("TRN2", target_bir_lowering=False)
    x_in = nc.dram_tensor("x", [N_TOK, D], F32, kind="ExternalInput").ap()
    norm_g = nc.dram_tensor("norm_g", [1024], F32, kind="ExternalInput").ap()
    norm_b = nc.dram_tensor("norm_b", [1024], F32, kind="ExternalInput").ap()
    w_h = nc.dram_tensor("w_h", [1024, 4096], F32, kind="ExternalInput").ap()
    b_h = nc.dram_tensor("b_h", [4096], F32, kind="ExternalInput").ap()
    w_qk = nc.dram_tensor("w_qk", [1024, 128], F32, kind="ExternalInput").ap()
    b_qk = nc.dram_tensor("b_qk", [128], F32, kind="ExternalInput").ap()
    qk_gamma = nc.dram_tensor("qk_gamma", [4, 128], F32, kind="ExternalInput").ap()
    qk_beta = nc.dram_tensor("qk_beta", [4, 128], F32, kind="ExternalInput").ap()
    rel_emb = nc.dram_tensor("rel_emb", [32, 1], F32, kind="ExternalInput").ap()
    w_o = nc.dram_tensor("w_o", [2048, 1024], F32, kind="ExternalInput").ap()
    b_o = nc.dram_tensor("b_o", [1024], F32, kind="ExternalInput").ap()
    ident = nc.dram_tensor("ident", [128, 128], BF16, kind="ExternalInput").ap()
    ones = nc.dram_tensor("ones", [1, 128], BF16, kind="ExternalInput").ap()
    onesc = nc.dram_tensor("onesc", [128, 1], BF16, kind="ExternalInput").ap()
    relmask = nc.dram_tensor("relmask", [128, 2 * 32 * 256], BF16, kind="ExternalInput").ap()
    x_out = nc.dram_tensor("x_out", [N_TOK, D], F32, kind="ExternalOutput").ap()
    hv_scr = nc.dram_tensor("hv_scr", [N_TOK, 4096], BF16).ap()
    vs_scr = nc.dram_tensor("vs_scr", [16, 128], BF16).ap()

    b = B(nc)
    emit_l0_attn(b, x_in, x_out, norm_g, norm_b, w_h, b_h, w_qk, b_qk, qk_gamma, qk_beta,
                 rel_emb, w_o, b_o, hv_scr, vs_scr, ident, ones, onesc, relmask, name="l0a")
    b.wait("gpsimd", b.dtok("xst"))
    return b.build()


def run_l0_attn(x_full, norm_g, norm_b, w_h, b_h, w_qk, b_qk, qk_gamma, qk_beta,
                rel_emb, w_o, b_o):
    nc = build_l0_attn_launch()
    in_maps = []
    for c in range(N_CORES):
        in_maps.append({
            "x": np.ascontiguousarray(x_full[c]),
            "norm_g": np.asarray(norm_g), "norm_b": np.asarray(norm_b),
            "w_h": np.asarray(w_h), "b_h": np.asarray(b_h),
            "w_qk": np.asarray(w_qk), "b_qk": np.asarray(b_qk),
            "qk_gamma": np.asarray(qk_gamma), "qk_beta": np.asarray(qk_beta),
            "rel_emb": np.asarray(rel_emb), "w_o": np.asarray(w_o), "b_o": np.asarray(b_o),
            "ident": _IDENT, "ones": _ONES, "onesc": _ONESC, "relmask": _RELMASK,
        })
    res = run_bass_kernel_spmd(nc, in_maps, core_ids=list(range(N_CORES)))
    return np.stack([res.results[c]["x_out"] for c in range(N_CORES)])



def _l2_maskadd_const():
    m = np.full((128, 128), -30000.0, dtype=bfdt)
    for g in range(16):
        m[g * 8:(g + 1) * 8, g * 8:(g + 1) * 8] = 0.0
    return m


_L2MASK = _l2_maskadd_const()


def emit_ln_only(b, x_in, xn_out, name="lno"):
    xt = b.sb(name + "_x", [128, 1024], F32)
    xn = b.sb(name + "_xn", [128, 1024], F32)
    st = {"stats": b.sb(name + "_st", [128, 12], F32),
          "mv": b.sb(name + "_mv", [128, 2], F32),
          "r1": b.sb(name + "_r1", [128, 1], F32)}
    prev = None
    prev_st = None
    for i in range(32):
        if prev is not None:
            b.wait("sync", prev)
        t_x = b.dma("sync", xt[:], x_in[i * 128:(i + 1) * 128, :], sem="xld")
        b.wait("vector", t_x)
        if prev_st is not None:
            b.wait("vector", prev_st)
        tln = emit_ln(b, xt, xn, st)
        prev = tln
        b.wait("gpsimd", tln)
        prev_st = b.dma("gpsimd", xn_out[i * 128:(i + 1) * 128, :], xn[:], sem="xst")


def build_l2_ln_launch():
    nc = bass.Bass("TRN2", target_bir_lowering=False)
    x_in = nc.dram_tensor("x", [N_TOK, D], F32, kind="ExternalInput").ap()
    xn_out = nc.dram_tensor("xn_out", [N_TOK, D], F32, kind="ExternalOutput").ap()
    b = B(nc)
    emit_ln_only(b, x_in, xn_out)
    b.wait("gpsimd", b.dtok("xst"))
    return b.build()


def emit_l2_attn_core(b, xn_in, out_hbm, wq, wk, wv, wo, bq, bk, bv, bo, gn, lq1, lk1, lq2, lk2,
                      idt_ap, ones_ap, maskadd_ap, name="l2a"):
    wq_sb = b.sb(name + "_wq", [128, 8 * 1024], BF16)
    wk_sb = b.sb(name + "_wk", [128, 8 * 1024], BF16)
    wv_sb = b.sb(name + "_wv", [128, 8 * 1024], BF16)
    wo_sb = b.sb(name + "_wo", [128, 8 * 1024], BF16)
    xt = b.sb(name + "_x", [128, 1024], BF16)
    xnT = b.sb(name + "_xnT", [128, 8 * 128], BF16)
    q_tm = b.sb(name + "_q", [128, 1024], BF16)
    k_tm = b.sb(name + "_k", [128, 1024], BF16)
    v_tm = b.sb(name + "_v", [128, 1024], BF16)
    q_fm = b.sb(name + "_qf", [128, 8 * 128], BF16)
    k_fm = b.sb(name + "_kf", [128, 8 * 128], BF16)
    brow = b.sb(name + "_brow", [1, 3 * 1024 + 1024], BF16)
    gncol = b.sb(name + "_gn", [128, 1], F32)
    g8 = b.sb(name + "_g8", [1, 128], BF16)
    lrow = b.sb(name + "_lrow", [1, 4 * 64], F32)
    lt = b.sb(name + "_lt", [1, 64], F32)
    lsc = b.sb(name + "_lsc", [1, 2], F32)
    nlam = b.sb(name + "_nlam", [1, 1], BF16)
    nlamc = b.sb(name + "_nlamc", [128, 1], F32)
    mka = b.sb(name + "_mka", [128, 128], BF16)
    sS = b.sb(name + "_sS", [128, 2 * 128], F32)
    mx = b.sb(name + "_mx", [128, 2], F32)
    sm = b.sb(name + "_sm", [128, 2], F32)
    pn = b.sb(name + "_pn", [128, 2 * 128], BF16)
    dd = b.sb(name + "_dd", [128, 128], BF16)
    ddT = b.sb(name + "_ddT", [128, 128], BF16)
    ao = b.sb(name + "_ao", [128, 1024], F32)
    aos = b.sb(name + "_aos", [128, 1024], F32)
    s2 = b.sb(name + "_s2", [128, 8], F32)
    ho = b.sb(name + "_ho", [128, 1024], BF16)
    hoT = b.sb(name + "_hoT", [128, 8 * 128], BF16)
    idt = b.sb(name + "_idt", [128, 128], BF16)
    ones_sb = b.sb(name + "_ones", [1, 128], BF16)
    tr = TransposeHelper(b, name, idt)
    ps_p = b.ps(name + "_psp", [128, 512], F32)
    ps_s = [b.ps(name + "_pss0", [128, 512], F32), b.ps(name + "_pss1", [128, 512], F32)]
    ps_o = b.ps(name + "_pso", [128, 512], F32)
    ps_y = b.ps(name + "_psy", [128, 1024], F32)

    b.dma("sync", idt[:], idt_ap[:, :], sem="wres")
    b.dma("sync", ones_sb[:], ones_ap[:, :], sem="wres")
    b.dma("sync", mka[:], maskadd_ap[:, :], sem="wres")
    for kc in range(8):
        b.dma("gpsimd", wq_sb[:, kc * 1024:(kc + 1) * 1024], wq[kc * 128:(kc + 1) * 128, :], sem="wcst")
        b.dma("gpsimd", wk_sb[:, kc * 1024:(kc + 1) * 1024], wk[kc * 128:(kc + 1) * 128, :], sem="wcst")
        b.dma("gpsimd", wv_sb[:, kc * 1024:(kc + 1) * 1024], wv[kc * 128:(kc + 1) * 128, :], sem="wcst")
        b.dma("gpsimd", wo_sb[:, kc * 1024:(kc + 1) * 1024], wo[kc * 128:(kc + 1) * 128, :], sem="wcst")
    b.dma("gpsimd", brow[0:1, 0:1024], bq.unsqueeze(0), sem="wcst")
    b.dma("gpsimd", brow[0:1, 1024:2048], bk.unsqueeze(0), sem="wcst")
    b.dma("gpsimd", brow[0:1, 2048:3072], bv.unsqueeze(0), sem="wcst")
    b.dma("gpsimd", brow[0:1, 3072:4096], bo.unsqueeze(0), sem="wcst")
    b.dma("gpsimd", g8[0:1, :], gn.unsqueeze(0), sem="wcst")
    b.dma("sync", lrow[0:1, 0:64], lq1.unsqueeze(0), sem="wres")
    b.dma("sync", lrow[0:1, 64:128], lk1.unsqueeze(0), sem="wres")
    b.dma("sync", lrow[0:1, 128:192], lq2.unsqueeze(0), sem="wres")
    b.dma("sync", lrow[0:1, 192:256], lk2.unsqueeze(0), sem="wres")
    w_tok = b.dtok("wres")
    c_tok = b.dtok("wcst")

    b.wait("vector", w_tok)
    b.wait("vector", c_tok)
    for j in range(2):
        b.op("vector", lambda e, j=j: e.tensor_tensor(
            lt[0:1, :], lrow[0:1, j * 128:j * 128 + 64], lrow[0:1, j * 128 + 64:j * 128 + 128],
            ALU.mult))
        b.drain("vector")
        b.op("vector", lambda e, j=j: e.tensor_reduce(lsc[0:1, j:j + 1], lt[0:1, :], AX.X, ALU.add))
        b.drain("vector")
    tsc = b.mark("vector")
    b.wait("scalar", tsc)
    tex = b.op("scalar", lambda e: e.activation(lsc[0:1, 0:2], lsc[0:1, 0:2], AF.Exp), inc=True)
    b.wait("vector", tex)
    b.op("vector", lambda e: e.tensor_tensor(lsc[0:1, 0:1], lsc[0:1, 1:2], lsc[0:1, 0:1],
                                             ALU.subtract))
    b.drain("vector")
    tnl = b.op("vector", lambda e: e.tensor_scalar(nlam[0:1, 0:1], lsc[0:1, 0:1], -0.8, None,
                                                   ALU.add), inc=True)
    b.wait("tensor", tnl)
    tbl = b.op("tensor", lambda e: e.matmul(ps_p[:, 0:1], ones_sb[0:1, :], nlam[0:1, 0:1],
                                            start=True, stop=True), inc=True)
    b.wait("vector", tbl)
    tnlc = b.op("vector", lambda e: e.tensor_copy(nlamc[:], ps_p[:, 0:1]), inc=True)
    ttg, vtg = tr.emit(g8[0:1, :], gncol[:, 0:1], pre_tensor_waits=[c_tok, tnlc],
                       src_p=1, src_f=128)
    b.wait("vector", vtg)
    b.drain("vector")
    tgs = b.op("vector", lambda e: e.tensor_scalar(gncol[:], gncol[:], 0.2, None, ALU.mult),
               inc=True)
    b.drain("vector")
    b.wait("vector", c_tok)
    for kc in range(8):
        b.op("vector", lambda e, kc=kc: e.tensor_scalar(
            wo_sb[:, kc * 1024:(kc + 1) * 1024], wo_sb[:, kc * 1024:(kc + 1) * 1024],
            gncol[:, 0:1], None, ALU.mult))
        b.drain("vector")
    tws = b.mark("vector")
    b.wait("tensor", tws)

    prev_store = None
    prev_pr = None
    prev_hoT = None
    for i in range(32):
        t_x = b.dma("gpsimd", xt[:], xn_in[i * 128:(i + 1) * 128, :], sem="xld")
        for kc in range(8):
            pre = [t_x]
            if kc == 0 and prev_pr is not None:
                pre.append(prev_pr)
            tt, vt = tr.emit(xt[:, kc * 128:(kc + 1) * 128], xnT[:, kc * 128:(kc + 1) * 128],
                             pre_tensor_waits=pre)
        b.wait("tensor", vt)
        evs = []
        for (wsb, dst, boff, scale) in ((wq_sb, q_tm, 0, 0.125), (wk_sb, k_tm, 1024, None),
                                        (wv_sb, v_tm, 2048, None)):
            for half in range(2):
                if evs:
                    b.wait("tensor", evs[-1])
                for kc in range(8):
                    b.op("tensor", lambda e, kc=kc, half=half, wsb=wsb: e.matmul(
                        ps_p[:, 0:512], xnT[:, kc * 128:(kc + 1) * 128],
                        wsb[:, kc * 1024 + half * 512: kc * 1024 + (half + 1) * 512],
                        start=(kc == 0), stop=False))
                tmm = b.op("tensor", lambda e, half=half, boff=boff: e.matmul(
                    ps_p[:, 0:512], ones_sb[0:1, :],
                    brow[0:1, boff + half * 512: boff + (half + 1) * 512],
                    start=False, stop=True), inc=True)
                b.wait("vector", tmm)
                if scale is None:
                    tev = b.op("vector", lambda e, half=half, dst=dst: e.tensor_copy(
                        dst[:, half * 512:(half + 1) * 512], ps_p[:, 0:512]), inc=True)
                else:
                    tev = b.op("vector", lambda e, half=half, dst=dst, scale=scale: e.tensor_scalar(
                        dst[:, half * 512:(half + 1) * 512], ps_p[:, 0:512],
                        scale, None, ALU.mult), inc=True)
                evs.append(tev)
        for kc in range(8):
            tt, vt = tr.emit(q_tm[:, kc * 128:(kc + 1) * 128], q_fm[:, kc * 128:(kc + 1) * 128],
                             pre_tensor_waits=[evs[1]] if kc == 0 else ())
            tt, vt2 = tr.emit(k_tm[:, kc * 128:(kc + 1) * 128], k_fm[:, kc * 128:(kc + 1) * 128],
                              pre_tensor_waits=[evs[3]] if kc == 0 else ())
        b.wait("tensor", vt2)
        prev_pr = b.tok("tensor")

        prev_ao = None
        for pr in range(8):
            exps = []
            for j in range(2):
                h = 2 * pr + j
                hc, par = h // 2, h % 2
                koff = par * 64
                b.op("tensor", lambda e, hc=hc, koff=koff, j=j: e.matmul(
                    ps_s[j][:, 0:128], q_fm[koff:koff + 64, hc * 128:(hc + 1) * 128],
                    k_fm[koff:koff + 64, hc * 128:(hc + 1) * 128], start=True, stop=False))
                tsm = b.op("tensor", lambda e, j=j: e.matmul(
                    ps_s[j][:, 0:128], idt[:, :], mka[:, :], start=False, stop=True), inc=True)
                b.wait("vector", tsm)
                tevs = b.op("vector", lambda e, j=j: e.tensor_copy(
                    sS[:, j * 128:(j + 1) * 128], ps_s[j][:, 0:128]), inc=True)
                b.drain("vector")
                b.op("vector", lambda e, j=j: e.tensor_reduce(
                    mx[:, j:j + 1], sS[:, j * 128:(j + 1) * 128], AX.X, ALU.max, negate=True))
                tmx = b.mark("vector")
                b.wait("scalar", tmx)
                texp = b.op("scalar", lambda e, j=j: e.activation(
                    sS[:, j * 128:(j + 1) * 128], sS[:, j * 128:(j + 1) * 128], AF.Exp,
                    bias=mx[:, j:j + 1]), inc=True)
                b.wait("vector", texp)
                b.drain("vector")
                b.op("vector", lambda e, j=j: e.tensor_reduce(
                    sm[:, j:j + 1], sS[:, j * 128:(j + 1) * 128], AX.X, ALU.add))
                b.drain("vector")
                b.op("vector", lambda e, j=j: e.reciprocal(sm[:, j:j + 1], sm[:, j:j + 1]))
                b.drain("vector")
                tpn = b.op("vector", lambda e, j=j: e.tensor_scalar(
                    pn[:, j * 128:(j + 1) * 128], sS[:, j * 128:(j + 1) * 128],
                    sm[:, j:j + 1], None, ALU.mult), inc=True)
                b.drain("vector")
                exps.append(tpn)
            tdd = b.op("vector", lambda e: e.scalar_tensor_tensor(
                dd[:, :], pn[:, 128:256], nlamc[:, 0:1], pn[:, 0:128], ALU.mult, ALU.add),
                inc=True)
            b.drain("vector")
            ttd, vtd = tr.emit(dd[:, :], ddT[:, :], pre_tensor_waits=[tdd])
            b.wait("tensor", vtd)
            if prev_ao is not None:
                b.wait("tensor", prev_ao)
            tav = b.op("tensor", lambda e, pr=pr: e.matmul(
                ps_o[:, 0:128], ddT[:, :], v_tm[:, pr * 128:(pr + 1) * 128],
                start=True, stop=True), inc=True)
            b.wait("vector", tav)
            prev_ao = b.op("vector", lambda e, pr=pr: e.tensor_copy(
                ao[:, pr * 128:(pr + 1) * 128], ps_o[:, 0:128]), inc=True)
        b.wait("scalar", prev_ao)
        if prev_store is not None:
            b.wait("scalar", prev_store)
        tsq = b.op("scalar", lambda e: e.activation(aos[:], ao[:], AF.Square), inc=True)
        b.wait("vector", tsq)
        b.drain("vector")
        b.op("vector", lambda e: e.tensor_reduce(
            s2[:], aos[:, :].rearrange("p (a c) -> p a c", c=128), AX.X, ALU.add))
        b.drain("vector")
        b.op("vector", lambda e: e.tensor_scalar(s2[:], s2[:], 1.0 / 128.0, 1e-5, ALU.mult,
                                                 op1=ALU.add))
        b.drain("vector")
        b.op("vector", lambda e: e.reciprocal(s2[:], s2[:]))
        trc2 = b.mark("vector")
        b.wait("scalar", trc2)
        trs = b.op("scalar", lambda e: e.activation(s2[:], s2[:], AF.Sqrt), inc=True)
        b.wait("vector", trs)
        tho = b.op("vector", lambda e: e.tensor_tensor(
            ho[:, :].rearrange("p (a c) -> p a c", c=128), ao[:, :].rearrange("p (a c) -> p a c", c=128),
            s2[:, :].unsqueeze(2).broadcast_to([128, 8, 128]), ALU.mult), inc=True)
        b.drain("vector")
        for kc in range(8):
            pre = [tho]
            if kc == 0 and prev_hoT is not None:
                pre.append(prev_hoT)
            tt, vt = tr.emit(ho[:, kc * 128:(kc + 1) * 128], hoT[:, kc * 128:(kc + 1) * 128],
                             pre_tensor_waits=pre if kc == 0 else ())
        b.wait("tensor", vt)
        for n2 in range(2):
            for kc in range(8):
                b.op("tensor", lambda e, n2=n2, kc=kc: e.matmul(
                    ps_y[:, n2 * 512:(n2 + 1) * 512], hoT[:, kc * 128:(kc + 1) * 128],
                    wo_sb[:, kc * 1024 + n2 * 512: kc * 1024 + (n2 + 1) * 512],
                    start=(kc == 0), stop=False))
            b.op("tensor", lambda e, n2=n2: e.matmul(
                ps_y[:, n2 * 512:(n2 + 1) * 512], ones_sb[0:1, :],
                brow[0:1, 3072 + n2 * 512: 3072 + (n2 + 1) * 512], start=False, stop=True))
        tym = b.mark("tensor")
        prev_hoT = tym
        b.wait("vector", tym)
        tvy = b.op("vector", lambda e: e.tensor_copy(aos[:], ps_y[:, 0:1024]), inc=True)
        b.wait("gpsimd", tvy)
        if prev_store is not None:
            b.wait("gpsimd", prev_store)
        prev_store = b.dma("gpsimd", out_hbm[i * 128:(i + 1) * 128, :], aos[:], sem="xst")
        b.wait("sync", tvy)
        b.wait("sync", prev_pr)


def build_l2_attn_core_launch():
    nc = bass.Bass("TRN2", target_bir_lowering=False)
    xn_in = nc.dram_tensor("xn", [N_TOK, D], F32, kind="ExternalInput").ap()
    names = ["Wq", "Wk", "Wv", "Wo"]
    w = {n: nc.dram_tensor(n, [1024, 1024], F32, kind="ExternalInput").ap() for n in names}
    bq = nc.dram_tensor("bq", [1024], F32, kind="ExternalInput").ap()
    bk = nc.dram_tensor("bk", [1024], F32, kind="ExternalInput").ap()
    bv = nc.dram_tensor("bv", [1024], F32, kind="ExternalInput").ap()
    bo = nc.dram_tensor("bo", [1024], F32, kind="ExternalInput").ap()
    gn = nc.dram_tensor("g_norm", [128], F32, kind="ExternalInput").ap()
    lq1 = nc.dram_tensor("lq1", [64], F32, kind="ExternalInput").ap()
    lk1 = nc.dram_tensor("lk1", [64], F32, kind="ExternalInput").ap()
    lq2 = nc.dram_tensor("lq2", [64], F32, kind="ExternalInput").ap()
    lk2 = nc.dram_tensor("lk2", [64], F32, kind="ExternalInput").ap()
    ident = nc.dram_tensor("ident", [128, 128], BF16, kind="ExternalInput").ap()
    ones = nc.dram_tensor("ones", [1, 128], BF16, kind="ExternalInput").ap()
    maskadd = nc.dram_tensor("maskadd", [128, 128], BF16, kind="ExternalInput").ap()
    out = nc.dram_tensor("out", [N_TOK, D], F32, kind="ExternalOutput").ap()
    wbf = {n: nc.dram_tensor(n + "_bf", [1024, 1024], BF16).ap() for n in names}

    b = B(nc)
    for n in names:
        emit_cast_pass(b, w[n], wbf[n], 1024)
    b.wait("gpsimd", b.dtok("ppcast"))
    emit_l2_attn_core(b, xn_in, out, wbf["Wq"], wbf["Wk"], wbf["Wv"], wbf["Wo"],
                      bq, bk, bv, bo, gn, lq1, lk1, lq2, lk2, ident, ones, maskadd)
    b.wait("gpsimd", b.dtok("xst"))
    return b.build()


def build_add_launch():
    nc = bass.Bass("TRN2", target_bir_lowering=False)
    x_in = nc.dram_tensor("x", [N_TOK, D], F32, kind="ExternalInput").ap()
    y_in = nc.dram_tensor("y", [N_TOK, D], F32, kind="ExternalInput").ap()
    x_out = nc.dram_tensor("x_out", [N_TOK, D], F32, kind="ExternalOutput").ap()
    b = B(nc)
    xt = b.sb("a_x", [128, 1024], F32)
    yt = b.sb("a_y", [128, 1024], F32)
    prev = None
    for i in range(32):
        if prev is not None:
            b.wait("sync", prev)
        t1 = b.dma("sync", xt[:], x_in[i * 128:(i + 1) * 128, :], sem="xld")
        t2 = b.dma("sync", yt[:], y_in[i * 128:(i + 1) * 128, :], sem="xld")
        b.wait("vector", t2)
        tv = b.op("vector", lambda e: e.tensor_tensor(xt[:], xt[:], yt[:], ALU.add), inc=True)
        b.drain("vector")
        b.wait("gpsimd", tv)
        prev = b.dma("gpsimd", x_out[i * 128:(i + 1) * 128, :], xt[:], sem="xst")
        b.wait("sync", tv)
    b.wait("gpsimd", b.dtok("xst"))
    return b.build()


def run_l2_attn(x_full, Wq, bq, Wk, bk, Wv, bv, Wo, bo, g_norm, lq1, lk1, lq2, lk2):
    nc1 = build_l2_ln_launch()
    in_maps = [{"x": np.ascontiguousarray(x_full[c])} for c in range(N_CORES)]
    r1 = run_bass_kernel_spmd(nc1, in_maps, core_ids=list(range(N_CORES)))
    xn = np.stack([r1.results[c]["xn_out"] for c in range(N_CORES)])
    xn_pm = [np.ascontiguousarray(
        xn[:, c * 512:(c + 1) * 512, :].transpose(1, 0, 2).reshape(N_TOK, D))
        for c in range(N_CORES)]
    nc2 = build_l2_attn_core_launch()
    in_maps = []
    for c in range(N_CORES):
        in_maps.append({
            "xn": xn_pm[c], "Wq": np.asarray(Wq), "Wk": np.asarray(Wk), "Wv": np.asarray(Wv),
            "Wo": np.asarray(Wo), "bq": np.asarray(bq), "bk": np.asarray(bk),
            "bv": np.asarray(bv), "bo": np.asarray(bo), "g_norm": np.asarray(g_norm),
            "lq1": np.asarray(lq1), "lk1": np.asarray(lk1), "lq2": np.asarray(lq2),
            "lk2": np.asarray(lk2),
            "ident": _IDENT, "ones": _ONES, "maskadd": _L2MASK,
        })
    r2 = run_bass_kernel_spmd(nc2, in_maps, core_ids=list(range(N_CORES)))
    outs = [r2.results[c]["out"].reshape(512, 8, D) for c in range(N_CORES)]
    out_full = np.concatenate(outs, axis=0).transpose(1, 0, 2)
    nc3 = build_add_launch()
    in_maps = [{"x": np.ascontiguousarray(x_full[c]), "y": np.ascontiguousarray(out_full[c])}
               for c in range(N_CORES)]
    r3 = run_bass_kernel_spmd(nc3, in_maps, core_ids=list(range(N_CORES)))
    return np.stack([r3.results[c]["x_out"] for c in range(N_CORES)])



def emit_l2_attn_fused(b, x_in, x_out, wq, wk, wv, wo, bq, bk, bv, bo, gn, lq1, lk1, lq2, lk2,
                       idt_ap, ones_ap, maskadd_ap, name="l2b"):
    wq_sb = b.sb(name + "_wq", [128, 8 * 1024], BF16)
    wk_sb = b.sb(name + "_wk", [128, 8 * 1024], BF16)
    wv_sb = b.sb(name + "_wv", [128, 8 * 1024], BF16)
    wo_sb = b.sb(name + "_wo", [128, 8 * 1024], BF16)
    xf = b.sb(name + "_xf", [128, 1024], F32)
    xt = b.sb(name + "_x", [128, 1024], BF16)
    xnT = b.sb(name + "_xnT", [128, 8 * 128], BF16)
    q_tm = b.sb(name + "_q", [128, 1024], BF16)
    k_tm = b.sb(name + "_k", [128, 1024], BF16)
    v_tm = b.sb(name + "_v", [128, 1024], BF16)
    q_fm = b.sb(name + "_qf", [128, 8 * 128], BF16)
    k_fm = b.sb(name + "_kf", [128, 8 * 128], BF16)
    brow = b.sb(name + "_brow", [1, 3 * 1024 + 1024], BF16)
    gncol = b.sb(name + "_gn", [128, 1], F32)
    g8 = b.sb(name + "_g8", [1, 128], BF16)
    lrow = b.sb(name + "_lrow", [1, 4 * 64], F32)
    lt = b.sb(name + "_lt", [1, 64], F32)
    lsc = b.sb(name + "_lsc", [1, 2], F32)
    nlam = b.sb(name + "_nlam", [1, 1], BF16)
    nlamc = b.sb(name + "_nlamc", [128, 1], F32)
    mka = b.sb(name + "_mka", [128, 128], BF16)
    sS = b.sb(name + "_sS", [128, 2 * 128], F32)
    mx = b.sb(name + "_mx", [128, 2], F32)
    sm = b.sb(name + "_sm", [128, 2], F32)
    pn = b.sb(name + "_pn", [128, 2 * 128], BF16)
    dd = b.sb(name + "_dd", [128, 128], BF16)
    ddT = b.sb(name + "_ddT", [128, 128], BF16)
    ao = b.sb(name + "_ao", [128, 1024], F32)
    aos = b.sb(name + "_aos", [128, 1024], F32)
    s2 = b.sb(name + "_s2", [128, 8], F32)
    ho = b.sb(name + "_ho", [128, 1024], BF16)
    hoT = b.sb(name + "_hoT", [128, 8 * 128], BF16)
    idt = b.sb(name + "_idt", [128, 128], BF16)
    ones_sb = b.sb(name + "_ones", [1, 128], BF16)
    st = {"stats": b.sb(name + "_st", [128, 12], F32),
          "mv": b.sb(name + "_mv", [128, 2], F32),
          "r1": b.sb(name + "_r1", [128, 1], F32)}
    tr = TransposeHelper(b, name, idt)
    ps_p = b.ps(name + "_psp", [128, 512], F32)
    ps_s = [b.ps(name + "_pss0", [128, 512], F32), b.ps(name + "_pss1", [128, 512], F32)]
    ps_o = b.ps(name + "_pso", [128, 512], F32)
    ps_y = b.ps(name + "_psy", [128, 1024], F32)

    b.dma("sync", idt[:], idt_ap[:, :], sem="wres")
    b.dma("sync", ones_sb[:], ones_ap[:, :], sem="wres")
    b.dma("sync", mka[:], maskadd_ap[:, :], sem="wres")
    for kc in range(8):
        b.dma("gpsimd", wq_sb[:, kc * 1024:(kc + 1) * 1024], wq[kc * 128:(kc + 1) * 128, :], sem="wcst")
        b.dma("gpsimd", wk_sb[:, kc * 1024:(kc + 1) * 1024], wk[kc * 128:(kc + 1) * 128, :], sem="wcst")
        b.dma("gpsimd", wv_sb[:, kc * 1024:(kc + 1) * 1024], wv[kc * 128:(kc + 1) * 128, :], sem="wcst")
        b.dma("gpsimd", wo_sb[:, kc * 1024:(kc + 1) * 1024], wo[kc * 128:(kc + 1) * 128, :], sem="wcst")
    b.dma("gpsimd", brow[0:1, 0:1024], bq.unsqueeze(0), sem="wcst")
    b.dma("gpsimd", brow[0:1, 1024:2048], bk.unsqueeze(0), sem="wcst")
    b.dma("gpsimd", brow[0:1, 2048:3072], bv.unsqueeze(0), sem="wcst")
    b.dma("gpsimd", brow[0:1, 3072:4096], bo.unsqueeze(0), sem="wcst")
    b.dma("gpsimd", g8[0:1, :], gn.unsqueeze(0), sem="wcst")
    b.dma("sync", lrow[0:1, 0:64], lq1.unsqueeze(0), sem="wres")
    b.dma("sync", lrow[0:1, 64:128], lk1.unsqueeze(0), sem="wres")
    b.dma("sync", lrow[0:1, 128:192], lq2.unsqueeze(0), sem="wres")
    b.dma("sync", lrow[0:1, 192:256], lk2.unsqueeze(0), sem="wres")
    w_tok = b.dtok("wres")
    c_tok = b.dtok("wcst")

    b.wait("vector", w_tok)
    b.wait("vector", c_tok)
    for j in range(2):
        b.op("vector", lambda e, j=j: e.tensor_tensor(
            lt[0:1, :], lrow[0:1, j * 128:j * 128 + 64], lrow[0:1, j * 128 + 64:j * 128 + 128],
            ALU.mult))
        b.drain("vector")
        b.op("vector", lambda e, j=j: e.tensor_reduce(lsc[0:1, j:j + 1], lt[0:1, :], AX.X, ALU.add))
        b.drain("vector")
    tsc = b.mark("vector")
    b.wait("scalar", tsc)
    tex = b.op("scalar", lambda e: e.activation(lsc[0:1, 0:2], lsc[0:1, 0:2], AF.Exp), inc=True)
    b.wait("vector", tex)
    b.op("vector", lambda e: e.tensor_tensor(lsc[0:1, 0:1], lsc[0:1, 1:2], lsc[0:1, 0:1],
                                             ALU.subtract))
    b.drain("vector")
    tnl = b.op("vector", lambda e: e.tensor_scalar(nlam[0:1, 0:1], lsc[0:1, 0:1], -0.8, None,
                                                   ALU.add), inc=True)
    b.wait("tensor", tnl)
    tbl = b.op("tensor", lambda e: e.matmul(ps_p[:, 0:1], ones_sb[0:1, :], nlam[0:1, 0:1],
                                            start=True, stop=True), inc=True)
    b.wait("vector", tbl)
    tnlc = b.op("vector", lambda e: e.tensor_copy(nlamc[:], ps_p[:, 0:1]), inc=True)
    ttg, vtg = tr.emit(g8[0:1, :], gncol[:, 0:1], pre_tensor_waits=[c_tok, tnlc],
                       src_p=1, src_f=128)
    b.wait("vector", vtg)
    b.drain("vector")
    tgs = b.op("vector", lambda e: e.tensor_scalar(gncol[:], gncol[:], 0.2, None, ALU.mult),
               inc=True)
    b.drain("vector")
    b.wait("vector", c_tok)
    for kc in range(8):
        b.op("vector", lambda e, kc=kc: e.tensor_scalar(
            wo_sb[:, kc * 1024:(kc + 1) * 1024], wo_sb[:, kc * 1024:(kc + 1) * 1024],
            gncol[:, 0:1], None, ALU.mult))
        b.drain("vector")
    tws = b.mark("vector")
    b.wait("tensor", tws)

    prev_store = None
    prev_pr = None
    prev_hoT = None
    prev_tln = None
    for i in range(32):
        if prev_tln is not None:
            b.wait("sync", prev_tln)
        t_x = b.dma("sync", xf[:], x_in[i * 128:(i + 1) * 128, :], sem="xldf")
        b.wait("vector", t_x)
        tln = emit_ln(b, xf, xt, st)
        prev_tln = tln
        for kc in range(8):
            pre = [tln]
            if kc == 0 and prev_pr is not None:
                pre.append(prev_pr)
            tt, vt = tr.emit(xt[:, kc * 128:(kc + 1) * 128], xnT[:, kc * 128:(kc + 1) * 128],
                             pre_tensor_waits=pre)
        b.wait("tensor", vt)
        evs = []
        for (wsb, dst, boff, scale) in ((wq_sb, q_tm, 0, 0.125), (wk_sb, k_tm, 1024, None),
                                        (wv_sb, v_tm, 2048, None)):
            for half in range(2):
                if evs:
                    b.wait("tensor", evs[-1])
                for kc in range(8):
                    b.op("tensor", lambda e, kc=kc, half=half, wsb=wsb: e.matmul(
                        ps_p[:, 0:512], xnT[:, kc * 128:(kc + 1) * 128],
                        wsb[:, kc * 1024 + half * 512: kc * 1024 + (half + 1) * 512],
                        start=(kc == 0), stop=False))
                tmm = b.op("tensor", lambda e, half=half, boff=boff: e.matmul(
                    ps_p[:, 0:512], ones_sb[0:1, :],
                    brow[0:1, boff + half * 512: boff + (half + 1) * 512],
                    start=False, stop=True), inc=True)
                b.wait("vector", tmm)
                if scale is None:
                    tev = b.op("vector", lambda e, half=half, dst=dst: e.tensor_copy(
                        dst[:, half * 512:(half + 1) * 512], ps_p[:, 0:512]), inc=True)
                else:
                    tev = b.op("vector", lambda e, half=half, dst=dst, scale=scale: e.tensor_scalar(
                        dst[:, half * 512:(half + 1) * 512], ps_p[:, 0:512],
                        scale, None, ALU.mult), inc=True)
                evs.append(tev)
        for kc in range(8):
            tt, vt = tr.emit(q_tm[:, kc * 128:(kc + 1) * 128], q_fm[:, kc * 128:(kc + 1) * 128],
                             pre_tensor_waits=[evs[1]] if kc == 0 else ())
            tt, vt2 = tr.emit(k_tm[:, kc * 128:(kc + 1) * 128], k_fm[:, kc * 128:(kc + 1) * 128],
                              pre_tensor_waits=[evs[3]] if kc == 0 else ())
        b.wait("tensor", vt2)
        prev_pr = b.tok("tensor")

        prev_ao = None
        for pr in range(8):
            for j in range(2):
                h = 2 * pr + j
                hc, par = h // 2, h % 2
                koff = par * 64
                b.op("tensor", lambda e, hc=hc, koff=koff, j=j: e.matmul(
                    ps_s[j][:, 0:128], q_fm[koff:koff + 64, hc * 128:(hc + 1) * 128],
                    k_fm[koff:koff + 64, hc * 128:(hc + 1) * 128], start=True, stop=False))
                tsm = b.op("tensor", lambda e, j=j: e.matmul(
                    ps_s[j][:, 0:128], idt[:, :], mka[:, :], start=False, stop=True), inc=True)
                b.wait("vector", tsm)
                tevs = b.op("vector", lambda e, j=j: e.tensor_copy(
                    sS[:, j * 128:(j + 1) * 128], ps_s[j][:, 0:128]), inc=True)
                b.drain("vector")
                b.op("vector", lambda e, j=j: e.tensor_reduce(
                    mx[:, j:j + 1], sS[:, j * 128:(j + 1) * 128], AX.X, ALU.max, negate=True))
                tmx = b.mark("vector")
                b.wait("scalar", tmx)
                texp = b.op("scalar", lambda e, j=j: e.activation(
                    sS[:, j * 128:(j + 1) * 128], sS[:, j * 128:(j + 1) * 128], AF.Exp,
                    bias=mx[:, j:j + 1]), inc=True)
                b.wait("vector", texp)
                b.drain("vector")
                b.op("vector", lambda e, j=j: e.tensor_reduce(
                    sm[:, j:j + 1], sS[:, j * 128:(j + 1) * 128], AX.X, ALU.add))
                b.drain("vector")
                b.op("vector", lambda e, j=j: e.reciprocal(sm[:, j:j + 1], sm[:, j:j + 1]))
                b.drain("vector")
                tpn = b.op("vector", lambda e, j=j: e.tensor_scalar(
                    pn[:, j * 128:(j + 1) * 128], sS[:, j * 128:(j + 1) * 128],
                    sm[:, j:j + 1], None, ALU.mult), inc=True)
                b.drain("vector")
            tdd = b.op("vector", lambda e: e.scalar_tensor_tensor(
                dd[:, :], pn[:, 128:256], nlamc[:, 0:1], pn[:, 0:128], ALU.mult, ALU.add),
                inc=True)
            b.drain("vector")
            ttd, vtd = tr.emit(dd[:, :], ddT[:, :], pre_tensor_waits=[tdd])
            b.wait("tensor", vtd)
            if prev_ao is not None:
                b.wait("tensor", prev_ao)
            tav = b.op("tensor", lambda e, pr=pr: e.matmul(
                ps_o[:, 0:128], ddT[:, :], v_tm[:, pr * 128:(pr + 1) * 128],
                start=True, stop=True), inc=True)
            b.wait("vector", tav)
            prev_ao = b.op("vector", lambda e, pr=pr: e.tensor_copy(
                ao[:, pr * 128:(pr + 1) * 128], ps_o[:, 0:128]), inc=True)
        b.wait("scalar", prev_ao)
        if prev_store is not None:
            b.wait("scalar", prev_store)
        tsq = b.op("scalar", lambda e: e.activation(aos[:], ao[:], AF.Square), inc=True)
        b.wait("vector", tsq)
        b.drain("vector")
        b.op("vector", lambda e: e.tensor_reduce(
            s2[:], aos[:, :].rearrange("p (a c) -> p a c", c=128), AX.X, ALU.add))
        b.drain("vector")
        b.op("vector", lambda e: e.tensor_scalar(s2[:], s2[:], 1.0 / 128.0, 1e-5, ALU.mult,
                                                 op1=ALU.add))
        b.drain("vector")
        b.op("vector", lambda e: e.reciprocal(s2[:], s2[:]))
        trc2 = b.mark("vector")
        b.wait("scalar", trc2)
        trs = b.op("scalar", lambda e: e.activation(s2[:], s2[:], AF.Sqrt), inc=True)
        b.wait("vector", trs)
        tho = b.op("vector", lambda e: e.tensor_tensor(
            ho[:, :].rearrange("p (a c) -> p a c", c=128), ao[:, :].rearrange("p (a c) -> p a c", c=128),
            s2[:, :].unsqueeze(2).broadcast_to([128, 8, 128]), ALU.mult), inc=True)
        b.drain("vector")
        for kc in range(8):
            pre = [tho]
            if kc == 0 and prev_hoT is not None:
                pre.append(prev_hoT)
            tt, vt = tr.emit(ho[:, kc * 128:(kc + 1) * 128], hoT[:, kc * 128:(kc + 1) * 128],
                             pre_tensor_waits=pre if kc == 0 else ())
        b.wait("tensor", vt)
        for n2 in range(2):
            for kc in range(8):
                b.op("tensor", lambda e, n2=n2, kc=kc: e.matmul(
                    ps_y[:, n2 * 512:(n2 + 1) * 512], hoT[:, kc * 128:(kc + 1) * 128],
                    wo_sb[:, kc * 1024 + n2 * 512: kc * 1024 + (n2 + 1) * 512],
                    start=(kc == 0), stop=False))
            b.op("tensor", lambda e, n2=n2: e.matmul(
                ps_y[:, n2 * 512:(n2 + 1) * 512], ones_sb[0:1, :],
                brow[0:1, 3072 + n2 * 512: 3072 + (n2 + 1) * 512], start=False, stop=True))
        tym = b.mark("tensor")
        prev_hoT = tym
        b.wait("vector", tym)
        tvy = b.op("vector", lambda e: e.tensor_tensor(aos[:], xf[:], ps_y[:, 0:1024], ALU.add),
                   inc=True)
        b.wait("gpsimd", tvy)
        if prev_store is not None:
            b.wait("gpsimd", prev_store)
        prev_store = b.dma("gpsimd", x_out[i * 128:(i + 1) * 128, :], aos[:], sem="xst")
        b.wait("sync", tvy)
        b.wait("sync", prev_pr)



def build_fused_launch1():
    nc = bass.Bass("TRN2", target_bir_lowering=False)
    x_in = nc.dram_tensor("x", [N_TOK, D], F32, kind="ExternalInput").ap()
    gw = nc.dram_tensor("l0f_gate_w", [8, 1024], F32, kind="ExternalInput").ap()
    w1 = nc.dram_tensor("l0f_w1", [8, 1024, 4096], F32, kind="ExternalInput").ap()
    b1 = nc.dram_tensor("l0f_b1", [8, 4096], F32, kind="ExternalInput").ap()
    w2 = nc.dram_tensor("l0f_w2", [8, 4096, 1024], F32, kind="ExternalInput").ap()
    b2 = nc.dram_tensor("l0f_b2", [8, 1024], F32, kind="ExternalInput").ap()
    norm_g = nc.dram_tensor("l0a_norm_g", [1024], F32, kind="ExternalInput").ap()
    norm_b = nc.dram_tensor("l0a_norm_b", [1024], F32, kind="ExternalInput").ap()
    w_h = nc.dram_tensor("l0a_w_h", [1024, 4096], F32, kind="ExternalInput").ap()
    b_h = nc.dram_tensor("l0a_b_h", [4096], F32, kind="ExternalInput").ap()
    w_qk = nc.dram_tensor("l0a_w_qk", [1024, 128], F32, kind="ExternalInput").ap()
    b_qk = nc.dram_tensor("l0a_b_qk", [128], F32, kind="ExternalInput").ap()
    qk_gamma = nc.dram_tensor("l0a_qk_gamma", [4, 128], F32, kind="ExternalInput").ap()
    qk_beta = nc.dram_tensor("l0a_qk_beta", [4, 128], F32, kind="ExternalInput").ap()
    rel_emb = nc.dram_tensor("l0a_rel_emb", [32, 1], F32, kind="ExternalInput").ap()
    w_o = nc.dram_tensor("l0a_w_o", [2048, 1024], F32, kind="ExternalInput").ap()
    b_o = nc.dram_tensor("l0a_b_o", [1024], F32, kind="ExternalInput").ap()
    wg1 = nc.dram_tensor("l1f_w_gate", [3072, 1024], I32, kind="ExternalInput").ap()
    wu1 = nc.dram_tensor("l1f_w_up", [3072, 1024], I32, kind="ExternalInput").ap()
    wd1 = nc.dram_tensor("l1f_w_down", [1024, 3072], I32, kind="ExternalInput").ap()
    wq1 = nc.dram_tensor("l1a_wq", [1024, 1024], F32, kind="ExternalInput").ap()
    wk1 = nc.dram_tensor("l1a_wk", [1024, 1024], F32, kind="ExternalInput").ap()
    wv1 = nc.dram_tensor("l1a_wv", [1024, 1024], F32, kind="ExternalInput").ap()
    ident = nc.dram_tensor("ident", [128, 128], BF16, kind="ExternalInput").ap()
    ones = nc.dram_tensor("ones", [1, 128], BF16, kind="ExternalInput").ap()
    onesc = nc.dram_tensor("onesc", [128, 1], BF16, kind="ExternalInput").ap()
    cmask = nc.dram_tensor("cmask", [128, 4 * 512], BF16, kind="ExternalInput").ap()
    relmask = nc.dram_tensor("relmask", [128, 2 * 32 * 256], BF16, kind="ExternalInput").ap()
    x_out = nc.dram_tensor("x_out", [N_TOK, D], F32, kind="ExternalOutput").ap()
    xa = nc.dram_tensor("xa", [N_TOK, D], F32).ap()
    xb = nc.dram_tensor("xb", [N_TOK, D], F32).ap()
    hv_scr = nc.dram_tensor("hv_scr", [N_TOK, 4096], BF16).ap()
    vs_scr = nc.dram_tensor("vs_scr", [16, 128], BF16).ap()
    wg_bf = nc.dram_tensor("wg_bf", [3072, 1024], BF16).ap()
    wu_bf = nc.dram_tensor("wu_bf", [3072, 1024], BF16).ap()
    wd_bf = nc.dram_tensor("wd_bf", [1024, 3072], BF16).ap()
    wgT = nc.dram_tensor("wgT", [1024, 3072], BF16).ap()
    wuT = nc.dram_tensor("wuT", [1024, 3072], BF16).ap()
    wdT = nc.dram_tensor("wdT", [3072, 1024], BF16).ap()
    wq_bf = nc.dram_tensor("wq_bf", [1024, 1024], BF16).ap()
    wk_bf = nc.dram_tensor("wk_bf", [1024, 1024], BF16).ap()
    wv_bf = nc.dram_tensor("wv_bf", [1024, 1024], BF16).ap()

    b = B(nc)
    with b.scope():
        emit_l1_ffn_prep(b, wg1, wu1, wd1, wg_bf, wu_bf, wd_bf, wgT, wuT, wdT)
        emit_cast_pass(b, wq1, wq_bf, 1024)
        emit_cast_pass(b, wk1, wk_bf, 1024)
        emit_cast_pass(b, wv1, wv_bf, 1024)
    b.barrier()
    with b.scope():
        emit_l0_ffn(b, x_in, xa, w1, w2, b1, b2, gw, ident, ones, name="f0a")
    b.barrier()
    with b.scope():
        emit_l0_attn(b, xa, xb, norm_g, norm_b, w_h, b_h, w_qk, b_qk, qk_gamma, qk_beta,
                     rel_emb, w_o, b_o, hv_scr, vs_scr, ident, ones, onesc, relmask, name="f0b")
    b.barrier()
    with b.scope():
        emit_l0_ffn(b, xb, xa, w1, w2, b1, b2, gw, ident, ones, name="f0c")
    b.barrier()
    with b.scope():
        emit_glu_ffn(b, xa, xb, wgT, wuT, wdT, HID=3072, idt_ap=ident, scale=0.5, name="f1a")
    b.barrier()
    with b.scope():
        emit_mha(b, xb, xa, wq_bf, wk_bf, wv_bf, ident, ones, onesc, cmask,
                 causal=True, name="f1b")
    b.barrier()
    with b.scope():
        emit_glu_ffn(b, xa, x_out, wgT, wuT, wdT, HID=3072, idt_ap=ident, scale=0.5, name="f1c")
    b.barrier()
    return b.build()


def build_fused_launch2():
    nc = bass.Bass("TRN2", target_bir_lowering=False)
    x_in = nc.dram_tensor("x", [N_TOK, D], F32, kind="ExternalInput").ap()
    wg = nc.dram_tensor("l2f_w_gate", [2816, 1024], F32, kind="ExternalInput").ap()
    wu = nc.dram_tensor("l2f_w_up", [2816, 1024], F32, kind="ExternalInput").ap()
    wd = nc.dram_tensor("l2f_w_down", [1024, 2816], F32, kind="ExternalInput").ap()
    names = ["Wq", "Wk", "Wv", "Wo"]
    w2a = {n: nc.dram_tensor("l2a_" + n, [1024, 1024], F32, kind="ExternalInput").ap() for n in names}
    bq = nc.dram_tensor("l2a_bq", [1024], F32, kind="ExternalInput").ap()
    bk = nc.dram_tensor("l2a_bk", [1024], F32, kind="ExternalInput").ap()
    bv = nc.dram_tensor("l2a_bv", [1024], F32, kind="ExternalInput").ap()
    bo = nc.dram_tensor("l2a_bo", [1024], F32, kind="ExternalInput").ap()
    gn = nc.dram_tensor("l2a_g_norm", [128], F32, kind="ExternalInput").ap()
    lq1 = nc.dram_tensor("l2a_lq1", [64], F32, kind="ExternalInput").ap()
    lk1 = nc.dram_tensor("l2a_lk1", [64], F32, kind="ExternalInput").ap()
    lq2 = nc.dram_tensor("l2a_lq2", [64], F32, kind="ExternalInput").ap()
    lk2 = nc.dram_tensor("l2a_lk2", [64], F32, kind="ExternalInput").ap()
    rw = nc.dram_tensor("l3f_router_w", [16, 1024], F32, kind="ExternalInput").ap()
    rb = nc.dram_tensor("l3f_router_b", [16], F32, kind="ExternalInput").ap()
    w13 = nc.dram_tensor("l3f_w1", [4096, 1024], F32, kind="ExternalInput").ap()
    b13 = nc.dram_tensor("l3f_b1", [4096], F32, kind="ExternalInput").ap()
    w23 = nc.dram_tensor("l3f_w2", [1024, 4096], F32, kind="ExternalInput").ap()
    b23 = nc.dram_tensor("l3f_b2", [1024], F32, kind="ExternalInput").ap()
    ident = nc.dram_tensor("ident", [128, 128], BF16, kind="ExternalInput").ap()
    ones = nc.dram_tensor("ones", [1, 128], BF16, kind="ExternalInput").ap()
    maskadd = nc.dram_tensor("maskadd", [128, 128], BF16, kind="ExternalInput").ap()
    x_out = nc.dram_tensor("x_out", [N_TOK, D], F32, kind="ExternalOutput").ap()
    xa = nc.dram_tensor("xa", [N_TOK, D], F32).ap()
    xb = nc.dram_tensor("xb", [N_TOK, D], F32).ap()
    wgq_bf = nc.dram_tensor("wgq_bf", [2816, 1024], BF16).ap()
    wuq_bf = nc.dram_tensor("wuq_bf", [2816, 1024], BF16).ap()
    wdq_bf = nc.dram_tensor("wdq_bf", [1024, 2816], BF16).ap()
    wgqT = nc.dram_tensor("wgqT", [1024, 2816], BF16).ap()
    wuqT = nc.dram_tensor("wuqT", [1024, 2816], BF16).ap()
    wdqT = nc.dram_tensor("wdqT", [2816, 1024], BF16).ap()
    wbf = {n: nc.dram_tensor("l2a_" + n + "_bf", [1024, 1024], BF16).ap() for n in names}
    w1_bf = nc.dram_tensor("w1_bf", [4096, 1024], BF16).ap()
    w2_bf = nc.dram_tensor("w2_bf", [1024, 4096], BF16).ap()
    rw_bf = nc.dram_tensor("rw_bf", [16, 1024], BF16).ap()
    w1T = nc.dram_tensor("w1T", [1024, 4096], BF16).ap()
    w2T = nc.dram_tensor("w2T", [4096, 1024], BF16).ap()
    rwT = nc.dram_tensor("rwT", [1024, 16], BF16).ap()

    b = B(nc)
    with b.scope():
        qz = Fp4Quant(b, "pp", 2816)
        emit_fp4_weight_quant(b, wg, wgq_bf, 2816, 1024, qz, "ppg")
        emit_fp4_weight_quant(b, wu, wuq_bf, 2816, 1024, qz, "ppu")
        emit_fp4_weight_quant(b, wd, wdq_bf, 1024, 2816, qz, "ppd")
        for nm in ("ppg", "ppu", "ppd"):
            b.wait("sync", b.dtok(nm + "st"))
        tb = b.sb("pp_tb2", [128, 4096], BF16)
        emit_transpose_pass(b, wgq_bf, wgqT, 2816, 1024, tb)
        emit_transpose_pass(b, wuq_bf, wuqT, 2816, 1024, tb)
        emit_transpose_pass(b, wdq_bf, wdqT, 1024, 2816, tb)
        for n in names:
            emit_cast_pass(b, w2a[n], wbf[n], 1024)
        emit_cast_pass(b, w13, w1_bf, 4096)
        emit_cast_pass(b, w23, w2_bf, 1024)
        emit_cast_pass(b, rw, rw_bf, 16)
        b.wait("sync", b.dtok("ppcast"))
        emit_transpose_pass(b, w1_bf, w1T, 4096, 1024, tb)
        emit_transpose_pass(b, w2_bf, w2T, 1024, 4096, tb)
        emit_transpose_pass(b, rw_bf, rwT, 16, 1024, tb)
    b.barrier()
    with b.scope():
        emit_l2_ffn(b, x_in, xa, wgqT, wuqT, wdqT, ident, name="f2a")
    b.barrier()
    with b.scope():
        emit_l2_attn_fused(b, xa, xb, wbf["Wq"], wbf["Wk"], wbf["Wv"], wbf["Wo"],
                           bq, bk, bv, bo, gn, lq1, lk1, lq2, lk2, ident, ones, maskadd,
                           name="f2b")
    b.barrier()
    with b.scope():
        emit_l2_ffn(b, xb, xa, wgqT, wuqT, wdqT, ident, name="f2c")
    b.barrier()
    with b.scope():
        emit_l3_ffn(b, xa, x_out, w1T, w2T, rwT, b13, b23, rb, ident, ones, name="f3a")
    b.barrier()
    return b.build()


def build_fused_launch3():
    nc = bass.Bass("TRN2", target_bir_lowering=False)
    x_in = nc.dram_tensor("x", [N_TOK, D], F32, kind="ExternalInput").ap()
    wqkv = nc.dram_tensor("l3a_w_qkv", [3072, 1024], F32, kind="ExternalInput").ap()
    rw = nc.dram_tensor("l3f_router_w", [16, 1024], F32, kind="ExternalInput").ap()
    rb = nc.dram_tensor("l3f_router_b", [16], F32, kind="ExternalInput").ap()
    w13 = nc.dram_tensor("l3f_w1", [4096, 1024], F32, kind="ExternalInput").ap()
    b13 = nc.dram_tensor("l3f_b1", [4096], F32, kind="ExternalInput").ap()
    w23 = nc.dram_tensor("l3f_w2", [1024, 4096], F32, kind="ExternalInput").ap()
    b23 = nc.dram_tensor("l3f_b2", [1024], F32, kind="ExternalInput").ap()
    ident = nc.dram_tensor("ident", [128, 128], BF16, kind="ExternalInput").ap()
    ones = nc.dram_tensor("ones", [1, 128], BF16, kind="ExternalInput").ap()
    onesc = nc.dram_tensor("onesc", [128, 1], BF16, kind="ExternalInput").ap()
    cmask = nc.dram_tensor("cmask", [128, 4 * 512], BF16, kind="ExternalInput").ap()
    x_out = nc.dram_tensor("x_out", [N_TOK, D], F32, kind="ExternalOutput").ap()
    xa = nc.dram_tensor("xa", [N_TOK, D], F32).ap()
    qkv_bf = nc.dram_tensor("qkv_bf", [3072, 1024], BF16).ap()
    qkvT = nc.dram_tensor("qkvT", [1024, 3072], BF16).ap()
    w1_bf = nc.dram_tensor("w1_bf", [4096, 1024], BF16).ap()
    w2_bf = nc.dram_tensor("w2_bf", [1024, 4096], BF16).ap()
    rw_bf = nc.dram_tensor("rw_bf", [16, 1024], BF16).ap()
    w1T = nc.dram_tensor("w1T", [1024, 4096], BF16).ap()
    w2T = nc.dram_tensor("w2T", [4096, 1024], BF16).ap()
    rwT = nc.dram_tensor("rwT", [1024, 16], BF16).ap()

    b = B(nc)
    with b.scope():
        emit_cast_pass(b, wqkv, qkv_bf, 3072)
        emit_cast_pass(b, w13, w1_bf, 4096)
        emit_cast_pass(b, w23, w2_bf, 1024)
        emit_cast_pass(b, rw, rw_bf, 16)
        b.wait("sync", b.dtok("ppcast"))
        tb = b.sb("pp_tb4", [128, 4096], BF16)
        emit_transpose_pass(b, qkv_bf, qkvT, 3072, 1024, tb)
        emit_transpose_pass(b, w1_bf, w1T, 4096, 1024, tb)
        emit_transpose_pass(b, w2_bf, w2T, 1024, 4096, tb)
        emit_transpose_pass(b, rw_bf, rwT, 16, 1024, tb)
    b.barrier()
    with b.scope():
        emit_mha(b, x_in, xa, qkvT[:, 0:1024], qkvT[:, 1024:2048], qkvT[:, 2048:3072],
                 ident, ones, onesc, cmask, causal=False, name="f3b")
    b.barrier()
    with b.scope():
        emit_l3_ffn(b, xa, x_out, w1T, w2T, rwT, b13, b23, rb, ident, ones, name="f3c")
    b.barrier()
    return b.build()



def trigger_boundary(b, nc, cc_name, gin, g):
    cc = b.es.enter_context(nc.semaphore(cc_name))

    def cl(eng, gin=gin, g=g, cc=cc):
        eng.collective_compute(
            "AllGather", ALU.bypass,
            replica_groups=[list(range(N_CORES))],
            ins=[gin.opt()], outs=[g.opt()]).then_inc(cc)
    b.ops["gpsimd"].append(cl)
    return cc


def wait_boundary(b, cc):
    for e in DMA_QUEUES:
        b.ops[e].append(lambda eng, cc=cc: eng.wait_ge(cc, 1))


def emit_boundary(b, nc, cc_name, gin, g):
    wait_boundary(b, trigger_boundary(b, nc, cc_name, gin, g))


def build_megakernel():
    nc = bass.Bass("TRN2", target_bir_lowering=False)
    ET = mybir.EngineType
    pid = nc.partition_id(engines=[ET.SP, ET.Activation, ET.Pool])

    x_in = nc.dram_tensor("x", [N_TOK, D], F32, kind="ExternalInput").ap()
    gw = nc.dram_tensor("l0f_gate_w", [8, 1024], F32, kind="ExternalInput").ap()
    w1 = nc.dram_tensor("l0f_w1", [8, 1024, 4096], F32, kind="ExternalInput").ap()
    b1 = nc.dram_tensor("l0f_b1", [8, 4096], F32, kind="ExternalInput").ap()
    w2 = nc.dram_tensor("l0f_w2", [8, 4096, 1024], F32, kind="ExternalInput").ap()
    b2 = nc.dram_tensor("l0f_b2", [8, 1024], F32, kind="ExternalInput").ap()
    norm_g = nc.dram_tensor("l0a_norm_g", [1024], F32, kind="ExternalInput").ap()
    norm_b = nc.dram_tensor("l0a_norm_b", [1024], F32, kind="ExternalInput").ap()
    w_h = nc.dram_tensor("l0a_w_h", [1024, 4096], F32, kind="ExternalInput").ap()
    b_h = nc.dram_tensor("l0a_b_h", [4096], F32, kind="ExternalInput").ap()
    w_qk = nc.dram_tensor("l0a_w_qk", [1024, 128], F32, kind="ExternalInput").ap()
    b_qk = nc.dram_tensor("l0a_b_qk", [128], F32, kind="ExternalInput").ap()
    qk_gamma = nc.dram_tensor("l0a_qk_gamma", [4, 128], F32, kind="ExternalInput").ap()
    qk_beta = nc.dram_tensor("l0a_qk_beta", [4, 128], F32, kind="ExternalInput").ap()
    rel_emb = nc.dram_tensor("l0a_rel_emb", [32, 1], F32, kind="ExternalInput").ap()
    w_o = nc.dram_tensor("l0a_w_o", [2048, 1024], F32, kind="ExternalInput").ap()
    b_o = nc.dram_tensor("l0a_b_o", [1024], F32, kind="ExternalInput").ap()
    wg1 = nc.dram_tensor("l1f_w_gate", [3072, 1024], I32, kind="ExternalInput").ap()
    wu1 = nc.dram_tensor("l1f_w_up", [3072, 1024], I32, kind="ExternalInput").ap()
    wd1 = nc.dram_tensor("l1f_w_down", [1024, 3072], I32, kind="ExternalInput").ap()
    wq1 = nc.dram_tensor("l1a_wq", [1024, 1024], F32, kind="ExternalInput").ap()
    wk1 = nc.dram_tensor("l1a_wk", [1024, 1024], F32, kind="ExternalInput").ap()
    wv1 = nc.dram_tensor("l1a_wv", [1024, 1024], F32, kind="ExternalInput").ap()
    wg2 = nc.dram_tensor("l2f_w_gate", [2816, 1024], F32, kind="ExternalInput").ap()
    wu2 = nc.dram_tensor("l2f_w_up", [2816, 1024], F32, kind="ExternalInput").ap()
    wd2 = nc.dram_tensor("l2f_w_down", [1024, 2816], F32, kind="ExternalInput").ap()
    names = ["Wq", "Wk", "Wv", "Wo"]
    w2a = {n: nc.dram_tensor("l2a_" + n, [1024, 1024], F32, kind="ExternalInput").ap() for n in names}
    bq2 = nc.dram_tensor("l2a_bq", [1024], F32, kind="ExternalInput").ap()
    bk2 = nc.dram_tensor("l2a_bk", [1024], F32, kind="ExternalInput").ap()
    bv2 = nc.dram_tensor("l2a_bv", [1024], F32, kind="ExternalInput").ap()
    bo2 = nc.dram_tensor("l2a_bo", [1024], F32, kind="ExternalInput").ap()
    gn2 = nc.dram_tensor("l2a_g_norm", [128], F32, kind="ExternalInput").ap()
    lq1 = nc.dram_tensor("l2a_lq1", [64], F32, kind="ExternalInput").ap()
    lk1 = nc.dram_tensor("l2a_lk1", [64], F32, kind="ExternalInput").ap()
    lq2 = nc.dram_tensor("l2a_lq2", [64], F32, kind="ExternalInput").ap()
    lk2 = nc.dram_tensor("l2a_lk2", [64], F32, kind="ExternalInput").ap()
    wqkv = nc.dram_tensor("l3a_w_qkv", [3072, 1024], F32, kind="ExternalInput").ap()
    rw = nc.dram_tensor("l3f_router_w", [16, 1024], F32, kind="ExternalInput").ap()
    rb = nc.dram_tensor("l3f_router_b", [16], F32, kind="ExternalInput").ap()
    w13 = nc.dram_tensor("l3f_w1", [4096, 1024], F32, kind="ExternalInput").ap()
    b13 = nc.dram_tensor("l3f_b1", [4096], F32, kind="ExternalInput").ap()
    w23 = nc.dram_tensor("l3f_w2", [1024, 4096], F32, kind="ExternalInput").ap()
    b23 = nc.dram_tensor("l3f_b2", [1024], F32, kind="ExternalInput").ap()
    ident = nc.dram_tensor("ident", [128, 128], BF16, kind="ExternalInput").ap()
    ones = nc.dram_tensor("ones", [1, 128], BF16, kind="ExternalInput").ap()
    onesc = nc.dram_tensor("onesc", [128, 1], BF16, kind="ExternalInput").ap()
    cmask = nc.dram_tensor("cmask", [128, 4 * 512], BF16, kind="ExternalInput").ap()
    relmask = nc.dram_tensor("relmask", [128, 2 * 32 * 256], BF16, kind="ExternalInput").ap()
    maskadd = nc.dram_tensor("maskadd", [128, 128], BF16, kind="ExternalInput").ap()
    x_out = nc.dram_tensor("x_out", [N_TOK, D], F32, kind="ExternalOutput").ap()

    xa = nc.dram_tensor("xa", [N_TOK, D], F32).ap()
    xb = nc.dram_tensor("xb", [N_TOK, D], F32).ap()
    hv_scr = nc.dram_tensor("hv_scr", [N_TOK, 4096], BF16).ap()
    vs_scr = nc.dram_tensor("vs_scr", [16, 128], BF16).ap()
    wg_bf = nc.dram_tensor("wg_bf", [3072, 1024], BF16).ap()
    wu_bf = nc.dram_tensor("wu_bf", [3072, 1024], BF16).ap()
    wd_bf = nc.dram_tensor("wd_bf", [1024, 3072], BF16).ap()
    wgT = nc.dram_tensor("wgT", [1024, 3072], BF16).ap()
    wuT = nc.dram_tensor("wuT", [1024, 3072], BF16).ap()
    wdT = nc.dram_tensor("wdT", [3072, 1024], BF16).ap()
    wq_bf = nc.dram_tensor("wq_bf", [1024, 1024], BF16).ap()
    wk_bf = nc.dram_tensor("wk_bf", [1024, 1024], BF16).ap()
    wv_bf = nc.dram_tensor("wv_bf", [1024, 1024], BF16).ap()
    wgq_bf = nc.dram_tensor("wgq_bf", [2816, 1024], BF16).ap()
    wuq_bf = nc.dram_tensor("wuq_bf", [2816, 1024], BF16).ap()
    wdq_bf = nc.dram_tensor("wdq_bf", [1024, 2816], BF16).ap()
    wgqT = nc.dram_tensor("wgqT", [1024, 2816], BF16).ap()
    wuqT = nc.dram_tensor("wuqT", [1024, 2816], BF16).ap()
    wdqT = nc.dram_tensor("wdqT", [2816, 1024], BF16).ap()
    wbf = {n: nc.dram_tensor("l2a_" + n + "_bf", [1024, 1024], BF16).ap() for n in names}
    w1_bf = nc.dram_tensor("w1_bf", [4096, 1024], BF16).ap()
    w2_bf = nc.dram_tensor("w2_bf", [1024, 4096], BF16).ap()
    rw_bf = nc.dram_tensor("rw_bf", [16, 1024], BF16).ap()
    w1T = nc.dram_tensor("w1T", [1024, 4096], BF16).ap()
    w2T = nc.dram_tensor("w2T", [4096, 1024], BF16).ap()
    rwT = nc.dram_tensor("rwT", [1024, 16], BF16).ap()
    qkv_bf = nc.dram_tensor("qkv_bf", [3072, 1024], BF16).ap()
    qkvT = nc.dram_tensor("qkvT", [1024, 3072], BF16).ap()
    gin1 = nc.dram_tensor("gin1", [N_TOK, D], F32).ap()
    g1t = nc.dram_tensor("g1", [N_CORES * N_TOK, D], F32)
    xpm = nc.dram_tensor("xpm", [N_TOK, D], F32)
    gin2 = nc.dram_tensor("gin2", [N_TOK, D], F32).ap()
    g2t = nc.dram_tensor("g2", [N_CORES * N_TOK, D], F32)
    xbm = nc.dram_tensor("xbm", [N_TOK, D], F32).ap()

    b = B(nc)
    with b.scope():
        emit_l1_ffn_prep(b, wg1, wu1, wd1, wg_bf, wu_bf, wd_bf, wgT, wuT, wdT)
        emit_cast_pass(b, wq1, wq_bf, 1024)
        emit_cast_pass(b, wk1, wk_bf, 1024)
        emit_cast_pass(b, wv1, wv_bf, 1024)
    b.barrier()
    with b.scope():
        emit_l0_ffn(b, x_in, xa, w1, w2, b1, b2, gw, ident, ones, name="f0a")
    b.barrier()
    with b.scope():
        emit_l0_attn(b, xa, xb, norm_g, norm_b, w_h, b_h, w_qk, b_qk, qk_gamma, qk_beta,
                     rel_emb, w_o, b_o, hv_scr, vs_scr, ident, ones, onesc, relmask, name="f0b")
    b.barrier()
    with b.scope():
        emit_l0_ffn(b, xb, xa, w1, w2, b1, b2, gw, ident, ones, name="f0c")
    b.barrier()
    with b.scope():
        emit_glu_ffn(b, xa, xb, wgT, wuT, wdT, HID=3072, idt_ap=ident, scale=0.5, name="f1a")
    b.barrier()
    with b.scope():
        emit_mha(b, xb, xa, wq_bf, wk_bf, wv_bf, ident, ones, onesc, cmask,
                 causal=True, name="f1b")
    b.barrier()
    with b.scope():
        emit_glu_ffn(b, xa, gin1, wgT, wuT, wdT, HID=3072, idt_ap=ident, scale=0.5, name="f1c")
    b.barrier()
    cc1 = trigger_boundary(b, nc, "cc1", gin1, g1t.ap())
    xpm_ap = xpm.ap()
    with b.scope():
        qz = Fp4Quant(b, "pp", 2816)
        emit_fp4_weight_quant(b, wg2, wgq_bf, 2816, 1024, qz, "ppg")
        emit_fp4_weight_quant(b, wu2, wuq_bf, 2816, 1024, qz, "ppu")
        emit_fp4_weight_quant(b, wd2, wdq_bf, 1024, 2816, qz, "ppd")
        for nm in ("ppg", "ppu", "ppd"):
            b.wait("sync", b.dtok(nm + "st"))
        tb = b.sb("pp_tb2", [128, 4096], BF16)
        emit_transpose_pass(b, wgq_bf, wgqT, 2816, 1024, tb)
        emit_transpose_pass(b, wuq_bf, wuqT, 2816, 1024, tb)
        emit_transpose_pass(b, wdq_bf, wdqT, 1024, 2816, tb)
        for n in names:
            emit_cast_pass(b, w2a[n], wbf[n], 1024)
        emit_cast_pass(b, w13, w1_bf, 4096)
        emit_cast_pass(b, w23, w2_bf, 1024)
        emit_cast_pass(b, rw, rw_bf, 16)
        emit_cast_pass(b, wqkv, qkv_bf, 3072)
        b.wait("sync", b.dtok("ppcast"))
        emit_transpose_pass(b, w1_bf, w1T, 4096, 1024, tb)
        emit_transpose_pass(b, w2_bf, w2T, 1024, 4096, tb)
        emit_transpose_pass(b, rw_bf, rwT, 16, 1024, tb)
        emit_transpose_pass(b, qkv_bf, qkvT, 3072, 1024, tb)
    wait_boundary(b, cc1)
    g1v = g1t.ap().rearrange("(m q) d -> q m d", m=N_CORES)
    xpmv = xpm.ap().rearrange("(p m) d -> p m d", m=N_CORES)
    for m in range(N_CORES):
        q = DMA_QUEUES[m % len(DMA_QUEUES)]
        b.dma(q, xpmv[:, m:m + 1, :], g1v[ds(pid * 512, 512), m:m + 1, :], sem="bnd1")
    b.barrier()
    with b.scope():
        emit_l2_ffn(b, xpm_ap, xa, wgqT, wuqT, wdqT, ident, name="f2a")
    b.barrier()
    with b.scope():
        emit_l2_attn_fused(b, xa, xb, wbf["Wq"], wbf["Wk"], wbf["Wv"], wbf["Wo"],
                           bq2, bk2, bv2, bo2, gn2, lq1, lk1, lq2, lk2, ident, ones, maskadd,
                           name="f2b")
    b.barrier()
    with b.scope():
        emit_l2_ffn(b, xb, xa, wgqT, wuqT, wdqT, ident, name="f2c")
    b.barrier()
    with b.scope():
        emit_l3_ffn(b, xa, gin2, w1T, w2T, rwT, b13, b23, rb, ident, ones, name="f3a")
    b.barrier()
    emit_boundary(b, nc, "cc2", gin2, g2t.ap())
    g2v = g2t.ap().rearrange("(q m) d -> q m d", m=N_CORES)
    NSPL = 4
    CH = N_TOK // NSPL
    for k in range(NSPL):
        q = DMA_QUEUES[k % len(DMA_QUEUES)]
        src = g2v[k * CH:(k + 1) * CH, ds(pid, 1), :].rearrange("q m d -> (q m) d")
        b.dma(q, xbm[k * CH:(k + 1) * CH, :], src, sem="bnd2")
    b.barrier()
    with b.scope():
        emit_mha(b, xbm, xa, qkvT[:, 0:1024], qkvT[:, 1024:2048], qkvT[:, 2048:3072],
                 ident, ones, onesc, cmask, causal=False, name="f3b")
    b.barrier()
    with b.scope():
        emit_l3_ffn(b, xa, x_out, w1T, w2T, rwT, b13, b23, rb, ident, ones, name="f3c")
    b.barrier()
    return b.build()


def kernel(**inputs):
    I = inputs
    x = np.ascontiguousarray(np.asarray(I["L0_attn_x"], dtype=np.float32))
    nc = build_megakernel()
    in_maps = []
    for c in range(N_CORES):
        in_maps.append({
            "x": np.ascontiguousarray(x[c]),
            "l0f_gate_w": np.asarray(I["L0_ffn_gate_w"]), "l0f_w1": np.asarray(I["L0_ffn_w1"]),
            "l0f_b1": np.asarray(I["L0_ffn_b1"]), "l0f_w2": np.asarray(I["L0_ffn_w2"]),
            "l0f_b2": np.asarray(I["L0_ffn_b2"]),
            "l0a_norm_g": np.asarray(I["L0_attn_norm_g"]), "l0a_norm_b": np.asarray(I["L0_attn_norm_b"]),
            "l0a_w_h": np.asarray(I["L0_attn_W_h"]), "l0a_b_h": np.asarray(I["L0_attn_b_h"]),
            "l0a_w_qk": np.asarray(I["L0_attn_W_qk"]), "l0a_b_qk": np.asarray(I["L0_attn_b_qk"]),
            "l0a_qk_gamma": np.asarray(I["L0_attn_qk_gamma"]), "l0a_qk_beta": np.asarray(I["L0_attn_qk_beta"]),
            "l0a_rel_emb": np.asarray(I["L0_attn_rel_emb"]), "l0a_w_o": np.asarray(I["L0_attn_W_o"]),
            "l0a_b_o": np.asarray(I["L0_attn_b_o"]),
            "l1f_w_gate": np.asarray(I["L1_ffn_w_gate"]), "l1f_w_up": np.asarray(I["L1_ffn_w_up"]),
            "l1f_w_down": np.asarray(I["L1_ffn_w_down"]),
            "l1a_wq": np.asarray(I["L1_attn_Wq"]), "l1a_wk": np.asarray(I["L1_attn_Wk"]),
            "l1a_wv": np.asarray(I["L1_attn_Wv"]),
            "l2f_w_gate": np.asarray(I["L2_ffn_w_gate"]), "l2f_w_up": np.asarray(I["L2_ffn_w_up"]),
            "l2f_w_down": np.asarray(I["L2_ffn_w_down"]),
            "l2a_Wq": np.asarray(I["L2_attn_Wq"]), "l2a_Wk": np.asarray(I["L2_attn_Wk"]),
            "l2a_Wv": np.asarray(I["L2_attn_Wv"]), "l2a_Wo": np.asarray(I["L2_attn_Wo"]),
            "l2a_bq": np.asarray(I["L2_attn_bq"]), "l2a_bk": np.asarray(I["L2_attn_bk"]),
            "l2a_bv": np.asarray(I["L2_attn_bv"]), "l2a_bo": np.asarray(I["L2_attn_bo"]),
            "l2a_g_norm": np.asarray(I["L2_attn_g_norm"]),
            "l2a_lq1": np.asarray(I["L2_attn_lq1"]), "l2a_lk1": np.asarray(I["L2_attn_lk1"]),
            "l2a_lq2": np.asarray(I["L2_attn_lq2"]), "l2a_lk2": np.asarray(I["L2_attn_lk2"]),
            "l3a_w_qkv": np.asarray(I["L3_attn_W_qkv"]),
            "l3f_router_w": np.asarray(I["L3_ffn_router_w"]), "l3f_router_b": np.asarray(I["L3_ffn_router_b"]),
            "l3f_w1": np.asarray(I["L3_ffn_w1"]), "l3f_b1": np.asarray(I["L3_ffn_b1"]),
            "l3f_w2": np.asarray(I["L3_ffn_w2"]), "l3f_b2": np.asarray(I["L3_ffn_b2"]),
            "ident": _IDENT, "ones": _ONES, "onesc": _ONESC, "cmask": _CMASK,
            "relmask": _RELMASK, "maskadd": _L2MASK,
        })
    res = run_bass_kernel_spmd(nc, in_maps, core_ids=list(range(N_CORES)))
    return np.stack([res.results[c]["x_out"] for c in range(N_CORES)]).astype(np.float32)


def kernel_3launch(**inputs):
    I = inputs
    x = np.ascontiguousarray(np.asarray(I["L0_attn_x"], dtype=np.float32))
    consts = {"ident": _IDENT, "ones": _ONES, "onesc": _ONESC, "cmask": _CMASK,
              "relmask": _RELMASK}
    nc1 = build_fused_launch1()
    in_maps = []
    for c in range(N_CORES):
        in_maps.append({
            "x": np.ascontiguousarray(x[c]),
            "l0f_gate_w": np.asarray(I["L0_ffn_gate_w"]), "l0f_w1": np.asarray(I["L0_ffn_w1"]),
            "l0f_b1": np.asarray(I["L0_ffn_b1"]), "l0f_w2": np.asarray(I["L0_ffn_w2"]),
            "l0f_b2": np.asarray(I["L0_ffn_b2"]),
            "l0a_norm_g": np.asarray(I["L0_attn_norm_g"]), "l0a_norm_b": np.asarray(I["L0_attn_norm_b"]),
            "l0a_w_h": np.asarray(I["L0_attn_W_h"]), "l0a_b_h": np.asarray(I["L0_attn_b_h"]),
            "l0a_w_qk": np.asarray(I["L0_attn_W_qk"]), "l0a_b_qk": np.asarray(I["L0_attn_b_qk"]),
            "l0a_qk_gamma": np.asarray(I["L0_attn_qk_gamma"]), "l0a_qk_beta": np.asarray(I["L0_attn_qk_beta"]),
            "l0a_rel_emb": np.asarray(I["L0_attn_rel_emb"]), "l0a_w_o": np.asarray(I["L0_attn_W_o"]),
            "l0a_b_o": np.asarray(I["L0_attn_b_o"]),
            "l1f_w_gate": np.asarray(I["L1_ffn_w_gate"]), "l1f_w_up": np.asarray(I["L1_ffn_w_up"]),
            "l1f_w_down": np.asarray(I["L1_ffn_w_down"]),
            "l1a_wq": np.asarray(I["L1_attn_Wq"]), "l1a_wk": np.asarray(I["L1_attn_Wk"]),
            "l1a_wv": np.asarray(I["L1_attn_Wv"]),
            **consts,
        })
    r1 = run_bass_kernel_spmd(nc1, in_maps, core_ids=list(range(N_CORES)))
    x = np.stack([r1.results[c]["x_out"] for c in range(N_CORES)])
    x_pm = [np.ascontiguousarray(
        x[:, c * 512:(c + 1) * 512, :].transpose(1, 0, 2).reshape(N_TOK, D))
        for c in range(N_CORES)]
    nc2 = build_fused_launch2()
    in_maps = []
    for c in range(N_CORES):
        in_maps.append({
            "x": x_pm[c],
            "l2f_w_gate": np.asarray(I["L2_ffn_w_gate"]), "l2f_w_up": np.asarray(I["L2_ffn_w_up"]),
            "l2f_w_down": np.asarray(I["L2_ffn_w_down"]),
            "l2a_Wq": np.asarray(I["L2_attn_Wq"]), "l2a_Wk": np.asarray(I["L2_attn_Wk"]),
            "l2a_Wv": np.asarray(I["L2_attn_Wv"]), "l2a_Wo": np.asarray(I["L2_attn_Wo"]),
            "l2a_bq": np.asarray(I["L2_attn_bq"]), "l2a_bk": np.asarray(I["L2_attn_bk"]),
            "l2a_bv": np.asarray(I["L2_attn_bv"]), "l2a_bo": np.asarray(I["L2_attn_bo"]),
            "l2a_g_norm": np.asarray(I["L2_attn_g_norm"]),
            "l2a_lq1": np.asarray(I["L2_attn_lq1"]), "l2a_lk1": np.asarray(I["L2_attn_lk1"]),
            "l2a_lq2": np.asarray(I["L2_attn_lq2"]), "l2a_lk2": np.asarray(I["L2_attn_lk2"]),
            "l3f_router_w": np.asarray(I["L3_ffn_router_w"]), "l3f_router_b": np.asarray(I["L3_ffn_router_b"]),
            "l3f_w1": np.asarray(I["L3_ffn_w1"]), "l3f_b1": np.asarray(I["L3_ffn_b1"]),
            "l3f_w2": np.asarray(I["L3_ffn_w2"]), "l3f_b2": np.asarray(I["L3_ffn_b2"]),
            "ident": _IDENT, "ones": _ONES, "maskadd": _L2MASK,
        })
    r2 = run_bass_kernel_spmd(nc2, in_maps, core_ids=list(range(N_CORES)))
    outs = [r2.results[c]["x_out"].reshape(512, 8, D) for c in range(N_CORES)]
    x = np.ascontiguousarray(np.concatenate(outs, axis=0).transpose(1, 0, 2))
    nc3 = build_fused_launch3()
    in_maps = []
    for c in range(N_CORES):
        in_maps.append({
            "x": np.ascontiguousarray(x[c]),
            "l3a_w_qkv": np.asarray(I["L3_attn_W_qkv"]),
            "l3f_router_w": np.asarray(I["L3_ffn_router_w"]), "l3f_router_b": np.asarray(I["L3_ffn_router_b"]),
            "l3f_w1": np.asarray(I["L3_ffn_w1"]), "l3f_b1": np.asarray(I["L3_ffn_b1"]),
            "l3f_w2": np.asarray(I["L3_ffn_w2"]), "l3f_b2": np.asarray(I["L3_ffn_b2"]),
            "ident": _IDENT, "ones": _ONES, "onesc": _ONESC, "cmask": _CMASK,
        })
    r3 = run_bass_kernel_spmd(nc3, in_maps, core_ids=list(range(N_CORES)))
    return np.stack([r3.results[c]["x_out"] for c in range(N_CORES)]).astype(np.float32)
